# version 15
# baseline (speedup 1.0000x reference)
import sys
import numpy as np

sys.path.insert(0, '/opt/trn_rl_repo')

import concourse.bass as bass
import concourse.mybir as mybir
from concourse.bass_utils import run_bass_kernel_spmd

F32 = mybir.dt.float32
BF16 = mybir.dt.bfloat16
AF = mybir.ActivationFunctionType
ALU = mybir.AluOpType
AXX = mybir.AxisListType.X

S = 4096
H = 1024
I = 2816
NT = S // 128
KC = H // 128
IC = I // 128
NSL = [(0, 512), (512, 512), (1024, 512), (1536, 512), (2048, 512), (2560, 256)]
HMAGIC = float(1.5 * 2 ** 22)
EPS = 1e-5
INV_H = 1.0 / H
SMSCALE = float(1.0 / 32.0)
NCORES = 8

from concourse import dve_ops as _dve_ops
from concourse.dve_op import DveOp as _DveOp
from concourse.dve_spec import (Spec as _Spec, lower as _dve_lower, Src0 as _S0,
                                Src1 as _S1, C0 as _C0, C1 as _C1, C2 as _C2,
                                One as _One, sq as _sq)


def _mk_mg(src):
    t2 = _sq(src)
    g2 = t2 >= _C0
    g4 = t2 >= _C1
    s3 = ((g2 + g4) + g4) + _One
    return s3 * _C2


def _np_step(x):
    t2 = x * x
    return np.where(t2 >= 16.0, 2.0, np.where(t2 >= 4.0, 1.0, 0.5))


def _ref_fp4rnd(in0, in1, s0, s1, imm2):
    st = _np_step(in0)
    mg = imm2 * (st / 0.5)
    return np.round(in0 / st) * st + mg


def _ref_fp4sub(in0, in1, s0, s1, imm2):
    st = _np_step(in1)
    mg = imm2 * (st / 0.5)
    return in0 - mg


def _register_fp4_ops():
    have = {o.name: o for o in _dve_ops.OPS}
    if 'FP4RND' in have:
        return have['FP4RND'], have['FP4SUB']
    specA = _Spec(body=_S0 + _mk_mg(_S0))
    specB = _Spec(body=_S0 - _mk_mg(_S1))
    shaA = _dve_lower(specA, ver='v3', perf_en=False, name='FP4RND').sha('v3')
    shaB = _dve_lower(specB, ver='v3', perf_en=False, name='FP4SUB').sha('v3')
    opA = _DveOp('FP4RND', specA, subdim=False, uops_sha={'v3': shaA},
                 reference=_ref_fp4rnd)
    opB = _DveOp('FP4SUB', specB, subdim=False, uops_sha={'v3': shaB},
                 reference=_ref_fp4sub)
    for op in (opA, opB):
        _dve_ops.OPS.append(op)
        _dve_ops._SUB_OPCODE_FOR_NAME[op.name] = (
            _dve_ops._CUSTOM_DVE_ROW_BASE + len(_dve_ops.OPS) - 1)
        _dve_ops.CUSTOM_DVE_REFS[op.name] = op.reference
        _dve_ops.CUSTOM_DVE_SPECS[op.name] = op.spec
    assert max(_dve_ops._SUB_OPCODE_FOR_NAME.values()) < 0x20
    return opA, opB


OP_FP4RND, OP_FP4SUB = _register_fp4_ops()


ENGINES = ('tensor', 'vector', 'scalar', 'gpsimd', 'sync')
DRAIN_ENGINES = {'vector', 'scalar', 'gpsimd'}


class Res:
    __slots__ = ('name', 'writers', 'readers', 'accessors', 'is_psum', 'interval',
                 'dma_writes', 'sem')

    def __init__(self, name, is_psum=False, interval=None):
        self.name = name
        self.writers = {}
        self.readers = {}
        self.accessors = {}
        self.is_psum = is_psum
        self.interval = interval
        self.dma_writes = 0
        self.sem = None


class Item:
    __slots__ = ('eng', 'idx', 'emit', 'deps', 'kind', 'need_inc', 'tag',
                 'dma_res', 'dma_count')

    def __init__(self, eng, idx, emit, deps, kind, tag):
        self.eng = eng
        self.idx = idx
        self.emit = emit
        self.deps = deps
        self.kind = kind
        self.need_inc = False
        self.tag = tag
        self.dma_res = None
        self.dma_count = 0


class Sched:
    def __init__(self):
        self.items = {e: [] for e in ENGINES}
        self.retired = []

    def res(self, name, is_psum=False, interval=None):
        return Res(name, is_psum, interval)

    @staticmethod
    def _merge(d, eng, idx, kind):
        cur = d.get(eng)
        if cur is None or idx > cur[0]:
            d[eng] = (idx, kind)

    def op(self, eng, emit, reads=(), writes=(), kind='op', tag=''):
        idx = len(self.items[eng])
        deps = []
        for r in reads:
            for e2, (j, k2) in r.writers.items():
                deps.append((e2, j, k2))
            if r.is_psum:
                for e2, (j, k2) in r.accessors.items():
                    if e2 != eng:
                        deps.append((e2, j, k2))
        for w in writes:
            for e2, (j, k2) in w.writers.items():
                deps.append((e2, j, k2))
            for e2, (j, k2) in w.readers.items():
                deps.append((e2, j, k2))
            if w.is_psum:
                for e2, (j, k2) in w.accessors.items():
                    if e2 != eng:
                        deps.append((e2, j, k2))
        it = Item(eng, idx, emit, deps, kind, tag)
        if kind == 'dma':
            assert len(writes) == 1, f'dma must write exactly one res: {tag}'
            w0 = writes[0]
            it.dma_res = w0
            it.dma_count = getattr(w0, 'dma_writes', 0) + 1
            w0.dma_writes = it.dma_count
        self.items[eng].append(it)
        for r in reads:
            self._merge(r.readers, eng, idx, kind)
            if r.is_psum:
                self._merge(r.accessors, eng, idx, kind)
        for w in writes:
            w.writers = {}
            w.readers = {}
            self._merge(w.writers, eng, idx, kind)
            if w.is_psum:
                self._merge(w.accessors, eng, idx, kind)
        return it

    def dma(self, eng, emit, reads=(), writes=(), tag=''):
        return self.op(eng, emit, reads, writes, kind='dma', tag=tag)

    def finalize(self, nc, block):
        waits = {e: [] for e in ENGINES}
        drain_req = {e: [None] * len(self.items[e]) for e in ENGINES}
        for e in ENGINES:
            for it in self.items[e]:
                wl = []
                for (e2, j, k2) in it.deps:
                    if e2 == e and k2 != 'dma' and it.kind != 'dma':
                        if e in DRAIN_ENGINES:
                            cur = drain_req[e][it.idx]
                            drain_req[e][it.idx] = j if cur is None else max(cur, j)
                    elif e2 == e and k2 != 'dma' and it.kind == 'dma':
                        self.items[e2][j].need_inc = True
                        wl.append((e2, j, k2))
                    elif e2 == e and k2 == 'dma':
                        if not (it.kind == 'dma'
                                and self.items[e2][j].dma_res is it.dma_res):
                            wl.append((e2, j, k2))
                    else:
                        self.items[e2][j].need_inc = True
                        wl.append((e2, j, k2))
                waits[e].append(wl)
        prefix = {e: [] for e in ENGINES}
        for e in ENGINES:
            c_op = 0
            for it in self.items[e]:
                if it.need_inc and it.kind != 'dma':
                    c_op += 1
                prefix[e].append(c_op)
        semC = {e: nc.alloc_semaphore(name=f'semC_{e}') for e in ENGINES}
        dma_resources = []
        for e in ENGINES:
            for it in self.items[e]:
                if it.kind == 'dma' and it.dma_res.sem is None:
                    it.dma_res.sem = nc.alloc_semaphore(
                        name=f'dsem{len(dma_resources)}')
                    dma_resources.append(it.dma_res)
        finals = {e: [] for e in ENGINES}
        seen = set()
        for e in ENGINES:
            for it in reversed(self.items[e]):
                if it.kind == 'dma' and id(it.dma_res) not in seen:
                    seen.add(id(it.dma_res))
                    finals[e].append((it.dma_res.sem, 16 * it.dma_res.dma_writes))

        def emit_stream(e, eng):
            watermark = {}
            drained_to = -1
            for it in self.items[e]:
                dr = drain_req[e][it.idx]
                if dr is not None and dr > drained_to:
                    eng.drain()
                    drained_to = it.idx - 1
                for (e2, j, k2) in waits[e][it.idx]:
                    tgt = self.items[e2][j]
                    if k2 == 'dma':
                        sem = tgt.dma_res.sem
                        val = 16 * tgt.dma_count
                        if watermark.get(id(sem), 0) < val:
                            eng.wait_ge(sem, val)
                            watermark[id(sem)] = val
                    else:
                        c_op = prefix[e2][j]
                        if watermark.get((e2, 'o'), 0) < c_op:
                            eng.wait_ge(semC[e2], c_op)
                            watermark[(e2, 'o')] = c_op
                inst = it.emit(eng)
                if it.kind == 'dma':
                    inst.then_inc(it.dma_res.sem, 16)
                elif it.need_inc:
                    inst.then_inc(semC[e], 1)
            for sem, val in finals[e]:
                eng.wait_ge(sem, val)

        for e in ENGINES:
            if not self.items[e]:
                continue
            getattr(block, e)(lambda eng, e=e: emit_stream(e, eng))


class SBAlloc:
    def __init__(self, sched, nc):
        self.s = sched
        self.base = (nc.sbuf_base + 63) // 64 * 64
        self.limit = nc.sbuf_top - self.base
        self.live = []
        self.hwm = 0

    def _find(self, size, align):
        spans = sorted((off, off + sz) for off, sz, _ in self.live)
        pos = 0
        for a, b in spans:
            p = (pos + align - 1) // align * align
            if p + size <= a:
                return p
            pos = max(pos, b)
        p = (pos + align - 1) // align * align
        if p + size > self.limit:
            raise MemoryError(f'SBUF overflow: need {size}B at {p}')
        return p

    def alloc(self, nc, name, shape, dtype, align=64):
        per_part = int(np.prod(shape[1:])) * mybir.dt.size(dtype)
        off = self._find(per_part, align)
        h = nc.alloc_sbuf_tensor_at(name, list(shape), dtype,
                                    offset=self.base + off, align_bytes=align)
        r = self.s.res(name, interval=(off, per_part))
        for (o2, s2, old) in self.s.retired:
            if not (off + per_part <= o2 or o2 + s2 <= off):
                for e2, v in old.writers.items():
                    Sched._merge(r.readers, e2, v[0], v[1])
                for e2, v in old.readers.items():
                    Sched._merge(r.readers, e2, v[0], v[1])
        self.live.append((off, per_part, r))
        self.hwm = max(self.hwm, off + per_part)
        return h.ap(), r

    def free(self, ap_res):
        _, r = ap_res
        self.live = [(o, s, rr) for (o, s, rr) in self.live if rr is not r]
        self.s.retired.append((r.interval[0], r.interval[1], r))



_BUILD_CACHE = {}


def build(debug=False, upto='D2'):
    key = ('k', debug, upto)
    if key in _BUILD_CACHE:
        return _BUILD_CACHE[key]

    nc = bass.Bass('TRN2', target_bir_lowering=False)
    S_ = Sched()
    al = SBAlloc(S_, nc)

    def sb(name, shape, dtype, align=64):
        return al.alloc(nc, name, shape, dtype, align)

    d_x = nc.dram_tensor('x', [S, H], F32, kind='ExternalInput').ap()
    d_wq = nc.dram_tensor('wq', [H, H], F32, kind='ExternalInput').ap()
    d_wk = nc.dram_tensor('wk', [H, H], F32, kind='ExternalInput').ap()
    d_wv = nc.dram_tensor('wv', [H, H], F32, kind='ExternalInput').ap()
    d_wg = nc.dram_tensor('wg', [I, H], F32, kind='ExternalInput').ap()
    d_wu = nc.dram_tensor('wu', [I, H], F32, kind='ExternalInput').ap()
    d_wd = nc.dram_tensor('wd', [H, I], F32, kind='ExternalInput').ap()
    d_c = nc.dram_tensor('cvec', [1, H], F32, kind='ExternalInput').ap()
    d_wm = nc.dram_tensor('wmod', [H, 6 * H], F32, kind='ExternalInput').ap()
    d_out = nc.dram_tensor('out', [S, H], F32, kind='ExternalOutput').ap()
    dbg = 'ExternalOutput' if debug else 'Internal'
    d_hfm = nc.dram_tensor('hfm_s', [KC, 128, S], BF16, kind=dbg).ap()
    d_x2 = nc.dram_tensor('x2_s', [S, H], F32, kind=dbg).ap()
    d_hmq = nc.dram_tensor('hmq_s', [NT, 128, IC * 128], BF16, kind=dbg).ap()
    d_mod = nc.dram_tensor('mod_s', [6, H], F32, kind=dbg).ap()
    d_wgq = nc.dram_tensor('wgq_s', [KC, 128, I], BF16, kind='Internal').ap()
    d_wuq = nc.dram_tensor('wuq_s', [KC, 128, I], BF16, kind='Internal').ap()

    r_hfm = [S_.res(f'd.hfm{t}') for t in range(8)]
    _x2g = [S_.res(f'd.x2g{i}') for i in range(4)]
    r_x2 = [_x2g[t // 8] for t in range(NT)]
    r_hmq = [S_.res(f'd.hmq{t}') for t in range(NT)]
    r_mod_d = S_.res('d.mod')
    _wgq1 = S_.res('d.wgq')
    _wuq1 = S_.res('d.wuq')
    r_wgq = [_wgq1] * 6
    r_wuq = [_wuq1] * 6
    r_out = S_.res('d.out')

    ps = []
    for i in range(8):
        dt_i = BF16 if i in (4, 7) else F32
        h = nc.alloc_psum_tensor(f'pb{i}', [128, 512], dt_i)
        ps.append((h.ap(), S_.res(f'pb{i}', is_psum=True)))

    idf, r_idf = sb('identF', [128, 128], F32)
    idb, r_idb = sb('identB', [128, 128], BF16)
    ca1f, r_ca1f = sb('ca1f', [128, H], BF16)
    saf, r_saf = sb('saf', [128, H], BF16)
    gaf, r_gaf = sb('gaf', [128, H], BF16)
    modBC_parts = [(ca1f, r_ca1f), (saf, r_saf), (gaf, r_gaf)]
    kfm, r_kfm = sb('K_fm', [128, KC * S], BF16)
    vtm, r_vtm = sb('V_tm', [128, NT * H], BF16)
    stats, r_stats = sb('stats', [128, 32], F32)
    onesb, r_onesb = sb('onesb', [1, 128], F32)
    S_.op('vector', lambda e: e.memset(onesb, 1.0), writes=[r_onesb],
          tag='p0.ones')

    def bcast_row(vec_ap, vec_res, dst_ap, dst_res, bank_i, tag):
        bank, r_bank = ps[bank_i]
        for sl in range(2):
            S_.op('tensor', lambda e, a=bank, b=onesb,
                  c=vec_ap[0:1, sl * 512:(sl + 1) * 512]:
                  e.matmul(a, b, c, start=True, stop=True),
                  reads=[r_onesb, vec_res], writes=[r_bank],
                  tag=tag + f'.mm{sl}')
            S_.op('vector', lambda e, a=dst_ap[:, sl * 512:(sl + 1) * 512],
                  b=bank: e.tensor_copy(out=a, in_=b),
                  reads=[r_bank], writes=[dst_res], tag=tag + f'.ev{sl}')

    def quant(x_ap, x_res, n, tmp, deq_out, deq_res, tag, deq_engine='vector',
              xn_engine='vector'):
        nb = n // 16
        (bm, bm_r), (sc, sc_r), (rsc, rsc_r) = tmp['bm'], tmp['sc'], tmp['rsc']
        (xn, xn_r), (z, z_r) = tmp['xn'], tmp['z']
        (y, y_r) = tmp['y'] if 'y' in tmp else (x_ap, x_res)
        x3 = x_ap.rearrange('p (b k) -> p b k', k=16)
        S_.op('vector', lambda e, a=bm[:, 0:nb], b=x3: e.tensor_reduce(
            out=a, in_=b, axis=AXX, op=ALU.max, apply_absolute_value=True),
            reads=[x_res], writes=[bm_r], tag=tag + '.bm')
        S_.op('vector', lambda e, a=sc[:, 0:nb], b=bm[:, 0:nb]: e.tensor_scalar(
            out=a, in0=b, scalar1=float(np.float32(1.0 / 6.0)), scalar2=1e-12,
            op0=ALU.mult, op1=ALU.max),
            reads=[bm_r], writes=[sc_r], tag=tag + '.sc')
        S_.op('vector', lambda e, a=rsc[:, 0:nb], b=sc[:, 0:nb]: e.reciprocal(
            out=a, in_=b), reads=[sc_r], writes=[rsc_r], tag=tag + '.rsc')
        rb = rsc[:, 0:nb].unsqueeze(2).broadcast_to([128, nb, 16])
        S_.op(xn_engine, lambda e, a=xn[:, 0:n].rearrange('p (b k) -> p b k', k=16),
              b=x3, c=rb: e.tensor_tensor(out=a, in0=b, in1=c, op=ALU.mult),
              reads=[x_res, rsc_r], writes=[xn_r], tag=tag + '.xn')
        S_.op('vector', lambda e, a=y[:, 0:n], b=xn[:, 0:n]: e._custom_dve(
            OP_FP4RND, out=a, in0=b, s0=4.0, s1=16.0, imm2=HMAGIC),
            reads=[xn_r], writes=[y_r], tag=tag + '.opA')
        S_.op('vector', lambda e, a=z[:, 0:n], b=y[:, 0:n], c=xn[:, 0:n]:
              e._custom_dve(OP_FP4SUB, out=a, in0=b, in1=c, s0=4.0, s1=16.0,
                            imm2=HMAGIC),
              reads=[y_r, xn_r], writes=[z_r], tag=tag + '.opB')
        sbb = sc[:, 0:nb].unsqueeze(2).broadcast_to([128, nb, 16])
        S_.op(deq_engine, lambda e,
              a=deq_out.rearrange('p (b k) -> p b k', k=16),
              b=z[:, 0:n].rearrange('p (b k) -> p b k', k=16), c=sbb:
              e.tensor_tensor(out=a, in0=b, in1=c, op=ALU.mult),
              reads=[z_r, sc_r], writes=[deq_res], tag=tag + '.deq')

    def ln(x_ap, x_res, trash, trash_r, y_ap, y_res, tag):
        st = stats
        S_.op('vector', lambda e, a=st[:, 0:1], b=x_ap: e.reduce_sum(
            out=a, in_=b, axis=AXX), reads=[x_res], writes=[r_stats],
            tag=tag + '.sum')
        S_.op('scalar', lambda e, a=trash, b=x_ap, c=st[:, 1:2]: e.activation(
            out=a, in_=b, func=AF.Square, accum_out=c),
            reads=[x_res], writes=[trash_r, r_stats], tag=tag + '.sq')
        S_.op('vector', lambda e, a=st[:, 2:3], b=st[:, 0:1]: e.tensor_scalar(
            out=a, in0=b, scalar1=INV_H, scalar2=None, op0=ALU.mult),
            reads=[r_stats], writes=[r_stats], tag=tag + '.m')
        S_.op('vector', lambda e, a=st[:, 3:4], b=st[:, 1:2]: e.tensor_scalar(
            out=a, in0=b, scalar1=INV_H, scalar2=None, op0=ALU.mult),
            reads=[r_stats], writes=[r_stats], tag=tag + '.ex2')
        S_.op('vector', lambda e, a=st[:, 4:5], b=st[:, 2:3]: e.tensor_tensor(
            out=a, in0=b, in1=b, op=ALU.mult),
            reads=[r_stats], writes=[r_stats], tag=tag + '.m2')
        S_.op('vector', lambda e, a=st[:, 5:6], b=st[:, 3:4], c=st[:, 4:5]:
              e.tensor_tensor(out=a, in0=b, in1=c, op=ALU.subtract),
              reads=[r_stats], writes=[r_stats], tag=tag + '.var')
        S_.op('vector', lambda e, a=st[:, 6:7], b=st[:, 5:6]: e.tensor_scalar(
            out=a, in0=b, scalar1=EPS, scalar2=None, op0=ALU.add),
            reads=[r_stats], writes=[r_stats], tag=tag + '.ve')
        S_.op('vector', lambda e, a=st[:, 7:8], b=st[:, 6:7]: e.reciprocal(
            out=a, in_=b), reads=[r_stats], writes=[r_stats], tag=tag + '.r')
        S_.op('scalar', lambda e, a=st[:, 8:9], b=st[:, 7:8]: e.activation(
            out=a, in_=b, func=AF.Sqrt),
            reads=[r_stats], writes=[r_stats], tag=tag + '.s0')
        S_.op('vector', lambda e, a=st[:, 9:10], b=st[:, 8:9]: e.tensor_tensor(
            out=a, in0=b, in1=b, op=ALU.mult),
            reads=[r_stats], writes=[r_stats], tag=tag + '.nsq')
        S_.op('vector', lambda e, a=st[:, 10:11], b=st[:, 9:10], c=st[:, 6:7]:
              e.tensor_tensor(out=a, in0=b, in1=c, op=ALU.mult),
              reads=[r_stats], writes=[r_stats], tag=tag + '.nt2')
        S_.op('vector', lambda e, a=st[:, 10:11], b=st[:, 10:11]: e.tensor_scalar(
            out=a, in0=b, scalar1=-0.5, scalar2=1.5, op0=ALU.mult, op1=ALU.add),
            reads=[r_stats], writes=[r_stats], tag=tag + '.nt3')
        S_.op('vector', lambda e, a=st[:, 11:12], b=st[:, 8:9], c=st[:, 10:11]:
              e.tensor_tensor(out=a, in0=b, in1=c, op=ALU.mult),
              reads=[r_stats], writes=[r_stats], tag=tag + '.rstd')
        S_.op('vector', lambda e, a=y_ap, b=x_ap, c=st[:, 2:3], d=st[:, 11:12]:
              e.tensor_scalar(out=a, in0=b, scalar1=c, scalar2=d,
                              op0=ALU.subtract, op1=ALU.mult),
              reads=[x_res, r_stats], writes=[y_res], tag=tag + '.norm')

    def prep_wT(d_w, dst, r_dst, stg, nm, banks=(1, 2)):
        dst3 = dst.rearrange('p (a b) -> p a b', b=H)
        for r in range(KC):
            wst, r_wst = stg[r % 2]
            S_.dma('sync', lambda e, a=wst, b=d_w[r * 128:(r + 1) * 128, :]:
                   e.dma_start(out=a, in_=b), writes=[r_wst], tag=f'{nm}.ld{r}')
            for half in range(2):
                bank, r_bank = ps[banks[half]]
                for j4 in range(4):
                    j = half * 4 + j4
                    S_.op('tensor', lambda e, a=bank[:, j4 * 128:(j4 + 1) * 128],
                          b=wst[:, j * 128:(j + 1) * 128]: e.transpose(a, b, idf),
                          reads=[r_wst, r_idf], writes=[r_bank],
                          tag=f'{nm}.tr{r}.{j}')
                dv = dst3[:, half * 4:half * 4 + 4, r * 128:r * 128 + 128]
                S_.op('scalar', lambda e, a=dv, b=bank[:, 0:512].rearrange(
                    'p (a b) -> p a b', b=128): e.copy(out=a, in_=b),
                    reads=[r_bank], writes=[r_dst], tag=f'{nm}.ev{r}.{half}')

    S_.op('gpsimd', lambda e: e.memset(idf, 0.0), writes=[r_idf], tag='p0.i0')
    S_.op('gpsimd', lambda e: e.affine_select(
        out=idf, in_=idf, compare_op=ALU.not_equal, fill=1.0, base=0,
        pattern=[[-1, 128]], channel_multiplier=1),
        reads=[r_idf], writes=[r_idf], tag='p0.i1')
    S_.op('gpsimd', lambda e: e.memset(idb, 0.0), writes=[r_idb], tag='p0.i2')
    S_.op('gpsimd', lambda e: e.affine_select(
        out=idb, in_=idb, compare_op=ALU.not_equal, fill=1.0, base=0,
        pattern=[[-1, 128]], channel_multiplier=1),
        reads=[r_idb], writes=[r_idb], tag='p0.i3')

    modstage, r_modstage = sb('modstage', [1, 6 * H], F32)
    csb, r_csb = sb('csb', [128, 8], F32)
    csbf, r_csbf = sb('csbf', [128, 8], BF16)
    wmst = [sb(f'wmst{i}', [128, 512], F32) for i in range(2)]
    wmbf = [sb(f'wmbf{i}', [128, 512], BF16) for i in range(2)]
    t1k = [sb(f't1k{i}', [1, H], F32) for i in range(3)]

    for kc in range(KC):
        S_.dma('sync', lambda e, a=csb[:, kc:kc + 1],
               b=d_c[0:1, kc * 128:(kc + 1) * 128].transpose([1, 0]):
               e.dma_start(out=a, in_=b), writes=[r_csb], tag=f'p0.c{kc}')
    S_.op('scalar', lambda e: e.activation(out=csbf, in_=csb, func=AF.Silu),
          reads=[r_csb], writes=[r_csbf], tag='p0.sc')
    pm, r_pm = ps[0]
    for sl in range(12):
        for kc in range(KC):
            i = (sl * KC + kc) % 2
            wst, r_wst = wmst[i]
            wbf, r_wbf = wmbf[i]
            S_.dma('sync', lambda e, a=wst,
                   b=d_wm[kc * 128:(kc + 1) * 128, sl * 512:(sl + 1) * 512]:
                   e.dma_start(out=a, in_=b), writes=[r_wst],
                   tag=f'p0.wl{sl}.{kc}')
            S_.op('scalar', lambda e, a=wbf, b=wst: e.copy(out=a, in_=b),
                  reads=[r_wst], writes=[r_wbf], tag=f'p0.wc{sl}.{kc}')
            S_.op('tensor', lambda e, a=pm[0:1, :], b=csbf[:, kc:kc + 1], c=wbf,
                  st=(kc == 0), sp=(kc == KC - 1):
                  e.matmul(a, b, c, start=st, stop=sp),
                  reads=[r_csbf, r_wbf], writes=[r_pm], tag=f'p0.wm{sl}.{kc}')
        S_.op('vector', lambda e, a=modstage[0:1, sl * 512:(sl + 1) * 512],
              b=pm[0:1, :]: e.tensor_copy(out=a, in_=b),
              reads=[r_pm], writes=[r_modstage], tag=f'p0.we{sl}')

    tv, r_tv = t1k[0]
    ta, r_ta = t1k[1]
    tb, r_tb = t1k[2]

    def derive(ji, jo, kind, bc_col):
        src = modstage[0:1, ji * H:(ji + 1) * H]
        if kind == 'copy':
            S_.op('vector', lambda e, a=tv, b=src: e.tensor_copy(out=a, in_=b),
                  reads=[r_modstage], writes=[r_tv], tag=f'p0.d{jo}a')
        elif kind == 'plus1':
            S_.op('vector', lambda e, a=tv, b=src: e.tensor_scalar(
                out=a, in0=b, scalar1=1.0, scalar2=None, op0=ALU.add),
                reads=[r_modstage], writes=[r_tv], tag=f'p0.d{jo}a')
        else:
            S_.op('vector', lambda e, a=ta, b=src: e.tensor_scalar(
                out=a, in0=b, scalar1=0.0, scalar2=None, op0=ALU.is_ge),
                reads=[r_modstage], writes=[r_ta], tag=f'p0.d{jo}b')
            S_.op('vector', lambda e, a=ta: e.tensor_scalar(
                out=a, in0=a, scalar1=2.0, scalar2=-1.0, op0=ALU.mult,
                op1=ALU.add), reads=[r_ta], writes=[r_ta], tag=f'p0.d{jo}c')
            S_.op('vector', lambda e, a=tb, b=src: e.tensor_scalar(
                out=a, in0=b, scalar1=-1.0, scalar2=None, op0=ALU.mult),
                reads=[r_modstage], writes=[r_tb], tag=f'p0.d{jo}d')
            S_.op('vector', lambda e, a=tb, b=src: e.tensor_tensor(
                out=a, in0=b, in1=a, op=ALU.max),
                reads=[r_modstage, r_tb], writes=[r_tb], tag=f'p0.d{jo}e')
            S_.op('vector', lambda e, a=tb: e.tensor_scalar(
                out=a, in0=a, scalar1=0.25, scalar2=None, op0=ALU.max),
                reads=[r_tb], writes=[r_tb], tag=f'p0.d{jo}f')
            S_.op('vector', lambda e, a=tv, b=ta, c=tb: e.tensor_tensor(
                out=a, in0=b, in1=c, op=ALU.mult),
                reads=[r_ta, r_tb], writes=[r_tv], tag=f'p0.d{jo}g')
        S_.dma('scalar', lambda e, a=d_mod[jo:jo + 1, :], b=tv:
               e.dma_start(out=a, in_=b), reads=[r_tv], writes=[r_mod_d],
               tag=f'p0.d{jo}s')
        if bc_col is not None:
            dap, dres = modBC_parts[bc_col]
            bcast_row(tv, r_tv, dap, dres, 1, f'p0.d{jo}bc')

    derive(1, 0, 'plus1', 0)
    derive(0, 1, 'copy', 1)
    derive(2, 2, 'gate', 2)
    derive(4, 3, 'plus1', None)
    derive(3, 4, 'copy', None)
    derive(5, 5, 'gate', None)

    al.free((modstage, r_modstage)); al.free((csb, r_csb))
    al.free((csbf, r_csbf))
    al.free(wmst[0]); al.free(wmst[1]); al.free(wmbf[0]); al.free(wmbf[1])
    al.free(t1k[0]); al.free(t1k[1]); al.free(t1k[2])

    wkt, r_wkt = sb('WkT', [128, KC * H], BF16)
    wvt, r_wvt = sb('WvT', [128, KC * H], BF16)
    wstg = [sb(f'wstg{i}', [128, H], F32) for i in range(2)]
    prep_wT(d_wk, wkt, r_wkt, wstg, 'p0.wk')
    prep_wT(d_wv, wvt, r_wvt, wstg, 'p0.wv')
    al.free(wstg[0]); al.free(wstg[1])

    if upto == 'P0':
        with nc.Block() as block:
            S_.finalize(nc, block)
        mybir.codegen_inst_isa_subclasses(nc)
        _BUILD_CACHE[key] = nc
        return nc

    xb = [sb(f'xb{i}', [128, H], F32) for i in range(4)]
    hb, r_hb = sb('hb', [128, H], F32)
    yb, r_yb = sb('yb', [128, H], F32)
    sqtr, r_sqtr = sb('sqtr', [128, H], F32)
    statsB, r_statsB = sb('statsB', [128, 64], F32)
    hfmg, r_hfmg = sb('hfmg', [128, KC * 512], BF16)
    hfmg3 = hfmg.rearrange('p (a b) -> p a b', b=512)

    def b_stats4(t0):
        st = stats
        for j in range(4):
            t = t0 + j
            xt, r_xt = xb[t % 4]
            S_.dma('sync', lambda e, a=xt, b=d_x[t * 128:(t + 1) * 128, :]:
                   e.dma_start(out=a, in_=b), writes=[r_xt], tag=f'B.x{t}')
            S_.op('vector', lambda e, a=st[:, j:j + 1], b=xt: e.reduce_sum(
                out=a, in_=b, axis=AXX), reads=[r_xt], writes=[r_stats],
                tag=f'B.ss{t}')
            S_.op('scalar', lambda e, a=sqtr, b=xt, c=st[:, 4 + j:5 + j]:
                  e.activation(out=a, in_=b, func=AF.Square, accum_out=c),
                  reads=[r_xt], writes=[r_sqtr, r_stats], tag=f'B.sq{t}')
        sB = statsB[:, 2 * (t0 % 16):2 * (t0 % 16) + 8].rearrange(
            'p (a b) -> p a b', b=2)
        S_.op('vector', lambda e, a=sB[:, :, 0:1].squeeze(2), b=st[:, 0:4]:
              e.tensor_scalar(out=a, in0=b, scalar1=INV_H, scalar2=None,
                              op0=ALU.mult),
              reads=[r_stats], writes=[r_statsB], tag=f'B.sm{t0}')
        S_.op('vector', lambda e, a=st[:, 8:12], b=st[:, 4:8]: e.tensor_scalar(
            out=a, in0=b, scalar1=INV_H, scalar2=None, op0=ALU.mult),
            reads=[r_stats], writes=[r_stats], tag=f'B.se{t0}')
        S_.op('vector', lambda e, a=st[:, 12:16], b=sB[:, :, 0:1].squeeze(2):
              e.tensor_tensor(out=a, in0=b, in1=b, op=ALU.mult),
              reads=[r_statsB], writes=[r_stats], tag=f'B.sm2{t0}')
        S_.op('vector', lambda e, a=st[:, 16:20], b=st[:, 8:12], c=st[:, 12:16]:
              e.tensor_tensor(out=a, in0=b, in1=c, op=ALU.subtract),
              reads=[r_stats], writes=[r_stats], tag=f'B.sv{t0}')
        S_.op('vector', lambda e, a=st[:, 16:20], b=st[:, 16:20]: e.tensor_scalar(
            out=a, in0=b, scalar1=EPS, scalar2=None, op0=ALU.add),
            reads=[r_stats], writes=[r_stats], tag=f'B.sve{t0}')
        S_.op('vector', lambda e, a=st[:, 20:24], b=st[:, 16:20]: e.reciprocal(
            out=a, in_=b), reads=[r_stats], writes=[r_stats], tag=f'B.sr{t0}')
        S_.op('scalar', lambda e, a=st[:, 24:28], b=st[:, 20:24]: e.activation(
            out=a, in_=b, func=AF.Sqrt),
            reads=[r_stats], writes=[r_stats], tag=f'B.ssq{t0}')
        S_.op('vector', lambda e, a=st[:, 8:12], b=st[:, 24:28]: e.tensor_tensor(
            out=a, in0=b, in1=b, op=ALU.mult),
            reads=[r_stats], writes=[r_stats], tag=f'B.sn1{t0}')
        S_.op('vector', lambda e, a=st[:, 12:16], b=st[:, 8:12], c=st[:, 16:20]:
              e.tensor_tensor(out=a, in0=b, in1=c, op=ALU.mult),
              reads=[r_stats], writes=[r_stats], tag=f'B.sn2{t0}')
        S_.op('vector', lambda e, a=st[:, 12:16], b=st[:, 12:16]: e.tensor_scalar(
            out=a, in0=b, scalar1=-0.5, scalar2=1.5, op0=ALU.mult, op1=ALU.add),
            reads=[r_stats], writes=[r_stats], tag=f'B.sn3{t0}')
        S_.op('vector', lambda e, a=sB[:, :, 1:2].squeeze(2), b=st[:, 24:28],
              c=st[:, 12:16]: e.tensor_tensor(out=a, in0=b, in1=c, op=ALU.mult),
              reads=[r_stats], writes=[r_statsB], tag=f'B.srstd{t0}')

    def b_head(t):
        xt, r_xt = xb[t % 4]
        col = 2 * (t % 16)
        S_.op('vector', lambda e, a=yb, b=xt, c=statsB[:, col:col + 1],
              d=statsB[:, col + 1:col + 2]: e.tensor_scalar(
              out=a, in0=b, scalar1=c, scalar2=d, op0=ALU.subtract,
              op1=ALU.mult),
              reads=[r_xt, r_statsB], writes=[r_yb], tag=f'B.nrm{t}')
        S_.op('vector', lambda e, a=hb, b=yb, c=ca1f:
              e.tensor_tensor(out=a, in0=b, in1=c, op=ALU.mult),
              reads=[r_yb, r_ca1f], writes=[r_hb], tag=f'B.h1{t}')
        S_.op('vector', lambda e, a=xt, b=hb, c=saf:
              e.tensor_tensor(out=a, in0=b, in1=c, op=ALU.add),
              reads=[r_hb, r_saf], writes=[r_xt], tag=f'B.h2{t}')

    def b_tail(t):
        tl = t % 4
        xt, r_xt = xb[t % 4]
        for half in range(2):
            bank, r_bank = ps[2 + half]
            for j4 in range(4):
                j = half * 4 + j4
                S_.op('tensor', lambda e, a=bank[:, j4 * 128:(j4 + 1) * 128],
                      b=xt[:, j * 128:(j + 1) * 128]: e.transpose(a, b, idf),
                      reads=[r_xt, r_idf], writes=[r_bank],
                      tag=f'B.tr{t}.{j}')
            dv = hfmg3[:, half * 4:half * 4 + 4, tl * 128:tl * 128 + 128]
            S_.op('scalar', lambda e, a=dv, b=bank[:, 0:512].rearrange(
                'p (a b) -> p a b', b=128): e.copy(out=a, in_=b),
                reads=[r_bank], writes=[r_hfmg], tag=f'B.he{t}.{half}')

    for g in range(8):
        b_stats4(g * 4)
        for tl in range(4):
            t = g * 4 + tl
            b_head(t)
            if tl > 0:
                b_tail(t - 1)
        b_tail(g * 4 + 3)
        for oc in range(KC):
            bank, r_bank = ps[5 + (oc % 2)]
            for kc in range(KC):
                S_.op('tensor', lambda e, a=bank,
                      b=wkt[:, kc * H + oc * 128:kc * H + oc * 128 + 128],
                      c=hfmg[:, kc * 512:(kc + 1) * 512],
                      st=(kc == 0), sp=(kc == KC - 1):
                      e.matmul(a, b, c, start=st, stop=sp),
                      reads=[r_wkt, r_hfmg], writes=[r_bank],
                      tag=f'B.km{g}.{oc}.{kc}')
            S_.op('scalar', lambda e,
                  a=kfm[:, oc * S + g * 512:oc * S + (g + 1) * 512], b=bank:
                  e.copy(out=a, in_=b),
                  reads=[r_bank], writes=[r_kfm], tag=f'B.ke{g}.{oc}')
        for tl in range(4):
            t = g * 4 + tl
            for hs in range(2):
                bank, r_bank = ps[5 + ((tl * 2 + hs) % 2)]
                for kc in range(KC):
                    S_.op('tensor', lambda e, a=bank,
                          b=hfmg[:, kc * 512 + tl * 128:kc * 512 + tl * 128 + 128],
                          c=wvt[:, kc * H + hs * 512:kc * H + (hs + 1) * 512],
                          st=(kc == 0), sp=(kc == KC - 1):
                          e.matmul(a, b, c, start=st, stop=sp),
                          reads=[r_wvt, r_hfmg], writes=[r_bank],
                          tag=f'B.vm{t}.{hs}.{kc}')
                S_.op('scalar', lambda e,
                      a=vtm[:, t * H + hs * 512:t * H + (hs + 1) * 512], b=bank:
                      e.copy(out=a, in_=b),
                      reads=[r_bank], writes=[r_vtm], tag=f'B.ve{t}.{hs}')
        ho = d_hfm[:, :, g * 512:(g + 1) * 512].transpose([1, 0, 2])
        S_.dma('scalar', lambda e, a=ho, b=hfmg3: e.dma_start(out=a, in_=b),
               reads=[r_hfmg], writes=[r_hfm[g]], tag=f'B.hs{g}')

    al.free((wkt, r_wkt)); al.free((wvt, r_wvt)); al.free((hfmg, r_hfmg))
    for i in range(4):
        al.free(xb[i])
    al.free((hb, r_hb)); al.free((yb, r_yb)); al.free((sqtr, r_sqtr))
    al.free((statsB, r_statsB))
    al.free((ca1f, r_ca1f)); al.free((saf, r_saf))

    if upto == 'B':
        with nc.Block() as block:
            S_.finalize(nc, block)
        mybir.codegen_inst_isa_subclasses(nc)
        _BUILD_CACHE[key] = nc
        return nc

    wqt, r_wqt = sb('WqT', [128, KC * H], BF16)
    wstg2 = [sb(f'wstg2_{i}', [128, H], F32) for i in range(2)]
    prep_wT(d_wq, wqt, r_wqt, wstg2, 'C.wq', banks=(0, 1))
    al.free(wstg2[0]); al.free(wstg2[1])
    msk, r_msk = sb('masks', [128, 4 * 512], BF16)
    for r in range(4):
        slm = msk[:, r * 512:(r + 1) * 512]
        S_.op('gpsimd', lambda e, a=slm: e.memset(a, 1.0), writes=[r_msk],
              tag=f'p0.m0{r}')
        S_.op('gpsimd', lambda e, a=slm, base=128 * r: e.affine_select(
            out=a, in_=a, compare_op=ALU.is_ge, fill=0.0, base=base,
            pattern=[[-1, 512]], channel_multiplier=1),
            reads=[r_msk], writes=[r_msk], tag=f'p0.m1{r}')
    hfmc, r_hfmc = sb('hfmc', [128, KC * 512], BF16)
    qg, r_qg = sb('qg', [128, KC * 512], BF16)
    pbuf2 = [sb(f'pbuf{i}', [128, S - 512], BF16) for i in range(2)]
    dscr, r_dscr = sb('dscr', [128, 512], BF16)
    pmb2 = [sb(f'pmb{i}', [128, 512], BF16) for i in range(2)]
    ptb = [sb(f'ptb{i}', [128, 512], BF16) for i in range(2)]
    cxb = [sb(f'cxb{i}', [128, H], F32) for i in range(2)]
    x2b2 = [sb('x2b0', [128, H], F32)] * 2
    ebuf, r_ebuf = sb('ebuf', [128, 512], F32)
    ebufB, r_ebufB = sb('ebufB', [128, 512], F32)
    obuf, r_obuf = sb('obuf', [128, H], BF16)
    stc2 = [sb(f'stc{i}', [128, 32], F32) for i in range(2)]

    def c_rounds(qc):
        full = list(range(qc))
        if len(full) > 4:
            return [full[:4], full[4:] + [qc]]
        if full:
            return [full, [qc]]
        return [[qc]]

    def c_part1(qi):
        par = qi % 2
        st, r_st = stc2[par]
        pbuf, r_pbuf = pbuf2[par]
        pmb, r_pmb = pmb2[par]
        xt, r_xt = cxb[par]
        qc = qi // 4
        rr = qi % 4
        S_.dma('sync', lambda e, a=xt, b=d_x[qi * 128:(qi + 1) * 128, :]:
               e.dma_start(out=a, in_=b), writes=[r_xt], tag=f'C.x{qi}')
        rounds = c_rounds(qc)
        nrounds = len(rounds)
        for ri, cs in enumerate(rounds):
            for k_, ci in enumerate(cs):
                bank, r_bank = ps[k_]
                for hc in range(KC):
                    S_.op('tensor', lambda e, a=bank,
                          b=qg[:, hc * 512 + rr * 128:hc * 512 + rr * 128 + 128],
                          c=kfm[:, hc * S + ci * 512:hc * S + (ci + 1) * 512],
                          st=(hc == 0), sp=(hc == KC - 1):
                          e.matmul(a, b, c, start=st, stop=sp),
                          reads=[r_qg, r_kfm], writes=[r_bank],
                          tag=f'C.sm{qi}.{ci}.{hc}')
            for k_, ci in enumerate(cs):
                bank, r_bank = ps[k_]
                S_.op('vector', lambda e, a=st[:, 16 + ci:17 + ci], b=bank:
                      e.reduce_max(out=a, in_=b, axis=AXX),
                      reads=[r_bank], writes=[r_st], tag=f'C.mx{qi}.{ci}')
            c0, c1 = cs[0], cs[-1]
            S_.op('vector', lambda e, a=st[:, 12 + ri:13 + ri],
                  b=st[:, 16 + c0:17 + c1]: e.reduce_max(
                  out=a, in_=b, axis=AXX),
                  reads=[r_st], writes=[r_st], tag=f'C.rm{qi}.{ri}')
            S_.op('vector', lambda e, a=st[:, 14 + ri:15 + ri],
                  b=st[:, 12 + ri:13 + ri]: e.tensor_scalar(
                  out=a, in0=b, scalar1=-SMSCALE, scalar2=None, op0=ALU.mult),
                  reads=[r_st], writes=[r_st], tag=f'C.nb{qi}.{ri}')
            for k_, ci in enumerate(cs):
                bank, r_bank = ps[k_]
                dst = dscr if ci == qc else pbuf[:, ci * 512:(ci + 1) * 512]
                r_dst = r_dscr if ci == qc else r_pbuf
                S_.op('scalar', lambda e,
                      a=dst, b=bank,
                      c=st[:, 14 + ri:15 + ri], d=st[:, 24 + ci:25 + ci]:
                      e.activation(out=a, in_=b, func=AF.Exp, scale=SMSCALE,
                                   bias=c, accum_out=d),
                      reads=[r_bank, r_st], writes=[r_dst, r_st],
                      tag=f'C.ex{qi}.{ci}')
        if nrounds == 2:
            S_.op('vector', lambda e, a=st[:, 11:12], b=st[:, 12:13],
                  c=st[:, 13:14]: e.tensor_tensor(
                  out=a, in0=b, in1=c, op=ALU.max),
                  reads=[r_st], writes=[r_st], tag=f'C.gm{qi}')
            S_.op('vector', lambda e, a=st[:, 18:20], b=st[:, 12:14],
                  c=st[:, 11:12]: e.tensor_scalar(
                  out=a, in0=b, scalar1=c, scalar2=SMSCALE,
                  op0=ALU.subtract, op1=ALU.mult),
                  reads=[r_st], writes=[r_st], tag=f'C.fd{qi}')
            S_.op('scalar', lambda e, a=st[:, 20:22], b=st[:, 18:20]:
                  e.activation(out=a, in_=b, func=AF.Exp),
                  reads=[r_st], writes=[r_st], tag=f'C.fe{qi}')
        S_.op('vector', lambda e, a=pmb, b=dscr,
              c=msk[:, (qi % 4) * 512:(qi % 4 + 1) * 512]:
              e.tensor_tensor(out=a, in0=b, in1=c, op=ALU.mult),
              reads=[r_dscr, r_msk], writes=[r_pmb], tag=f'C.pm{qi}')
        S_.op('vector', lambda e, a=st[:, 24 + qc:25 + qc], b=pmb:
              e.reduce_sum(out=a, in_=b, axis=AXX),
              reads=[r_pmb], writes=[r_st], tag=f'C.ls{qi}')
        if nrounds == 2:
            na = len(rounds[0])
            S_.op('vector', lambda e, a=st[:, 22:23],
                  b=st[:, 24:24 + na]: e.reduce_sum(out=a, in_=b, axis=AXX),
                  reads=[r_st], writes=[r_st], tag=f'C.la{qi}')
            S_.op('vector', lambda e, a=st[:, 23:24],
                  b=st[:, 24 + na:25 + qc]: e.reduce_sum(
                  out=a, in_=b, axis=AXX),
                  reads=[r_st], writes=[r_st], tag=f'C.lb{qi}')
            S_.op('vector', lambda e, a=st[:, 22:24], b=st[:, 22:24],
                  c=st[:, 20:22]: e.tensor_tensor(
                  out=a, in0=b, in1=c, op=ALU.mult),
                  reads=[r_st], writes=[r_st], tag=f'C.lw{qi}')
            S_.op('vector', lambda e, a=st[:, 15:16], b=st[:, 22:23],
                  c=st[:, 23:24]: e.tensor_tensor(
                  out=a, in0=b, in1=c, op=ALU.add),
                  reads=[r_st], writes=[r_st], tag=f'C.l{qi}')
        else:
            S_.op('vector', lambda e, a=st[:, 15:16], b=st[:, 17:18],
                  c=st[:, 24 + qc:25 + qc]: e.tensor_copy(out=a, in_=c),
                  reads=[r_st], writes=[r_st], tag=f'C.l{qi}')
        S_.op('vector', lambda e, a=st[:, 10:11], b=st[:, 15:16]:
              e.reciprocal(out=a, in_=b),
              reads=[r_st], writes=[r_st], tag=f'C.rl{qi}')

    def c_part2(qi):
        par = qi % 2
        st, r_st = stc2[par]
        pbuf, r_pbuf = pbuf2[par]
        pmb, r_pmb = pmb2[par]
        xt, r_xt = cxb[par]
        x2b, r_x2b = x2b2[par]
        qc = qi // 4
        rounds = c_rounds(qc)
        nrounds = len(rounds)
        trbanks = [ps[7], ps[4]]
        ob = [ps[5], ps[6]]
        rbound = []
        if nrounds == 2:
            na = len(rounds[0])
            rbound.append((0, 4 * na - 1, 0))
            rbound.append((4 * na, qi, 1))
        else:
            rbound.append((0, qi, 0))
        batches = []
        for (j0, j1, ri) in rbound:
            for jq in range(j0, j1 + 1, 4):
                batches.append((jq, min(4, j1 + 1 - jq), j0, j1, ri))

        def emit_tr(bi):
            jq, jn, j0, j1, ri = batches[bi]
            half = (jq // 4) % 2
            trbank_full, r_trb = trbanks[half]
            trbank = trbank_full[:, 0:jn * 128]
            pt, r_pt = ptb[half]
            for j4 in range(jn):
                j = jq + j4
                ci = j // 4
                if ci == qc:
                    src = pmb[:, (j % 4) * 128:(j % 4 + 1) * 128]
                    r_src = r_pmb
                else:
                    src = pbuf[:, j * 128:(j + 1) * 128]
                    r_src = r_pbuf
                S_.op('tensor', lambda e,
                      a=trbank_full[:, j4 * 128:(j4 + 1) * 128],
                      b=src, c=idb: e.transpose(a, b, c),
                      reads=[r_src, r_idb], writes=[r_trb],
                      tag=f'C.pt{qi}.{j}')
            ev_eng = 'scalar' if half == 0 else 'vector'
            ev = ((lambda e, a=pt[:, 0:jn * 128], b=trbank: e.copy(out=a, in_=b))
                  if ev_eng == 'scalar' else
                  (lambda e, a=pt[:, 0:jn * 128], b=trbank: e.tensor_copy(out=a, in_=b)))
            S_.op(ev_eng, ev,
                  reads=[r_trb], writes=[r_pt],
                  tag=f'C.pe{qi}.{jq}')

        def emit_av(bi):
            jq, jn, j0, j1, ri = batches[bi]
            half = (jq // 4) % 2
            pt, r_pt = ptb[half]
            for j4 in range(jn):
                j = jq + j4
                for hs in range(2):
                    bank, r_bank = ob[hs]
                    S_.op('tensor', lambda e, a=bank,
                          b=pt[:, j4 * 128:(j4 + 1) * 128],
                          c=vtm[:, j * H + hs * 512:j * H + (hs + 1) * 512],
                          st=(j == j0), sp=(j == j1):
                          e.matmul(a, b, c, start=st, stop=sp),
                          reads=[r_pt, r_vtm], writes=[r_bank],
                          tag=f'C.av{qi}.{j}.{hs}')
            if nrounds == 2 and ri == 0 and jq + jn == j1 + 1:
                for hs in range(2):
                    bank, r_bank = ob[hs]
                    S_.op('scalar', lambda e,
                          a=obuf[:, hs * 512:(hs + 1) * 512], b=bank,
                          c=st[:, 20:21]: e.activation(
                          out=a, in_=b, func=AF.Copy, scale=c),
                          reads=[r_bank, r_st], writes=[r_obuf],
                          tag=f'C.oe{qi}.{hs}')

        for s in range(len(batches) + 1):
            if s < len(batches):
                emit_tr(s)
            if s >= 1:
                emit_av(s - 1)
        for hs in range(2):
            bank, r_bank = ob[hs]
            if nrounds == 2:
                S_.op('vector', lambda e, a=ebuf, b=bank, c=st[:, 21:22],
                      d=obuf[:, hs * 512:(hs + 1) * 512]:
                      e.scalar_tensor_tensor(out=a, in0=b, scalar=c, in1=d,
                                             op0=ALU.mult, op1=ALU.add),
                      reads=[r_bank, r_st, r_obuf], writes=[r_ebuf],
                      tag=f'C.e0{qi}.{hs}')
                S_.op('vector', lambda e, a=ebufB, b=ebuf, c=st[:, 10:11],
                      d=gaf[:, hs * 512:(hs + 1) * 512]:
                      e.scalar_tensor_tensor(out=a, in0=b, scalar=c, in1=d,
                                             op0=ALU.mult, op1=ALU.mult),
                      reads=[r_ebuf, r_st, r_gaf], writes=[r_ebufB],
                      tag=f'C.e1{qi}.{hs}')
            else:
                S_.op('vector', lambda e, a=ebufB, b=bank, c=st[:, 10:11],
                      d=gaf[:, hs * 512:(hs + 1) * 512]:
                      e.scalar_tensor_tensor(out=a, in0=b, scalar=c, in1=d,
                                             op0=ALU.mult, op1=ALU.mult),
                      reads=[r_bank, r_st, r_gaf], writes=[r_ebufB],
                      tag=f'C.e1{qi}.{hs}')
            S_.op('vector', lambda e, a=x2b[:, hs * 512:(hs + 1) * 512],
                  b=ebufB, c=xt[:, hs * 512:(hs + 1) * 512]:
                  e.tensor_tensor(out=a, in0=b, in1=c, op=ALU.add),
                  reads=[r_ebufB, r_xt], writes=[r_x2b], tag=f'C.e2{qi}.{hs}')
        S_.dma('scalar', lambda e, a=d_x2[qi * 128:(qi + 1) * 128, :], b=x2b:
               e.dma_start(out=a, in_=b),
               reads=[r_x2b], writes=[r_x2[qi]], tag=f'C.xs{qi}')

    def load_hfmc(g):
        for qd, qu in enumerate(('sync', 'scalar', 'sync', 'scalar')):
            kc0 = qd * (KC // 4)
            hi = d_hfm[kc0:kc0 + KC // 4, :,
                       g * 512:(g + 1) * 512].transpose([1, 0, 2])
            S_.dma(qu, lambda e, a=hfmc[:, kc0 * 512:(kc0 + KC // 4) * 512]
                   .rearrange('p (a b) -> p a b', b=512),
                   b=hi: e.dma_start(out=a, in_=b),
                   reads=[r_hfm[g]], writes=[r_hfmc], tag=f'C.hl{g}.{qd}')

    load_hfmc(0)
    for g in range(8):
        for oc in range(KC):
            bank, r_bank = ps[5 + (oc % 2)]
            for kc in range(KC):
                S_.op('tensor', lambda e, a=bank,
                      b=wqt[:, kc * H + oc * 128:kc * H + oc * 128 + 128],
                      c=hfmc[:, kc * 512:(kc + 1) * 512],
                      st=(kc == 0), sp=(kc == KC - 1):
                      e.matmul(a, b, c, start=st, stop=sp),
                      reads=[r_wqt, r_hfmc], writes=[r_bank],
                      tag=f'C.qm{g}.{oc}.{kc}')
            S_.op('scalar', lambda e, a=qg[:, oc * 512:(oc + 1) * 512], b=bank:
                  e.copy(out=a, in_=b),
                  reads=[r_bank], writes=[r_qg], tag=f'C.qe{g}.{oc}')
        if g + 1 < 8:
            load_hfmc(g + 1)
        for rr in range(4):
            qi = g * 4 + rr
            c_part1(qi)
            if qi > 0:
                c_part2(qi - 1)
    c_part2(NT - 1)

    al.free((wqt, r_wqt))
    al.free((hfmc, r_hfmc)); al.free((qg, r_qg))
    al.free(pbuf2[0]); al.free(pbuf2[1]); al.free((dscr, r_dscr)); al.free(pmb2[0]); al.free(pmb2[1]); al.free(ptb[0]); al.free(ptb[1])
    al.free(cxb[0]); al.free(cxb[1]); al.free(x2b2[0])
    al.free(stc2[0]); al.free(stc2[1])
    al.free((ebuf, r_ebuf)); al.free((ebufB, r_ebufB)); al.free((obuf, r_obuf))
    al.free((kfm, r_kfm)); al.free((vtm, r_vtm)); al.free((gaf, r_gaf))
    al.free((msk, r_msk))

    if upto == 'C':
        with nc.Block() as block:
            S_.finalize(nc, block)
        mybir.codegen_inst_isa_subclasses(nc)
        _BUILD_CACHE[key] = nc
        return nc

    modDa, r_modDa = sb('modDa', [128, 2 * H], BF16)
    modDg, r_modDg = sb('modDg', [128, H], BF16)
    mdt, r_mdt = sb('mdt', [1, H], F32)
    for j in range(3):
        S_.dma('sync', lambda e, a=mdt, b=d_mod[3 + j:4 + j, :]:
               e.dma_start(out=a, in_=b), reads=[r_mod_d], writes=[r_mdt],
               tag=f'D.ml{j}')
        if j < 2:
            bcast_row(mdt, r_mdt, modDa[:, j * H:(j + 1) * H], r_modDa,
                      6, f'D.mb{j}')
        else:
            bcast_row(mdt, r_mdt, modDg, r_modDg, 6, f'D.mb{j}')
    al.free((mdt, r_mdt)); al.free((onesb, r_onesb))

    h2q, r_h2q = sb('h2q', [128, KC * S], BF16)
    r_h2qt = [S_.res(f'h2qt{t}') for t in range(NT)]
    wdt, r_wdt = sb('wdT', [128, IC * H], BF16)
    wdt3 = wdt.rearrange('p (a b) -> p a b', b=H)
    IH = I // 2
    tq = {
        'bm': sb('tq_bm', [128, 64], F32),
        'sc': sb('tq_sc', [128, 64], F32),
        'rsc': sb('tq_rsc', [128, 64], F32),
        'xn': sb('tq_xn', [128, H], F32),
        'z': sb('tq_z', [128, H], F32),
    }
    dxb = [sb(f'dxb{i}', [128, H], F32) for i in range(2)]
    dq1, r_dq1 = sb('dq1', [128, H], F32)
    dtr, r_dtr = sb('dtr', [128, H], F32)
    h2q3 = h2q.rearrange('p (a b) -> p a b', b=S)
    wldst1, r_wldst1 = sb('wldst1', [128, H], F32)

    statsD, r_statsD = sb('statsD', [128, 64], F32)

    def d1_stats4(t0):
        st = stats
        for j in range(4):
            t = t0 + j
            xt, r_xt = dxb[t % 2]
            S_.dma('sync', lambda e, a=xt, b=d_x2[t * 128:(t + 1) * 128, :]:
                   e.dma_start(out=a, in_=b),
                   reads=[r_x2[t]], writes=[r_xt], tag=f'D.sxl{t}')
            S_.op('vector', lambda e, a=st[:, j:j + 1], b=xt: e.reduce_sum(
                out=a, in_=b, axis=AXX), reads=[r_xt], writes=[r_stats],
                tag=f'D.ss{t}')
            S_.op('scalar', lambda e, a=dtr, b=xt, c=st[:, 4 + j:5 + j]:
                  e.activation(out=a, in_=b, func=AF.Square, accum_out=c),
                  reads=[r_xt], writes=[r_dtr, r_stats], tag=f'D.sq{t}')
        sD = statsD[:, 2 * t0:2 * t0 + 8].rearrange('p (a b) -> p a b', b=2)
        S_.op('vector', lambda e, a=sD[:, :, 0:1].squeeze(2), b=st[:, 0:4]:
              e.tensor_scalar(out=a, in0=b, scalar1=INV_H, scalar2=None,
                              op0=ALU.mult),
              reads=[r_stats], writes=[r_statsD], tag=f'D.sm{t0}')
        S_.op('vector', lambda e, a=st[:, 8:12], b=st[:, 4:8]: e.tensor_scalar(
            out=a, in0=b, scalar1=INV_H, scalar2=None, op0=ALU.mult),
            reads=[r_stats], writes=[r_stats], tag=f'D.se{t0}')
        S_.op('vector', lambda e, a=st[:, 12:16], b=sD[:, :, 0:1].squeeze(2):
              e.tensor_tensor(out=a, in0=b, in1=b, op=ALU.mult),
              reads=[r_statsD], writes=[r_stats], tag=f'D.sm2{t0}')
        S_.op('vector', lambda e, a=st[:, 16:20], b=st[:, 8:12], c=st[:, 12:16]:
              e.tensor_tensor(out=a, in0=b, in1=c, op=ALU.subtract),
              reads=[r_stats], writes=[r_stats], tag=f'D.sv{t0}')
        S_.op('vector', lambda e, a=st[:, 16:20], b=st[:, 16:20]: e.tensor_scalar(
            out=a, in0=b, scalar1=EPS, scalar2=None, op0=ALU.add),
            reads=[r_stats], writes=[r_stats], tag=f'D.sve{t0}')
        S_.op('vector', lambda e, a=st[:, 20:24], b=st[:, 16:20]: e.reciprocal(
            out=a, in_=b), reads=[r_stats], writes=[r_stats], tag=f'D.sr{t0}')
        S_.op('scalar', lambda e, a=st[:, 24:28], b=st[:, 20:24]: e.activation(
            out=a, in_=b, func=AF.Sqrt),
            reads=[r_stats], writes=[r_stats], tag=f'D.ssq{t0}')
        S_.op('vector', lambda e, a=st[:, 8:12], b=st[:, 24:28]: e.tensor_tensor(
            out=a, in0=b, in1=b, op=ALU.mult),
            reads=[r_stats], writes=[r_stats], tag=f'D.sn1{t0}')
        S_.op('vector', lambda e, a=st[:, 12:16], b=st[:, 8:12], c=st[:, 16:20]:
              e.tensor_tensor(out=a, in0=b, in1=c, op=ALU.mult),
              reads=[r_stats], writes=[r_stats], tag=f'D.sn2{t0}')
        S_.op('vector', lambda e, a=st[:, 12:16], b=st[:, 12:16]: e.tensor_scalar(
            out=a, in0=b, scalar1=-0.5, scalar2=1.5, op0=ALU.mult, op1=ALU.add),
            reads=[r_stats], writes=[r_stats], tag=f'D.sn3{t0}')
        S_.op('vector', lambda e, a=sD[:, :, 1:2].squeeze(2), b=st[:, 24:28],
              c=st[:, 12:16]: e.tensor_tensor(out=a, in0=b, in1=c, op=ALU.mult),
              reads=[r_stats], writes=[r_statsD], tag=f'D.srstd{t0}')

    def d1_prep(t):
        xt, r_xt = dxb[t % 2]
        S_.dma('sync', lambda e, a=xt, b=d_x2[t * 128:(t + 1) * 128, :]:
               e.dma_start(out=a, in_=b),
               reads=[r_x2[t]], writes=[r_xt], tag=f'D.xl{t}')
        S_.op('vector', lambda e, a=dq1, b=xt, c=statsD[:, 2 * t:2 * t + 1],
              d=statsD[:, 2 * t + 1:2 * t + 2]: e.tensor_scalar(
              out=a, in0=b, scalar1=c, scalar2=d, op0=ALU.subtract,
              op1=ALU.mult),
              reads=[r_xt, r_statsD], writes=[r_dq1], tag=f'D.nrm{t}')
        S_.op('vector', lambda e, a=dtr, b=dq1, c=modDa[:, 0:H]:
              e.tensor_tensor(out=a, in0=b, in1=c, op=ALU.mult),
              reads=[r_dq1, r_modDa], writes=[r_dtr], tag=f'D.h1{t}')
        S_.op('vector', lambda e, a=dq1, b=dtr, c=modDa[:, H:2 * H]:
              e.tensor_tensor(out=a, in0=b, in1=c, op=ALU.add),
              reads=[r_dtr, r_modDa], writes=[r_dq1], tag=f'D.h2{t}')
        quant(dq1, r_dq1, H, tq, dtr, r_dtr, f'D.q{t}')
        for half in range(2):
            bank, r_bank = ps[5 + half]
            for j4 in range(4):
                j = half * 4 + j4
                S_.op('tensor', lambda e, a=bank[:, j4 * 128:(j4 + 1) * 128],
                      b=dtr[:, j * 128:(j + 1) * 128]: e.transpose(a, b, idf),
                      reads=[r_dtr, r_idf], writes=[r_bank], tag=f'D.tr{t}.{j}')
            dv = h2q3[:, half * 4:half * 4 + 4, t * 128:t * 128 + 128]
            S_.op('scalar', lambda e, a=dv, b=bank[:, 0:512].rearrange(
                'p (a b) -> p a b', b=128): e.copy(out=a, in_=b),
                reads=[r_bank], writes=[r_h2qt[t]], tag=f'D.h2e{t}.{half}')

    wqstg, r_wqstg = sb('wqstg', [128, 512], BF16)

    def gu_subjob(d_w, d_q, r_q, c, s_, nm):
        c0, wc = NSL[c]
        row = c0 + s_ * 128
        S_.dma('sync', lambda e, a=wldst1,
               b=d_w[row:row + 128, :]: e.dma_start(out=a, in_=b),
               writes=[r_wldst1], tag=f'{nm}.l{c}.{s_}')
        quant(wldst1, r_wldst1, H, tq, tq['xn'][0][:, 0:H],
              tq['xn'][1], f'{nm}.q{c}.{s_}')
        dqw, r_dqw = tq['xn']
        for half in range(2):
            bank, r_bank = ps[5 + half]
            for j4 in range(4):
                j = half * 4 + j4
                S_.op('tensor', lambda e, a=bank[:, j4 * 128:(j4 + 1) * 128],
                      b=dqw[:, j * 128:(j + 1) * 128]: e.transpose(a, b, idf),
                      reads=[r_dqw, r_idf], writes=[r_bank],
                      tag=f'{nm}.t{c}.{s_}.{j}')
            S_.op('scalar', lambda e, a=wqstg, b=bank[:, 0:512]:
                  e.copy(out=a, in_=b),
                  reads=[r_bank], writes=[r_wqstg], tag=f'{nm}.e{c}.{s_}.{half}')
            dvo = d_q[half * 4:half * 4 + 4, :, row:row + 128].transpose([1, 0, 2])
            S_.dma('scalar', lambda e, a=dvo,
                   b=wqstg.rearrange('p (a b) -> p a b', b=128):
                   e.dma_start(out=a, in_=b),
                   reads=[r_wqstg], writes=[r_q], tag=f'{nm}.s{c}.{s_}.{half}')

    def d1_head(cp, t):
        ca, cb = 2 * cp, 2 * cp + 1
        wca, wcb = NSL[ca][1], NSL[cb][1]
        wtot = wca + wcb
        par = t % 2
        silb, r_silb = silb2[par]
        hmb, r_hmb = hmb2[par]
        dqh, r_dqh = dqh2[par]
        pg = [ps[0], ps[1]]
        pu = [ps[2], ps[3]]
        for kc in range(KC):
            for ci, wc in ((0, wca), (1, wcb)):
                S_.op('tensor', lambda e, a=pg[ci][0][:, 0:wc],
                      b=h2q[:, kc * S + t * 128:kc * S + t * 128 + 128],
                      c=wgqc2[ci][0][:, kc * 512:kc * 512 + wc],
                      st=(kc == 0), sp=(kc == KC - 1):
                      e.matmul(a, b, c, start=st, stop=sp),
                      reads=[r_h2qt[t], wgqc2[ci][1]], writes=[pg[ci][1]],
                      tag=f'D.gm{ca + ci}.{t}.{kc}')
                S_.op('tensor', lambda e, a=pu[ci][0][:, 0:wc],
                      b=h2q[:, kc * S + t * 128:kc * S + t * 128 + 128],
                      c=wuqc2[ci][0][:, kc * 512:kc * 512 + wc],
                      st=(kc == 0), sp=(kc == KC - 1):
                      e.matmul(a, b, c, start=st, stop=sp),
                      reads=[r_h2qt[t], wuqc2[ci][1]], writes=[pu[ci][1]],
                      tag=f'D.um{ca + ci}.{t}.{kc}')
        for ci, off, wc in ((0, 0, wca), (1, wca, wcb)):
            S_.op('scalar', lambda e, a=silb[:, off:off + wc],
                  b=pg[ci][0][:, 0:wc]: e.activation(out=a, in_=b, func=AF.Silu),
                  reads=[pg[ci][1]], writes=[r_silb], tag=f'D.si{ca + ci}.{t}')
            S_.op('vector', lambda e, a=hmb[:, off:off + wc],
                  b=silb[:, off:off + wc], c=pu[ci][0][:, 0:wc]:
                  e.tensor_tensor(out=a, in0=b, in1=c, op=ALU.mult),
                  reads=[r_silb, pu[ci][1]], writes=[r_hmb],
                  tag=f'D.hm{ca + ci}.{t}')
        quant(hmb[:, 0:wtot], r_hmb, wtot, tqs[par], dqh[:, 0:wtot], r_dqh,
              f'D.qh{cp}.{t}')

    def d1_tail(cp, t):
        ca = 2 * cp
        c0 = NSL[ca][0]
        wtot = NSL[ca][1] + NSL[ca + 1][1]
        subs = wtot // 128
        par = t % 2
        dqh, r_dqh = dqh2[par]
        for half in range(2):
            bank, r_bank = ps[(7, 4)[half]]
            jn = min(4, subs - half * 4)
            if jn <= 0:
                continue
            for j4 in range(jn):
                j = half * 4 + j4
                S_.op('tensor', lambda e, a=bank[:, j4 * 128:(j4 + 1) * 128],
                      b=dqh[:, j * 128:(j + 1) * 128]: e.transpose(a, b, idb),
                      reads=[r_dqh, r_idb], writes=[r_bank],
                      tag=f'D.ht{cp}.{t}.{j}')
            S_.op('scalar', lambda e, a=hqst[:, half * 512:half * 512 + jn * 128],
                  b=bank[:, 0:jn * 128]: e.copy(out=a, in_=b),
                  reads=[r_bank], writes=[r_hqst], tag=f'D.he{cp}.{t}.{half}')
        hoo = d_hmq[t, :, c0:c0 + wtot]
        S_.dma('scalar', lambda e, a=hoo, b=hqst[:, 0:wtot]:
               e.dma_start(out=a, in_=b),
               reads=[r_hqst], writes=[r_hmq[t]], tag=f'D.hs{cp}.{t}')

    for t0 in range(0, NT, 4):
        d1_stats4(t0)
    wldst, r_wldst = sb('wldst', [128, IH], F32)

    def wd_subjob(r, hh):
        dqw, r_dqw = wldst, r_wldst
        S_.dma('sync', lambda e, a=wldst,
               b=d_wd[r * 128:(r + 1) * 128, hh * IH:(hh + 1) * IH]:
               e.dma_start(out=a, in_=b), writes=[r_wldst],
               tag=f'D.wdl{r}.{hh}')
        for hv in range(2):
            sl = wldst[:, hv * 704:(hv + 1) * 704]
            quant(sl, r_wldst, 704, tq, sl, r_wldst, f'D.wdq{r}.{hh}.{hv}')
        for half in range(3):
            bank, r_bank = ps[5 + (half % 2)]
            jn = 4 if half < 2 else 3
            for j4 in range(jn):
                j = half * 4 + j4
                jg = hh * 11 + j
                S_.op('tensor', lambda e,
                      a=bank[:, j4 * 128:(j4 + 1) * 128],
                      b=dqw[:, j * 128:(j + 1) * 128]:
                      e.transpose(a, b, idf),
                      reads=[r_dqw, r_idf], writes=[r_bank],
                      tag=f'D.wdt{r}.{jg}')
            g0 = hh * 11 + half * 4
            dv = wdt3[:, g0:g0 + jn, r * 128:r * 128 + 128]
            S_.op('scalar', lambda e, a=dv, b=bank[:, 0:jn * 128].rearrange(
                'p (a b) -> p a b', b=128): e.copy(out=a, in_=b),
                reads=[r_bank], writes=[r_wdt], tag=f'D.wde{r}.{hh}.{half}')

    d2bufs = {}
    silb2 = [sb(f'silb{i}', [128, 1024], F32) for i in range(2)]
    hmb2 = [sb(f'hmb{i}', [128, 1024], F32) for i in range(2)]
    dqh2 = [sb(f'dqh{i}', [128, 1024], BF16) for i in range(2)]
    hqst, r_hqst = sb('hqst', [128, 1024], BF16)
    wgqc2 = [sb(f'wgqc{i}', [128, KC * 512], BF16) for i in range(2)]
    wuqc2 = [sb(f'wuqc{i}', [128, KC * 512], BF16) for i in range(2)]
    tqs = []
    for i in range(2):
        tqs.append({
            'bm': sb(f'tqs{i}_bm', [128, 64], F32),
            'sc': sb(f'tqs{i}_sc', [128, 64], F32),
            'rsc': sb(f'tqs{i}_rsc', [128, 64], F32),
            'xn': silb2[i],
            'z': tq['z'],
        })

    def d2_tile(t):
        hmqt, (outb, r_outb), (eb2, r_eb2) = (
            d2bufs['hmqt'], d2bufs['outb'], d2bufs['eb2'])
        hq, r_hq = hmqt[t % 2]
        S_.dma('sync', lambda e, a=hq, b=d_hmq[t]: e.dma_start(out=a, in_=b),
               reads=[r_hmq[t]], writes=[r_hq], tag=f'D2.hl{t}')
        xt, r_xt = dxb[t % 2]
        S_.dma('sync', lambda e, a=xt, b=d_x2[t * 128:(t + 1) * 128, :]:
               e.dma_start(out=a, in_=b),
               reads=[r_x2[t]], writes=[r_xt], tag=f'D2.xl{t}')
        for hs in range(2):
            bank, r_bank = ps[0 + hs]
            for kc in range(IC):
                S_.op('tensor', lambda e, a=bank,
                      b=hq[:, kc * 128:(kc + 1) * 128],
                      c=wdt[:, kc * H + hs * 512:kc * H + (hs + 1) * 512],
                      st=(kc == 0), sp=(kc == IC - 1):
                      e.matmul(a, b, c, start=st, stop=sp),
                      reads=[r_hq, r_wdt], writes=[r_bank],
                      tag=f'D2.dm{t}.{hs}.{kc}')
            S_.op('vector', lambda e, a=eb2, b=bank,
                  c=modDg[:, hs * 512:(hs + 1) * 512]:
                  e.tensor_tensor(out=a, in0=b, in1=c, op=ALU.mult),
                  reads=[r_bank, r_modDg], writes=[r_eb2], tag=f'D2.e1{t}.{hs}')
            S_.op('vector', lambda e, a=outb[:, hs * 512:(hs + 1) * 512],
                  b=eb2, c=xt[:, hs * 512:(hs + 1) * 512]:
                  e.tensor_tensor(out=a, in0=b, in1=c, op=ALU.add),
                  reads=[r_eb2, r_xt], writes=[r_outb], tag=f'D2.e2{t}.{hs}')
        S_.dma('scalar', lambda e, a=d_out[t * 128:(t + 1) * 128, :], b=outb:
               e.dma_start(out=a, in_=b),
               reads=[r_outb], writes=[r_out], tag=f'D2.os{t}')

    def load_guq(c):
        c0, wc = NSL[c]
        wgqc, r_wgqcc = wgqc2[c % 2]
        wuqc, r_wuqcc = wuqc2[c % 2]
        for (dq, rq, dst, rdst, nm) in ((d_wgq, r_wgq[c], wgqc, r_wgqcc, 'g'),
                                        (d_wuq, r_wuq[c], wuqc, r_wuqcc, 'u')):
            hi = dq[:, :, c0:c0 + wc].transpose([1, 0, 2])
            S_.dma('sync', lambda e, a=dst.rearrange(
                'p (a b) -> p a b', b=512)[:, :, 0:wc], b=hi:
                e.dma_start(out=a, in_=b),
                reads=[rq], writes=[rdst], tag=f'D.wl{nm}{c}')

    gu_jobs = []
    for c in range(6):
        for s_ in range(NSL[c][1] // 128):
            gu_jobs.append(lambda c=c, s_=s_: gu_subjob(d_wg, d_wgq, r_wgq[c],
                                                        c, s_, 'D.g'))
            gu_jobs.append(lambda c=c, s_=s_: gu_subjob(d_wu, d_wuq, r_wuq[c],
                                                        c, s_, 'D.u'))
    wd_jobs = []
    for r in range(8):
        for hh in range(2):
            wd_jobs.append(lambda r=r, hh=hh: wd_subjob(r, hh))

    LAG = 9
    for t in range(NT):
        u = t - LAG
        if u >= 0:
            d1_head(0, u)
            if u > 0:
                d1_tail(0, u - 1)
        d1_prep(t)
        for _ in range(2):
            if gu_jobs:
                gu_jobs.pop(0)()
        if t == 7:
            load_guq(0)
            load_guq(1)
    for u in range(NT - LAG, NT):
        d1_head(0, u)
        d1_tail(0, u - 1)
    d1_tail(0, NT - 1)

    load_guq(2)
    load_guq(3)
    for t in range(NT):
        d1_head(1, t)
        if t > 0:
            d1_tail(1, t - 1)
        if t % 2 == 0 and wd_jobs:
            wd_jobs.pop(0)()
    d1_tail(1, NT - 1)

    al.free((dq1, r_dq1)); al.free((dtr, r_dtr)); al.free((modDa, r_modDa))
    al.free((wldst, r_wldst)); al.free((wldst1, r_wldst1))
    al.free((wqstg, r_wqstg))
    for k in ('bm', 'sc', 'rsc', 'xn'):
        al.free(tq[k])
    d2bufs['hmqt'] = [sb(f'hmqt{i}', [128, IC * 128], BF16) for i in range(2)]
    d2bufs['outb'] = sb('outb', [128, H], F32)
    d2bufs['eb2'] = sb('eb2', [128, 512], F32)

    load_guq(4)
    load_guq(5)
    for t in range(NT):
        d1_head(2, t)
        if t > 0:
            d1_tail(2, t - 1)
        if t >= 3:
            d2_tile(t - 3)
    d1_tail(2, NT - 1)
    for t in range(NT - 3, NT):
        d2_tile(t)

    if upto == 'D1':
        with nc.Block() as block:
            S_.finalize(nc, block)
        mybir.codegen_inst_isa_subclasses(nc)
        _BUILD_CACHE[key] = nc
        return nc


    with nc.Block() as block:
        S_.finalize(nc, block)
    mybir.codegen_inst_isa_subclasses(nc)
    _BUILD_CACHE[key] = nc
    return nc



def kernel(L0_attn_x, L0_attn_Wq, L0_attn_Wk, L0_attn_Wv, L0_ffn_w_gate,
           L0_ffn_w_up, L0_ffn_w_down, L0_c, L0_wmod):
    nc = build(debug=False)
    x = np.ascontiguousarray(np.asarray(L0_attn_x, dtype=np.float32))
    base = {
        'wq': np.ascontiguousarray(np.asarray(L0_attn_Wq, np.float32)),
        'wk': np.ascontiguousarray(np.asarray(L0_attn_Wk, np.float32)),
        'wv': np.ascontiguousarray(np.asarray(L0_attn_Wv, np.float32)),
        'wg': np.ascontiguousarray(np.asarray(L0_ffn_w_gate, np.float32)),
        'wu': np.ascontiguousarray(np.asarray(L0_ffn_w_up, np.float32)),
        'wd': np.ascontiguousarray(np.asarray(L0_ffn_w_down, np.float32)),
        'wmod': np.ascontiguousarray(np.asarray(L0_wmod, np.float32)),
    }
    carr = np.ascontiguousarray(np.asarray(L0_c, np.float32))
    in_maps = []
    for b in range(NCORES):
        m = dict(base)
        m['x'] = np.ascontiguousarray(x[b])
        m['cvec'] = np.ascontiguousarray(carr[b:b + 1])
        in_maps.append(m)
    res = run_bass_kernel_spmd(nc, in_maps, core_ids=list(range(NCORES)))
    out = np.stack([res.results[b]['out'] for b in range(NCORES)], axis=0)
    return out


# version 16
# speedup vs baseline: 1.0325x; 1.0325x over previous
import sys
import numpy as np

sys.path.insert(0, '/opt/trn_rl_repo')

import concourse.bass as bass
import concourse.mybir as mybir
from concourse.bass_utils import run_bass_kernel_spmd

F32 = mybir.dt.float32
BF16 = mybir.dt.bfloat16
AF = mybir.ActivationFunctionType
ALU = mybir.AluOpType
AXX = mybir.AxisListType.X

S = 4096
H = 1024
I = 2816
NT = S // 128
KC = H // 128
IC = I // 128
NSL = [(0, 512), (512, 512), (1024, 512), (1536, 512), (2048, 512), (2560, 256)]
HMAGIC = float(1.5 * 2 ** 22)
EPS = 1e-5
INV_H = 1.0 / H
SMSCALE = float(1.0 / 32.0)
NCORES = 8

from concourse import dve_ops as _dve_ops
from concourse.dve_op import DveOp as _DveOp
from concourse.dve_spec import (Spec as _Spec, lower as _dve_lower, Src0 as _S0,
                                Src1 as _S1, C0 as _C0, C1 as _C1, C2 as _C2,
                                One as _One, sq as _sq)


def _mk_mg(src):
    t2 = _sq(src)
    g2 = t2 >= _C0
    g4 = t2 >= _C1
    s3 = ((g2 + g4) + g4) + _One
    return s3 * _C2


def _np_step(x):
    t2 = x * x
    return np.where(t2 >= 16.0, 2.0, np.where(t2 >= 4.0, 1.0, 0.5))


def _ref_fp4rnd(in0, in1, s0, s1, imm2):
    st = _np_step(in0)
    mg = imm2 * (st / 0.5)
    return np.round(in0 / st) * st + mg


def _ref_fp4sub(in0, in1, s0, s1, imm2):
    st = _np_step(in1)
    mg = imm2 * (st / 0.5)
    return in0 - mg


def _register_fp4_ops():
    have = {o.name: o for o in _dve_ops.OPS}
    if 'FP4RND' in have:
        return have['FP4RND'], have['FP4SUB']
    specA = _Spec(body=_S0 + _mk_mg(_S0))
    specB = _Spec(body=_S0 - _mk_mg(_S1))
    shaA = _dve_lower(specA, ver='v3', perf_en=False, name='FP4RND').sha('v3')
    shaB = _dve_lower(specB, ver='v3', perf_en=False, name='FP4SUB').sha('v3')
    opA = _DveOp('FP4RND', specA, subdim=False, uops_sha={'v3': shaA},
                 reference=_ref_fp4rnd)
    opB = _DveOp('FP4SUB', specB, subdim=False, uops_sha={'v3': shaB},
                 reference=_ref_fp4sub)
    for op in (opA, opB):
        _dve_ops.OPS.append(op)
        _dve_ops._SUB_OPCODE_FOR_NAME[op.name] = (
            _dve_ops._CUSTOM_DVE_ROW_BASE + len(_dve_ops.OPS) - 1)
        _dve_ops.CUSTOM_DVE_REFS[op.name] = op.reference
        _dve_ops.CUSTOM_DVE_SPECS[op.name] = op.spec
    assert max(_dve_ops._SUB_OPCODE_FOR_NAME.values()) < 0x20
    return opA, opB


OP_FP4RND, OP_FP4SUB = _register_fp4_ops()


ENGINES = ('tensor', 'vector', 'scalar', 'gpsimd', 'sync')
DRAIN_ENGINES = {'vector', 'scalar', 'gpsimd'}


class Res:
    __slots__ = ('name', 'writers', 'readers', 'accessors', 'is_psum', 'interval',
                 'dma_writes', 'sem')

    def __init__(self, name, is_psum=False, interval=None):
        self.name = name
        self.writers = {}
        self.readers = {}
        self.accessors = {}
        self.is_psum = is_psum
        self.interval = interval
        self.dma_writes = 0
        self.sem = None


class Item:
    __slots__ = ('eng', 'idx', 'emit', 'deps', 'kind', 'need_inc', 'tag',
                 'dma_res', 'dma_count')

    def __init__(self, eng, idx, emit, deps, kind, tag):
        self.eng = eng
        self.idx = idx
        self.emit = emit
        self.deps = deps
        self.kind = kind
        self.need_inc = False
        self.tag = tag
        self.dma_res = None
        self.dma_count = 0


class Sched:
    def __init__(self):
        self.items = {e: [] for e in ENGINES}
        self.retired = []

    def res(self, name, is_psum=False, interval=None):
        return Res(name, is_psum, interval)

    @staticmethod
    def _merge(d, eng, idx, kind):
        cur = d.get(eng)
        if cur is None or idx > cur[0]:
            d[eng] = (idx, kind)

    def op(self, eng, emit, reads=(), writes=(), kind='op', tag=''):
        idx = len(self.items[eng])
        deps = []
        for r in reads:
            for e2, (j, k2) in r.writers.items():
                deps.append((e2, j, k2))
            if r.is_psum:
                for e2, (j, k2) in r.accessors.items():
                    if e2 != eng:
                        deps.append((e2, j, k2))
        for w in writes:
            for e2, (j, k2) in w.writers.items():
                deps.append((e2, j, k2))
            for e2, (j, k2) in w.readers.items():
                deps.append((e2, j, k2))
            if w.is_psum:
                for e2, (j, k2) in w.accessors.items():
                    if e2 != eng:
                        deps.append((e2, j, k2))
        it = Item(eng, idx, emit, deps, kind, tag)
        if kind == 'dma':
            assert len(writes) == 1, f'dma must write exactly one res: {tag}'
            w0 = writes[0]
            it.dma_res = w0
            it.dma_count = getattr(w0, 'dma_writes', 0) + 1
            w0.dma_writes = it.dma_count
        self.items[eng].append(it)
        for r in reads:
            self._merge(r.readers, eng, idx, kind)
            if r.is_psum:
                self._merge(r.accessors, eng, idx, kind)
        for w in writes:
            w.writers = {}
            w.readers = {}
            self._merge(w.writers, eng, idx, kind)
            if w.is_psum:
                self._merge(w.accessors, eng, idx, kind)
        return it

    def dma(self, eng, emit, reads=(), writes=(), tag=''):
        return self.op(eng, emit, reads, writes, kind='dma', tag=tag)

    def finalize(self, nc, block):
        waits = {e: [] for e in ENGINES}
        drain_req = {e: [None] * len(self.items[e]) for e in ENGINES}
        for e in ENGINES:
            for it in self.items[e]:
                wl = []
                for (e2, j, k2) in it.deps:
                    if e2 == e and k2 != 'dma' and it.kind != 'dma':
                        if e in DRAIN_ENGINES:
                            cur = drain_req[e][it.idx]
                            drain_req[e][it.idx] = j if cur is None else max(cur, j)
                    elif e2 == e and k2 != 'dma' and it.kind == 'dma':
                        self.items[e2][j].need_inc = True
                        wl.append((e2, j, k2))
                    elif e2 == e and k2 == 'dma':
                        if not (it.kind == 'dma'
                                and self.items[e2][j].dma_res is it.dma_res):
                            wl.append((e2, j, k2))
                    else:
                        self.items[e2][j].need_inc = True
                        wl.append((e2, j, k2))
                waits[e].append(wl)
        prefix = {e: [] for e in ENGINES}
        for e in ENGINES:
            c_op = 0
            for it in self.items[e]:
                if it.need_inc and it.kind != 'dma':
                    c_op += 1
                prefix[e].append(c_op)
        semC = {e: nc.alloc_semaphore(name=f'semC_{e}') for e in ENGINES}
        dma_resources = []
        for e in ENGINES:
            for it in self.items[e]:
                if it.kind == 'dma' and it.dma_res.sem is None:
                    it.dma_res.sem = nc.alloc_semaphore(
                        name=f'dsem{len(dma_resources)}')
                    dma_resources.append(it.dma_res)
        finals = {e: [] for e in ENGINES}
        seen = set()
        for e in ENGINES:
            for it in reversed(self.items[e]):
                if it.kind == 'dma' and id(it.dma_res) not in seen:
                    seen.add(id(it.dma_res))
                    finals[e].append((it.dma_res.sem, 16 * it.dma_res.dma_writes))

        def emit_stream(e, eng):
            watermark = {}
            drained_to = -1
            for it in self.items[e]:
                dr = drain_req[e][it.idx]
                if dr is not None and dr > drained_to:
                    eng.drain()
                    drained_to = it.idx - 1
                for (e2, j, k2) in waits[e][it.idx]:
                    tgt = self.items[e2][j]
                    if k2 == 'dma':
                        sem = tgt.dma_res.sem
                        val = 16 * tgt.dma_count
                        if watermark.get(id(sem), 0) < val:
                            eng.wait_ge(sem, val)
                            watermark[id(sem)] = val
                    else:
                        c_op = prefix[e2][j]
                        if watermark.get((e2, 'o'), 0) < c_op:
                            eng.wait_ge(semC[e2], c_op)
                            watermark[(e2, 'o')] = c_op
                inst = it.emit(eng)
                if it.kind == 'dma':
                    inst.then_inc(it.dma_res.sem, 16)
                elif it.need_inc:
                    inst.then_inc(semC[e], 1)
            for sem, val in finals[e]:
                eng.wait_ge(sem, val)

        for e in ENGINES:
            if not self.items[e]:
                continue
            getattr(block, e)(lambda eng, e=e: emit_stream(e, eng))


class SBAlloc:
    def __init__(self, sched, nc):
        self.s = sched
        self.base = (nc.sbuf_base + 63) // 64 * 64
        self.limit = nc.sbuf_top - self.base
        self.live = []
        self.hwm = 0

    def _find(self, size, align):
        spans = sorted((off, off + sz) for off, sz, _ in self.live)
        pos = 0
        for a, b in spans:
            p = (pos + align - 1) // align * align
            if p + size <= a:
                return p
            pos = max(pos, b)
        p = (pos + align - 1) // align * align
        if p + size > self.limit:
            raise MemoryError(f'SBUF overflow: need {size}B at {p}')
        return p

    def alloc(self, nc, name, shape, dtype, align=64):
        per_part = int(np.prod(shape[1:])) * mybir.dt.size(dtype)
        off = self._find(per_part, align)
        h = nc.alloc_sbuf_tensor_at(name, list(shape), dtype,
                                    offset=self.base + off, align_bytes=align)
        r = self.s.res(name, interval=(off, per_part))
        for (o2, s2, old) in self.s.retired:
            if not (off + per_part <= o2 or o2 + s2 <= off):
                for e2, v in old.writers.items():
                    Sched._merge(r.readers, e2, v[0], v[1])
                for e2, v in old.readers.items():
                    Sched._merge(r.readers, e2, v[0], v[1])
        self.live.append((off, per_part, r))
        self.hwm = max(self.hwm, off + per_part)
        return h.ap(), r

    def free(self, ap_res):
        _, r = ap_res
        self.live = [(o, s, rr) for (o, s, rr) in self.live if rr is not r]
        self.s.retired.append((r.interval[0], r.interval[1], r))



_BUILD_CACHE = {}


def build(debug=False, upto='D2'):
    key = ('k', debug, upto)
    if key in _BUILD_CACHE:
        return _BUILD_CACHE[key]

    nc = bass.Bass('TRN2', target_bir_lowering=False)
    S_ = Sched()
    al = SBAlloc(S_, nc)

    def sb(name, shape, dtype, align=64):
        return al.alloc(nc, name, shape, dtype, align)

    d_x = nc.dram_tensor('x', [S, H], F32, kind='ExternalInput').ap()
    d_wq = nc.dram_tensor('wq', [H, H], F32, kind='ExternalInput').ap()
    d_wk = nc.dram_tensor('wk', [H, H], F32, kind='ExternalInput').ap()
    d_wv = nc.dram_tensor('wv', [H, H], F32, kind='ExternalInput').ap()
    d_wg = nc.dram_tensor('wg', [I, H], F32, kind='ExternalInput').ap()
    d_wu = nc.dram_tensor('wu', [I, H], F32, kind='ExternalInput').ap()
    d_wd = nc.dram_tensor('wd', [H, I], F32, kind='ExternalInput').ap()
    d_c = nc.dram_tensor('cvec', [1, H], F32, kind='ExternalInput').ap()
    d_wm = nc.dram_tensor('wmod', [H, 6 * H], F32, kind='ExternalInput').ap()
    d_out = nc.dram_tensor('out', [S, H], F32, kind='ExternalOutput').ap()
    dbg = 'ExternalOutput' if debug else 'Internal'
    d_hfm = nc.dram_tensor('hfm_s', [KC, 128, S], BF16, kind=dbg).ap()
    d_x2 = nc.dram_tensor('x2_s', [S, H], F32, kind=dbg).ap()
    d_hmq = nc.dram_tensor('hmq_s', [NT, 128, IC * 128], BF16, kind=dbg).ap()
    d_mod = nc.dram_tensor('mod_s', [6, H], F32, kind=dbg).ap()
    d_wgq = nc.dram_tensor('wgq_s', [KC, 128, I], BF16, kind='Internal').ap()
    d_wuq = nc.dram_tensor('wuq_s', [KC, 128, I], BF16, kind='Internal').ap()

    r_hfm = [S_.res(f'd.hfm{t}') for t in range(8)]
    _x2g = [S_.res(f'd.x2g{i}') for i in range(4)]
    r_x2 = [_x2g[t // 8] for t in range(NT)]
    r_hmq = [S_.res(f'd.hmq{t}') for t in range(NT)]
    r_mod_d = S_.res('d.mod')
    _wgq1 = S_.res('d.wgq')
    _wuq1 = S_.res('d.wuq')
    r_wgq = [_wgq1] * 6
    r_wuq = [_wuq1] * 6
    r_out = S_.res('d.out')

    ps = []
    for i in range(8):
        dt_i = BF16 if i in (4, 7) else F32
        h = nc.alloc_psum_tensor(f'pb{i}', [128, 512], dt_i)
        ps.append((h.ap(), S_.res(f'pb{i}', is_psum=True)))

    idf, r_idf = sb('identF', [128, 128], F32)
    idb, r_idb = sb('identB', [128, 128], BF16)
    ca1f, r_ca1f = sb('ca1f', [128, H], BF16)
    saf, r_saf = sb('saf', [128, H], BF16)
    gaf, r_gaf = sb('gaf', [128, H], BF16)
    modBC_parts = [(ca1f, r_ca1f), (saf, r_saf), (gaf, r_gaf)]
    kfm, r_kfm = sb('K_fm', [128, KC * S], BF16)
    vtm, r_vtm = sb('V_tm', [128, NT * H], BF16)
    stats, r_stats = sb('stats', [128, 32], F32)
    onesb, r_onesb = sb('onesb', [1, 128], F32)
    S_.op('vector', lambda e: e.memset(onesb, 1.0), writes=[r_onesb],
          tag='p0.ones')

    def bcast_row(vec_ap, vec_res, dst_ap, dst_res, bank_i, tag):
        bank, r_bank = ps[bank_i]
        for sl in range(2):
            S_.op('tensor', lambda e, a=bank, b=onesb,
                  c=vec_ap[0:1, sl * 512:(sl + 1) * 512]:
                  e.matmul(a, b, c, start=True, stop=True),
                  reads=[r_onesb, vec_res], writes=[r_bank],
                  tag=tag + f'.mm{sl}')
            S_.op('vector', lambda e, a=dst_ap[:, sl * 512:(sl + 1) * 512],
                  b=bank: e.tensor_copy(out=a, in_=b),
                  reads=[r_bank], writes=[dst_res], tag=tag + f'.ev{sl}')

    def quant(x_ap, x_res, n, tmp, deq_out, deq_res, tag, deq_engine='vector',
              xn_engine='vector'):
        nb = n // 16
        (bm, bm_r), (sc, sc_r), (rsc, rsc_r) = tmp['bm'], tmp['sc'], tmp['rsc']
        (xn, xn_r), (z, z_r) = tmp['xn'], tmp['z']
        (y, y_r) = tmp['y'] if 'y' in tmp else (x_ap, x_res)
        x3 = x_ap.rearrange('p (b k) -> p b k', k=16)
        S_.op('vector', lambda e, a=bm[:, 0:nb], b=x3: e.tensor_reduce(
            out=a, in_=b, axis=AXX, op=ALU.max, apply_absolute_value=True),
            reads=[x_res], writes=[bm_r], tag=tag + '.bm')
        S_.op('vector', lambda e, a=sc[:, 0:nb], b=bm[:, 0:nb]: e.tensor_scalar(
            out=a, in0=b, scalar1=float(np.float32(1.0 / 6.0)), scalar2=1e-12,
            op0=ALU.mult, op1=ALU.max),
            reads=[bm_r], writes=[sc_r], tag=tag + '.sc')
        S_.op('vector', lambda e, a=rsc[:, 0:nb], b=sc[:, 0:nb]: e.reciprocal(
            out=a, in_=b), reads=[sc_r], writes=[rsc_r], tag=tag + '.rsc')
        rb = rsc[:, 0:nb].unsqueeze(2).broadcast_to([128, nb, 16])
        S_.op(xn_engine, lambda e, a=xn[:, 0:n].rearrange('p (b k) -> p b k', k=16),
              b=x3, c=rb: e.tensor_tensor(out=a, in0=b, in1=c, op=ALU.mult),
              reads=[x_res, rsc_r], writes=[xn_r], tag=tag + '.xn')
        S_.op('vector', lambda e, a=y[:, 0:n], b=xn[:, 0:n]: e._custom_dve(
            OP_FP4RND, out=a, in0=b, s0=4.0, s1=16.0, imm2=HMAGIC),
            reads=[xn_r], writes=[y_r], tag=tag + '.opA')
        S_.op('vector', lambda e, a=z[:, 0:n], b=y[:, 0:n], c=xn[:, 0:n]:
              e._custom_dve(OP_FP4SUB, out=a, in0=b, in1=c, s0=4.0, s1=16.0,
                            imm2=HMAGIC),
              reads=[y_r, xn_r], writes=[z_r], tag=tag + '.opB')
        sbb = sc[:, 0:nb].unsqueeze(2).broadcast_to([128, nb, 16])
        S_.op(deq_engine, lambda e,
              a=deq_out.rearrange('p (b k) -> p b k', k=16),
              b=z[:, 0:n].rearrange('p (b k) -> p b k', k=16), c=sbb:
              e.tensor_tensor(out=a, in0=b, in1=c, op=ALU.mult),
              reads=[z_r, sc_r], writes=[deq_res], tag=tag + '.deq')

    def ln(x_ap, x_res, trash, trash_r, y_ap, y_res, tag):
        st = stats
        S_.op('vector', lambda e, a=st[:, 0:1], b=x_ap: e.reduce_sum(
            out=a, in_=b, axis=AXX), reads=[x_res], writes=[r_stats],
            tag=tag + '.sum')
        S_.op('scalar', lambda e, a=trash, b=x_ap, c=st[:, 1:2]: e.activation(
            out=a, in_=b, func=AF.Square, accum_out=c),
            reads=[x_res], writes=[trash_r, r_stats], tag=tag + '.sq')
        S_.op('vector', lambda e, a=st[:, 2:3], b=st[:, 0:1]: e.tensor_scalar(
            out=a, in0=b, scalar1=INV_H, scalar2=None, op0=ALU.mult),
            reads=[r_stats], writes=[r_stats], tag=tag + '.m')
        S_.op('vector', lambda e, a=st[:, 3:4], b=st[:, 1:2]: e.tensor_scalar(
            out=a, in0=b, scalar1=INV_H, scalar2=None, op0=ALU.mult),
            reads=[r_stats], writes=[r_stats], tag=tag + '.ex2')
        S_.op('vector', lambda e, a=st[:, 4:5], b=st[:, 2:3]: e.tensor_tensor(
            out=a, in0=b, in1=b, op=ALU.mult),
            reads=[r_stats], writes=[r_stats], tag=tag + '.m2')
        S_.op('vector', lambda e, a=st[:, 5:6], b=st[:, 3:4], c=st[:, 4:5]:
              e.tensor_tensor(out=a, in0=b, in1=c, op=ALU.subtract),
              reads=[r_stats], writes=[r_stats], tag=tag + '.var')
        S_.op('vector', lambda e, a=st[:, 6:7], b=st[:, 5:6]: e.tensor_scalar(
            out=a, in0=b, scalar1=EPS, scalar2=None, op0=ALU.add),
            reads=[r_stats], writes=[r_stats], tag=tag + '.ve')
        S_.op('vector', lambda e, a=st[:, 7:8], b=st[:, 6:7]: e.reciprocal(
            out=a, in_=b), reads=[r_stats], writes=[r_stats], tag=tag + '.r')
        S_.op('scalar', lambda e, a=st[:, 8:9], b=st[:, 7:8]: e.activation(
            out=a, in_=b, func=AF.Sqrt),
            reads=[r_stats], writes=[r_stats], tag=tag + '.s0')
        S_.op('vector', lambda e, a=st[:, 9:10], b=st[:, 8:9]: e.tensor_tensor(
            out=a, in0=b, in1=b, op=ALU.mult),
            reads=[r_stats], writes=[r_stats], tag=tag + '.nsq')
        S_.op('vector', lambda e, a=st[:, 10:11], b=st[:, 9:10], c=st[:, 6:7]:
              e.tensor_tensor(out=a, in0=b, in1=c, op=ALU.mult),
              reads=[r_stats], writes=[r_stats], tag=tag + '.nt2')
        S_.op('vector', lambda e, a=st[:, 10:11], b=st[:, 10:11]: e.tensor_scalar(
            out=a, in0=b, scalar1=-0.5, scalar2=1.5, op0=ALU.mult, op1=ALU.add),
            reads=[r_stats], writes=[r_stats], tag=tag + '.nt3')
        S_.op('vector', lambda e, a=st[:, 11:12], b=st[:, 8:9], c=st[:, 10:11]:
              e.tensor_tensor(out=a, in0=b, in1=c, op=ALU.mult),
              reads=[r_stats], writes=[r_stats], tag=tag + '.rstd')
        S_.op('vector', lambda e, a=y_ap, b=x_ap, c=st[:, 2:3], d=st[:, 11:12]:
              e.tensor_scalar(out=a, in0=b, scalar1=c, scalar2=d,
                              op0=ALU.subtract, op1=ALU.mult),
              reads=[x_res, r_stats], writes=[y_res], tag=tag + '.norm')

    def prep_wT(d_w, dst, r_dst, stg, nm, banks=(1, 2)):
        dst3 = dst.rearrange('p (a b) -> p a b', b=H)
        for r in range(KC):
            wst, r_wst = stg[r % 2]
            S_.dma('sync', lambda e, a=wst, b=d_w[r * 128:(r + 1) * 128, :]:
                   e.dma_start(out=a, in_=b), writes=[r_wst], tag=f'{nm}.ld{r}')
            for half in range(2):
                bank, r_bank = ps[banks[half]]
                for j4 in range(4):
                    j = half * 4 + j4
                    S_.op('tensor', lambda e, a=bank[:, j4 * 128:(j4 + 1) * 128],
                          b=wst[:, j * 128:(j + 1) * 128]: e.transpose(a, b, idf),
                          reads=[r_wst, r_idf], writes=[r_bank],
                          tag=f'{nm}.tr{r}.{j}')
                dv = dst3[:, half * 4:half * 4 + 4, r * 128:r * 128 + 128]
                S_.op('scalar', lambda e, a=dv, b=bank[:, 0:512].rearrange(
                    'p (a b) -> p a b', b=128): e.copy(out=a, in_=b),
                    reads=[r_bank], writes=[r_dst], tag=f'{nm}.ev{r}.{half}')

    S_.op('gpsimd', lambda e: e.memset(idf, 0.0), writes=[r_idf], tag='p0.i0')
    S_.op('gpsimd', lambda e: e.affine_select(
        out=idf, in_=idf, compare_op=ALU.not_equal, fill=1.0, base=0,
        pattern=[[-1, 128]], channel_multiplier=1),
        reads=[r_idf], writes=[r_idf], tag='p0.i1')
    S_.op('gpsimd', lambda e: e.memset(idb, 0.0), writes=[r_idb], tag='p0.i2')
    S_.op('gpsimd', lambda e: e.affine_select(
        out=idb, in_=idb, compare_op=ALU.not_equal, fill=1.0, base=0,
        pattern=[[-1, 128]], channel_multiplier=1),
        reads=[r_idb], writes=[r_idb], tag='p0.i3')

    modstage, r_modstage = sb('modstage', [1, 6 * H], F32)
    csb, r_csb = sb('csb', [128, 8], F32)
    csbf, r_csbf = sb('csbf', [128, 8], BF16)
    wmst = [sb(f'wmst{i}', [128, 512], F32) for i in range(2)]
    wmbf = [sb(f'wmbf{i}', [128, 512], BF16) for i in range(2)]
    t1k = [sb(f't1k{i}', [1, H], F32) for i in range(3)]

    for kc in range(KC):
        S_.dma('sync', lambda e, a=csb[:, kc:kc + 1],
               b=d_c[0:1, kc * 128:(kc + 1) * 128].transpose([1, 0]):
               e.dma_start(out=a, in_=b), writes=[r_csb], tag=f'p0.c{kc}')
    S_.op('scalar', lambda e: e.activation(out=csbf, in_=csb, func=AF.Silu),
          reads=[r_csb], writes=[r_csbf], tag='p0.sc')
    pm, r_pm = ps[0]
    for sl in range(12):
        for kc in range(KC):
            i = (sl * KC + kc) % 2
            wst, r_wst = wmst[i]
            wbf, r_wbf = wmbf[i]
            S_.dma('sync', lambda e, a=wst,
                   b=d_wm[kc * 128:(kc + 1) * 128, sl * 512:(sl + 1) * 512]:
                   e.dma_start(out=a, in_=b), writes=[r_wst],
                   tag=f'p0.wl{sl}.{kc}')
            S_.op('scalar', lambda e, a=wbf, b=wst: e.copy(out=a, in_=b),
                  reads=[r_wst], writes=[r_wbf], tag=f'p0.wc{sl}.{kc}')
            S_.op('tensor', lambda e, a=pm[0:1, :], b=csbf[:, kc:kc + 1], c=wbf,
                  st=(kc == 0), sp=(kc == KC - 1):
                  e.matmul(a, b, c, start=st, stop=sp),
                  reads=[r_csbf, r_wbf], writes=[r_pm], tag=f'p0.wm{sl}.{kc}')
        S_.op('vector', lambda e, a=modstage[0:1, sl * 512:(sl + 1) * 512],
              b=pm[0:1, :]: e.tensor_copy(out=a, in_=b),
              reads=[r_pm], writes=[r_modstage], tag=f'p0.we{sl}')

    tv, r_tv = t1k[0]
    ta, r_ta = t1k[1]
    tb, r_tb = t1k[2]

    def derive(ji, jo, kind, bc_col):
        src = modstage[0:1, ji * H:(ji + 1) * H]
        if kind == 'copy':
            S_.op('vector', lambda e, a=tv, b=src: e.tensor_copy(out=a, in_=b),
                  reads=[r_modstage], writes=[r_tv], tag=f'p0.d{jo}a')
        elif kind == 'plus1':
            S_.op('vector', lambda e, a=tv, b=src: e.tensor_scalar(
                out=a, in0=b, scalar1=1.0, scalar2=None, op0=ALU.add),
                reads=[r_modstage], writes=[r_tv], tag=f'p0.d{jo}a')
        else:
            S_.op('vector', lambda e, a=ta, b=src: e.tensor_scalar(
                out=a, in0=b, scalar1=0.0, scalar2=None, op0=ALU.is_ge),
                reads=[r_modstage], writes=[r_ta], tag=f'p0.d{jo}b')
            S_.op('vector', lambda e, a=ta: e.tensor_scalar(
                out=a, in0=a, scalar1=2.0, scalar2=-1.0, op0=ALU.mult,
                op1=ALU.add), reads=[r_ta], writes=[r_ta], tag=f'p0.d{jo}c')
            S_.op('vector', lambda e, a=tb, b=src: e.tensor_scalar(
                out=a, in0=b, scalar1=-1.0, scalar2=None, op0=ALU.mult),
                reads=[r_modstage], writes=[r_tb], tag=f'p0.d{jo}d')
            S_.op('vector', lambda e, a=tb, b=src: e.tensor_tensor(
                out=a, in0=b, in1=a, op=ALU.max),
                reads=[r_modstage, r_tb], writes=[r_tb], tag=f'p0.d{jo}e')
            S_.op('vector', lambda e, a=tb: e.tensor_scalar(
                out=a, in0=a, scalar1=0.25, scalar2=None, op0=ALU.max),
                reads=[r_tb], writes=[r_tb], tag=f'p0.d{jo}f')
            S_.op('vector', lambda e, a=tv, b=ta, c=tb: e.tensor_tensor(
                out=a, in0=b, in1=c, op=ALU.mult),
                reads=[r_ta, r_tb], writes=[r_tv], tag=f'p0.d{jo}g')
        S_.dma('scalar', lambda e, a=d_mod[jo:jo + 1, :], b=tv:
               e.dma_start(out=a, in_=b), reads=[r_tv], writes=[r_mod_d],
               tag=f'p0.d{jo}s')
        if bc_col is not None:
            dap, dres = modBC_parts[bc_col]
            bcast_row(tv, r_tv, dap, dres, 1, f'p0.d{jo}bc')

    derive(1, 0, 'plus1', 0)
    derive(0, 1, 'copy', 1)
    derive(2, 2, 'gate', 2)
    derive(4, 3, 'plus1', None)
    derive(3, 4, 'copy', None)
    derive(5, 5, 'gate', None)

    al.free((modstage, r_modstage)); al.free((csb, r_csb))
    al.free((csbf, r_csbf))
    al.free(wmst[0]); al.free(wmst[1]); al.free(wmbf[0]); al.free(wmbf[1])
    al.free(t1k[0]); al.free(t1k[1]); al.free(t1k[2])

    wkt, r_wkt = sb('WkT', [128, KC * H], BF16)
    wvt, r_wvt = sb('WvT', [128, KC * H], BF16)
    wstg = [sb(f'wstg{i}', [128, H], F32) for i in range(2)]
    prep_wT(d_wk, wkt, r_wkt, wstg, 'p0.wk')
    prep_wT(d_wv, wvt, r_wvt, wstg, 'p0.wv')
    al.free(wstg[0]); al.free(wstg[1])

    if upto == 'P0':
        with nc.Block() as block:
            S_.finalize(nc, block)
        mybir.codegen_inst_isa_subclasses(nc)
        _BUILD_CACHE[key] = nc
        return nc

    xb = [sb(f'xb{i}', [128, H], F32) for i in range(4)]
    hb, r_hb = sb('hb', [128, H], F32)
    yb, r_yb = sb('yb', [128, H], F32)
    sqtr, r_sqtr = sb('sqtr', [128, H], F32)
    statsB, r_statsB = sb('statsB', [128, 64], F32)
    hfmg, r_hfmg = sb('hfmg', [128, KC * 512], BF16)
    hfmg3 = hfmg.rearrange('p (a b) -> p a b', b=512)

    def b_stats4(t0):
        st = stats
        for j in range(4):
            t = t0 + j
            xt, r_xt = xb[t % 4]
            S_.dma('sync', lambda e, a=xt, b=d_x[t * 128:(t + 1) * 128, :]:
                   e.dma_start(out=a, in_=b), writes=[r_xt], tag=f'B.x{t}')
            S_.op('vector', lambda e, a=st[:, j:j + 1], b=xt: e.reduce_sum(
                out=a, in_=b, axis=AXX), reads=[r_xt], writes=[r_stats],
                tag=f'B.ss{t}')
            S_.op('scalar', lambda e, a=sqtr, b=xt, c=st[:, 4 + j:5 + j]:
                  e.activation(out=a, in_=b, func=AF.Square, accum_out=c),
                  reads=[r_xt], writes=[r_sqtr, r_stats], tag=f'B.sq{t}')
        sB = statsB[:, 2 * (t0 % 16):2 * (t0 % 16) + 8].rearrange(
            'p (a b) -> p a b', b=2)
        S_.op('vector', lambda e, a=sB[:, :, 0:1].squeeze(2), b=st[:, 0:4]:
              e.tensor_scalar(out=a, in0=b, scalar1=INV_H, scalar2=None,
                              op0=ALU.mult),
              reads=[r_stats], writes=[r_statsB], tag=f'B.sm{t0}')
        S_.op('vector', lambda e, a=st[:, 8:12], b=st[:, 4:8]: e.tensor_scalar(
            out=a, in0=b, scalar1=INV_H, scalar2=None, op0=ALU.mult),
            reads=[r_stats], writes=[r_stats], tag=f'B.se{t0}')
        S_.op('vector', lambda e, a=st[:, 12:16], b=sB[:, :, 0:1].squeeze(2):
              e.tensor_tensor(out=a, in0=b, in1=b, op=ALU.mult),
              reads=[r_statsB], writes=[r_stats], tag=f'B.sm2{t0}')
        S_.op('vector', lambda e, a=st[:, 16:20], b=st[:, 8:12], c=st[:, 12:16]:
              e.tensor_tensor(out=a, in0=b, in1=c, op=ALU.subtract),
              reads=[r_stats], writes=[r_stats], tag=f'B.sv{t0}')
        S_.op('vector', lambda e, a=st[:, 16:20], b=st[:, 16:20]: e.tensor_scalar(
            out=a, in0=b, scalar1=EPS, scalar2=None, op0=ALU.add),
            reads=[r_stats], writes=[r_stats], tag=f'B.sve{t0}')
        S_.op('vector', lambda e, a=st[:, 20:24], b=st[:, 16:20]: e.reciprocal(
            out=a, in_=b), reads=[r_stats], writes=[r_stats], tag=f'B.sr{t0}')
        S_.op('scalar', lambda e, a=st[:, 24:28], b=st[:, 20:24]: e.activation(
            out=a, in_=b, func=AF.Sqrt),
            reads=[r_stats], writes=[r_stats], tag=f'B.ssq{t0}')
        S_.op('vector', lambda e, a=st[:, 8:12], b=st[:, 24:28]: e.tensor_tensor(
            out=a, in0=b, in1=b, op=ALU.mult),
            reads=[r_stats], writes=[r_stats], tag=f'B.sn1{t0}')
        S_.op('vector', lambda e, a=st[:, 12:16], b=st[:, 8:12], c=st[:, 16:20]:
              e.tensor_tensor(out=a, in0=b, in1=c, op=ALU.mult),
              reads=[r_stats], writes=[r_stats], tag=f'B.sn2{t0}')
        S_.op('vector', lambda e, a=st[:, 12:16], b=st[:, 12:16]: e.tensor_scalar(
            out=a, in0=b, scalar1=-0.5, scalar2=1.5, op0=ALU.mult, op1=ALU.add),
            reads=[r_stats], writes=[r_stats], tag=f'B.sn3{t0}')
        S_.op('vector', lambda e, a=sB[:, :, 1:2].squeeze(2), b=st[:, 24:28],
              c=st[:, 12:16]: e.tensor_tensor(out=a, in0=b, in1=c, op=ALU.mult),
              reads=[r_stats], writes=[r_statsB], tag=f'B.srstd{t0}')

    def b_head(t):
        xt, r_xt = xb[t % 4]
        col = 2 * (t % 16)
        S_.op('vector', lambda e, a=yb, b=xt, c=statsB[:, col:col + 1],
              d=statsB[:, col + 1:col + 2]: e.tensor_scalar(
              out=a, in0=b, scalar1=c, scalar2=d, op0=ALU.subtract,
              op1=ALU.mult),
              reads=[r_xt, r_statsB], writes=[r_yb], tag=f'B.nrm{t}')
        S_.op('vector', lambda e, a=hb, b=yb, c=ca1f:
              e.tensor_tensor(out=a, in0=b, in1=c, op=ALU.mult),
              reads=[r_yb, r_ca1f], writes=[r_hb], tag=f'B.h1{t}')
        S_.op('vector', lambda e, a=xt, b=hb, c=saf:
              e.tensor_tensor(out=a, in0=b, in1=c, op=ALU.add),
              reads=[r_hb, r_saf], writes=[r_xt], tag=f'B.h2{t}')

    def b_tail(t):
        tl = t % 4
        xt, r_xt = xb[t % 4]
        for half in range(2):
            bank, r_bank = ps[2 + half]
            for j4 in range(4):
                j = half * 4 + j4
                S_.op('tensor', lambda e, a=bank[:, j4 * 128:(j4 + 1) * 128],
                      b=xt[:, j * 128:(j + 1) * 128]: e.transpose(a, b, idf),
                      reads=[r_xt, r_idf], writes=[r_bank],
                      tag=f'B.tr{t}.{j}')
            dv = hfmg3[:, half * 4:half * 4 + 4, tl * 128:tl * 128 + 128]
            S_.op('scalar', lambda e, a=dv, b=bank[:, 0:512].rearrange(
                'p (a b) -> p a b', b=128): e.copy(out=a, in_=b),
                reads=[r_bank], writes=[r_hfmg], tag=f'B.he{t}.{half}')

    for g in range(8):
        b_stats4(g * 4)
        for tl in range(4):
            t = g * 4 + tl
            b_head(t)
            if tl > 0:
                b_tail(t - 1)
        b_tail(g * 4 + 3)
        for oc in range(KC):
            bank, r_bank = ps[5 + (oc % 2)]
            for kc in range(KC):
                S_.op('tensor', lambda e, a=bank,
                      b=wkt[:, kc * H + oc * 128:kc * H + oc * 128 + 128],
                      c=hfmg[:, kc * 512:(kc + 1) * 512],
                      st=(kc == 0), sp=(kc == KC - 1):
                      e.matmul(a, b, c, start=st, stop=sp),
                      reads=[r_wkt, r_hfmg], writes=[r_bank],
                      tag=f'B.km{g}.{oc}.{kc}')
            S_.op('scalar', lambda e,
                  a=kfm[:, oc * S + g * 512:oc * S + (g + 1) * 512], b=bank:
                  e.copy(out=a, in_=b),
                  reads=[r_bank], writes=[r_kfm], tag=f'B.ke{g}.{oc}')
        for tl in range(4):
            t = g * 4 + tl
            for hs in range(2):
                bank, r_bank = ps[5 + ((tl * 2 + hs) % 2)]
                for kc in range(KC):
                    S_.op('tensor', lambda e, a=bank,
                          b=hfmg[:, kc * 512 + tl * 128:kc * 512 + tl * 128 + 128],
                          c=wvt[:, kc * H + hs * 512:kc * H + (hs + 1) * 512],
                          st=(kc == 0), sp=(kc == KC - 1):
                          e.matmul(a, b, c, start=st, stop=sp),
                          reads=[r_wvt, r_hfmg], writes=[r_bank],
                          tag=f'B.vm{t}.{hs}.{kc}')
                S_.op('scalar', lambda e,
                      a=vtm[:, t * H + hs * 512:t * H + (hs + 1) * 512], b=bank:
                      e.copy(out=a, in_=b),
                      reads=[r_bank], writes=[r_vtm], tag=f'B.ve{t}.{hs}')
        ho = d_hfm[:, :, g * 512:(g + 1) * 512].transpose([1, 0, 2])
        S_.dma('scalar', lambda e, a=ho, b=hfmg3: e.dma_start(out=a, in_=b),
               reads=[r_hfmg], writes=[r_hfm[g]], tag=f'B.hs{g}')

    al.free((wkt, r_wkt)); al.free((wvt, r_wvt)); al.free((hfmg, r_hfmg))
    for i in range(4):
        al.free(xb[i])
    al.free((hb, r_hb)); al.free((yb, r_yb)); al.free((sqtr, r_sqtr))
    al.free((statsB, r_statsB))
    al.free((ca1f, r_ca1f)); al.free((saf, r_saf))

    if upto == 'B':
        with nc.Block() as block:
            S_.finalize(nc, block)
        mybir.codegen_inst_isa_subclasses(nc)
        _BUILD_CACHE[key] = nc
        return nc

    wqt, r_wqt = sb('WqT', [128, KC * H], BF16)
    wstg2 = [sb(f'wstg2_{i}', [128, H], F32) for i in range(2)]
    prep_wT(d_wq, wqt, r_wqt, wstg2, 'C.wq', banks=(0, 1))
    al.free(wstg2[0]); al.free(wstg2[1])
    msk, r_msk = sb('masks', [128, 4 * 512], BF16)
    for r in range(4):
        slm = msk[:, r * 512:(r + 1) * 512]
        S_.op('gpsimd', lambda e, a=slm: e.memset(a, 1.0), writes=[r_msk],
              tag=f'p0.m0{r}')
        S_.op('gpsimd', lambda e, a=slm, base=128 * r: e.affine_select(
            out=a, in_=a, compare_op=ALU.is_ge, fill=0.0, base=base,
            pattern=[[-1, 512]], channel_multiplier=1),
            reads=[r_msk], writes=[r_msk], tag=f'p0.m1{r}')
    hfmc, r_hfmc = sb('hfmc', [128, KC * 512], BF16)
    qg, r_qg = sb('qg', [128, KC * 512], BF16)
    pbuf2 = [sb(f'pbuf{i}', [128, S - 512], BF16) for i in range(2)]
    dscr, r_dscr = sb('dscr', [128, 512], BF16)
    pmb2 = [sb(f'pmb{i}', [128, 512], BF16) for i in range(2)]
    ptb = [sb(f'ptb{i}', [128, 512], BF16) for i in range(2)]
    cxb = [sb(f'cxb{i}', [128, H], F32) for i in range(2)]
    x2b2 = [sb('x2b0', [128, H], F32)] * 2
    ebuf, r_ebuf = sb('ebuf', [128, 512], F32)
    ebufB, r_ebufB = sb('ebufB', [128, 512], F32)
    obuf, r_obuf = sb('obuf', [128, H], BF16)
    stc2 = [sb(f'stc{i}', [128, 32], F32) for i in range(2)]

    def c_rounds(qc):
        full = list(range(qc))
        if len(full) > 4:
            return [full[:4], full[4:] + [qc]]
        if full:
            return [full, [qc]]
        return [[qc]]

    def c_part1(qi):
        par = qi % 2
        st, r_st = stc2[par]
        pbuf, r_pbuf = pbuf2[par]
        pmb, r_pmb = pmb2[par]
        xt, r_xt = cxb[par]
        qc = qi // 4
        rr = qi % 4
        S_.dma('sync', lambda e, a=xt, b=d_x[qi * 128:(qi + 1) * 128, :]:
               e.dma_start(out=a, in_=b), writes=[r_xt], tag=f'C.x{qi}')
        rounds = c_rounds(qc)
        nrounds = len(rounds)
        for ri, cs in enumerate(rounds):
            for k_, ci in enumerate(cs):
                bank, r_bank = ps[k_]
                for hc in range(KC):
                    S_.op('tensor', lambda e, a=bank,
                          b=qg[:, hc * 512 + rr * 128:hc * 512 + rr * 128 + 128],
                          c=kfm[:, hc * S + ci * 512:hc * S + (ci + 1) * 512],
                          st=(hc == 0), sp=(hc == KC - 1):
                          e.matmul(a, b, c, start=st, stop=sp),
                          reads=[r_qg, r_kfm], writes=[r_bank],
                          tag=f'C.sm{qi}.{ci}.{hc}')
            for k_, ci in enumerate(cs):
                bank, r_bank = ps[k_]
                S_.op('vector', lambda e, a=st[:, 16 + ci:17 + ci], b=bank:
                      e.reduce_max(out=a, in_=b, axis=AXX),
                      reads=[r_bank], writes=[r_st], tag=f'C.mx{qi}.{ci}')
            c0, c1 = cs[0], cs[-1]
            S_.op('vector', lambda e, a=st[:, 12 + ri:13 + ri],
                  b=st[:, 16 + c0:17 + c1]: e.reduce_max(
                  out=a, in_=b, axis=AXX),
                  reads=[r_st], writes=[r_st], tag=f'C.rm{qi}.{ri}')
            S_.op('vector', lambda e, a=st[:, 14 + ri:15 + ri],
                  b=st[:, 12 + ri:13 + ri]: e.tensor_scalar(
                  out=a, in0=b, scalar1=-SMSCALE, scalar2=None, op0=ALU.mult),
                  reads=[r_st], writes=[r_st], tag=f'C.nb{qi}.{ri}')
            for k_, ci in enumerate(cs):
                bank, r_bank = ps[k_]
                dst = dscr if ci == qc else pbuf[:, ci * 512:(ci + 1) * 512]
                r_dst = r_dscr if ci == qc else r_pbuf
                S_.op('scalar', lambda e,
                      a=dst, b=bank,
                      c=st[:, 14 + ri:15 + ri], d=st[:, 24 + ci:25 + ci]:
                      e.activation(out=a, in_=b, func=AF.Exp, scale=SMSCALE,
                                   bias=c, accum_out=d),
                      reads=[r_bank, r_st], writes=[r_dst, r_st],
                      tag=f'C.ex{qi}.{ci}')
        if nrounds == 2:
            S_.op('vector', lambda e, a=st[:, 11:12], b=st[:, 12:13],
                  c=st[:, 13:14]: e.tensor_tensor(
                  out=a, in0=b, in1=c, op=ALU.max),
                  reads=[r_st], writes=[r_st], tag=f'C.gm{qi}')
            S_.op('vector', lambda e, a=st[:, 18:20], b=st[:, 12:14],
                  c=st[:, 11:12]: e.tensor_scalar(
                  out=a, in0=b, scalar1=c, scalar2=SMSCALE,
                  op0=ALU.subtract, op1=ALU.mult),
                  reads=[r_st], writes=[r_st], tag=f'C.fd{qi}')
            S_.op('scalar', lambda e, a=st[:, 20:22], b=st[:, 18:20]:
                  e.activation(out=a, in_=b, func=AF.Exp),
                  reads=[r_st], writes=[r_st], tag=f'C.fe{qi}')
        S_.op('vector', lambda e, a=pmb, b=dscr,
              c=msk[:, (qi % 4) * 512:(qi % 4 + 1) * 512]:
              e.tensor_tensor(out=a, in0=b, in1=c, op=ALU.mult),
              reads=[r_dscr, r_msk], writes=[r_pmb], tag=f'C.pm{qi}')
        S_.op('vector', lambda e, a=st[:, 24 + qc:25 + qc], b=pmb:
              e.reduce_sum(out=a, in_=b, axis=AXX),
              reads=[r_pmb], writes=[r_st], tag=f'C.ls{qi}')
        if nrounds == 2:
            na = len(rounds[0])
            S_.op('vector', lambda e, a=st[:, 22:23],
                  b=st[:, 24:24 + na]: e.reduce_sum(out=a, in_=b, axis=AXX),
                  reads=[r_st], writes=[r_st], tag=f'C.la{qi}')
            S_.op('vector', lambda e, a=st[:, 23:24],
                  b=st[:, 24 + na:25 + qc]: e.reduce_sum(
                  out=a, in_=b, axis=AXX),
                  reads=[r_st], writes=[r_st], tag=f'C.lb{qi}')
            S_.op('vector', lambda e, a=st[:, 22:24], b=st[:, 22:24],
                  c=st[:, 20:22]: e.tensor_tensor(
                  out=a, in0=b, in1=c, op=ALU.mult),
                  reads=[r_st], writes=[r_st], tag=f'C.lw{qi}')
            S_.op('vector', lambda e, a=st[:, 15:16], b=st[:, 22:23],
                  c=st[:, 23:24]: e.tensor_tensor(
                  out=a, in0=b, in1=c, op=ALU.add),
                  reads=[r_st], writes=[r_st], tag=f'C.l{qi}')
        else:
            S_.op('vector', lambda e, a=st[:, 15:16], b=st[:, 17:18],
                  c=st[:, 24 + qc:25 + qc]: e.tensor_copy(out=a, in_=c),
                  reads=[r_st], writes=[r_st], tag=f'C.l{qi}')
        S_.op('vector', lambda e, a=st[:, 10:11], b=st[:, 15:16]:
              e.reciprocal(out=a, in_=b),
              reads=[r_st], writes=[r_st], tag=f'C.rl{qi}')

    def c_part2(qi):
        par = qi % 2
        st, r_st = stc2[par]
        pbuf, r_pbuf = pbuf2[par]
        pmb, r_pmb = pmb2[par]
        xt, r_xt = cxb[par]
        x2b, r_x2b = x2b2[par]
        qc = qi // 4
        rounds = c_rounds(qc)
        nrounds = len(rounds)
        trbanks = [ps[7], ps[4]]
        ob = [ps[5], ps[6]]
        rbound = []
        if nrounds == 2:
            na = len(rounds[0])
            rbound.append((0, 4 * na - 1, 0))
            rbound.append((4 * na, qi, 1))
        else:
            rbound.append((0, qi, 0))
        batches = []
        for (j0, j1, ri) in rbound:
            for jq in range(j0, j1 + 1, 4):
                batches.append((jq, min(4, j1 + 1 - jq), j0, j1, ri))

        def emit_tr(bi):
            jq, jn, j0, j1, ri = batches[bi]
            half = (jq // 4) % 2
            trbank_full, r_trb = trbanks[half]
            trbank = trbank_full[:, 0:jn * 128]
            pt, r_pt = ptb[half]
            for j4 in range(jn):
                j = jq + j4
                ci = j // 4
                if ci == qc:
                    src = pmb[:, (j % 4) * 128:(j % 4 + 1) * 128]
                    r_src = r_pmb
                else:
                    src = pbuf[:, j * 128:(j + 1) * 128]
                    r_src = r_pbuf
                S_.op('tensor', lambda e,
                      a=trbank_full[:, j4 * 128:(j4 + 1) * 128],
                      b=src, c=idb: e.transpose(a, b, c),
                      reads=[r_src, r_idb], writes=[r_trb],
                      tag=f'C.pt{qi}.{j}')
            ev_eng = 'scalar' if half == 0 else 'vector'
            ev = ((lambda e, a=pt[:, 0:jn * 128], b=trbank: e.copy(out=a, in_=b))
                  if ev_eng == 'scalar' else
                  (lambda e, a=pt[:, 0:jn * 128], b=trbank: e.tensor_copy(out=a, in_=b)))
            S_.op(ev_eng, ev,
                  reads=[r_trb], writes=[r_pt],
                  tag=f'C.pe{qi}.{jq}')

        def emit_av(bi):
            jq, jn, j0, j1, ri = batches[bi]
            half = (jq // 4) % 2
            pt, r_pt = ptb[half]
            for j4 in range(jn):
                j = jq + j4
                for hs in range(2):
                    bank, r_bank = ob[hs]
                    S_.op('tensor', lambda e, a=bank,
                          b=pt[:, j4 * 128:(j4 + 1) * 128],
                          c=vtm[:, j * H + hs * 512:j * H + (hs + 1) * 512],
                          st=(j == j0), sp=(j == j1):
                          e.matmul(a, b, c, start=st, stop=sp),
                          reads=[r_pt, r_vtm], writes=[r_bank],
                          tag=f'C.av{qi}.{j}.{hs}')
            if nrounds == 2 and ri == 0 and jq + jn == j1 + 1:
                for hs in range(2):
                    bank, r_bank = ob[hs]
                    S_.op('scalar', lambda e,
                          a=obuf[:, hs * 512:(hs + 1) * 512], b=bank,
                          c=st[:, 20:21]: e.activation(
                          out=a, in_=b, func=AF.Copy, scale=c),
                          reads=[r_bank, r_st], writes=[r_obuf],
                          tag=f'C.oe{qi}.{hs}')

        for s in range(len(batches) + 1):
            if s < len(batches):
                emit_tr(s)
            if s >= 1:
                emit_av(s - 1)
        for hs in range(2):
            bank, r_bank = ob[hs]
            if nrounds == 2:
                S_.op('vector', lambda e, a=ebuf, b=bank, c=st[:, 21:22],
                      d=obuf[:, hs * 512:(hs + 1) * 512]:
                      e.scalar_tensor_tensor(out=a, in0=b, scalar=c, in1=d,
                                             op0=ALU.mult, op1=ALU.add),
                      reads=[r_bank, r_st, r_obuf], writes=[r_ebuf],
                      tag=f'C.e0{qi}.{hs}')
                S_.op('vector', lambda e, a=ebufB, b=ebuf, c=st[:, 10:11],
                      d=gaf[:, hs * 512:(hs + 1) * 512]:
                      e.scalar_tensor_tensor(out=a, in0=b, scalar=c, in1=d,
                                             op0=ALU.mult, op1=ALU.mult),
                      reads=[r_ebuf, r_st, r_gaf], writes=[r_ebufB],
                      tag=f'C.e1{qi}.{hs}')
            else:
                S_.op('vector', lambda e, a=ebufB, b=bank, c=st[:, 10:11],
                      d=gaf[:, hs * 512:(hs + 1) * 512]:
                      e.scalar_tensor_tensor(out=a, in0=b, scalar=c, in1=d,
                                             op0=ALU.mult, op1=ALU.mult),
                      reads=[r_bank, r_st, r_gaf], writes=[r_ebufB],
                      tag=f'C.e1{qi}.{hs}')
            S_.op('vector', lambda e, a=x2b[:, hs * 512:(hs + 1) * 512],
                  b=ebufB, c=xt[:, hs * 512:(hs + 1) * 512]:
                  e.tensor_tensor(out=a, in0=b, in1=c, op=ALU.add),
                  reads=[r_ebufB, r_xt], writes=[r_x2b], tag=f'C.e2{qi}.{hs}')
        S_.dma('scalar', lambda e, a=d_x2[qi * 128:(qi + 1) * 128, :], b=x2b:
               e.dma_start(out=a, in_=b),
               reads=[r_x2b], writes=[r_x2[qi]], tag=f'C.xs{qi}')

    def load_hfmc(g):
        for qd, qu in enumerate(('sync', 'scalar', 'sync', 'scalar')):
            kc0 = qd * (KC // 4)
            hi = d_hfm[kc0:kc0 + KC // 4, :,
                       g * 512:(g + 1) * 512].transpose([1, 0, 2])
            S_.dma(qu, lambda e, a=hfmc[:, kc0 * 512:(kc0 + KC // 4) * 512]
                   .rearrange('p (a b) -> p a b', b=512),
                   b=hi: e.dma_start(out=a, in_=b),
                   reads=[r_hfm[g]], writes=[r_hfmc], tag=f'C.hl{g}.{qd}')

    load_hfmc(0)
    for g in range(8):
        for oc in range(KC):
            bank, r_bank = ps[5 + (oc % 2)]
            for kc in range(KC):
                S_.op('tensor', lambda e, a=bank,
                      b=wqt[:, kc * H + oc * 128:kc * H + oc * 128 + 128],
                      c=hfmc[:, kc * 512:(kc + 1) * 512],
                      st=(kc == 0), sp=(kc == KC - 1):
                      e.matmul(a, b, c, start=st, stop=sp),
                      reads=[r_wqt, r_hfmc], writes=[r_bank],
                      tag=f'C.qm{g}.{oc}.{kc}')
            S_.op('scalar', lambda e, a=qg[:, oc * 512:(oc + 1) * 512], b=bank:
                  e.copy(out=a, in_=b),
                  reads=[r_bank], writes=[r_qg], tag=f'C.qe{g}.{oc}')
        if g + 1 < 8:
            load_hfmc(g + 1)
        for rr in range(4):
            qi = g * 4 + rr
            c_part1(qi)
            if qi > 0:
                c_part2(qi - 1)
    c_part2(NT - 1)

    al.free((wqt, r_wqt))
    al.free((hfmc, r_hfmc)); al.free((qg, r_qg))
    al.free(pbuf2[0]); al.free(pbuf2[1]); al.free((dscr, r_dscr)); al.free(pmb2[0]); al.free(pmb2[1]); al.free(ptb[0]); al.free(ptb[1])
    al.free(cxb[0]); al.free(cxb[1]); al.free(x2b2[0])
    al.free(stc2[0]); al.free(stc2[1])
    al.free((ebuf, r_ebuf)); al.free((ebufB, r_ebufB)); al.free((obuf, r_obuf))
    al.free((kfm, r_kfm)); al.free((vtm, r_vtm)); al.free((gaf, r_gaf))
    al.free((msk, r_msk))

    if upto == 'C':
        with nc.Block() as block:
            S_.finalize(nc, block)
        mybir.codegen_inst_isa_subclasses(nc)
        _BUILD_CACHE[key] = nc
        return nc

    modDa, r_modDa = sb('modDa', [128, 2 * H], BF16)
    modDg, r_modDg = sb('modDg', [128, H], BF16)
    mdt, r_mdt = sb('mdt', [1, H], F32)
    for j in range(3):
        S_.dma('sync', lambda e, a=mdt, b=d_mod[3 + j:4 + j, :]:
               e.dma_start(out=a, in_=b), reads=[r_mod_d], writes=[r_mdt],
               tag=f'D.ml{j}')
        if j < 2:
            bcast_row(mdt, r_mdt, modDa[:, j * H:(j + 1) * H], r_modDa,
                      6, f'D.mb{j}')
        else:
            bcast_row(mdt, r_mdt, modDg, r_modDg, 6, f'D.mb{j}')
    al.free((mdt, r_mdt)); al.free((onesb, r_onesb))

    h2q, r_h2q = sb('h2q', [128, KC * S], BF16)
    r_h2qt = [S_.res(f'h2qt{t}') for t in range(NT)]
    wdt, r_wdt = sb('wdT', [128, IC * H], BF16)
    wdt3 = wdt.rearrange('p (a b) -> p a b', b=H)
    IH = I // 2
    tq = {
        'bm': sb('tq_bm', [128, 64], F32),
        'sc': sb('tq_sc', [128, 64], F32),
        'rsc': sb('tq_rsc', [128, 64], F32),
        'xn': sb('tq_xn', [128, H], F32),
        'z': sb('tq_z', [128, H], F32),
    }
    dxb = [sb(f'dxb{i}', [128, H], F32) for i in range(2)]
    dq1, r_dq1 = sb('dq1', [128, H], F32)
    dtr, r_dtr = sb('dtr', [128, H], F32)
    h2q3 = h2q.rearrange('p (a b) -> p a b', b=S)
    wldst1, r_wldst1 = sb('wldst1', [128, H], F32)

    statsD, r_statsD = sb('statsD', [128, 64], F32)

    def d1_stats4(t0):
        st = stats
        for j in range(4):
            t = t0 + j
            xt, r_xt = dxb[t % 2]
            S_.dma('sync', lambda e, a=xt, b=d_x2[t * 128:(t + 1) * 128, :]:
                   e.dma_start(out=a, in_=b),
                   reads=[r_x2[t]], writes=[r_xt], tag=f'D.sxl{t}')
            S_.op('vector', lambda e, a=st[:, j:j + 1], b=xt: e.reduce_sum(
                out=a, in_=b, axis=AXX), reads=[r_xt], writes=[r_stats],
                tag=f'D.ss{t}')
            S_.op('scalar', lambda e, a=dtr, b=xt, c=st[:, 4 + j:5 + j]:
                  e.activation(out=a, in_=b, func=AF.Square, accum_out=c),
                  reads=[r_xt], writes=[r_dtr, r_stats], tag=f'D.sq{t}')
        sD = statsD[:, 2 * t0:2 * t0 + 8].rearrange('p (a b) -> p a b', b=2)
        S_.op('vector', lambda e, a=sD[:, :, 0:1].squeeze(2), b=st[:, 0:4]:
              e.tensor_scalar(out=a, in0=b, scalar1=INV_H, scalar2=None,
                              op0=ALU.mult),
              reads=[r_stats], writes=[r_statsD], tag=f'D.sm{t0}')
        S_.op('vector', lambda e, a=st[:, 8:12], b=st[:, 4:8]: e.tensor_scalar(
            out=a, in0=b, scalar1=INV_H, scalar2=None, op0=ALU.mult),
            reads=[r_stats], writes=[r_stats], tag=f'D.se{t0}')
        S_.op('vector', lambda e, a=st[:, 12:16], b=sD[:, :, 0:1].squeeze(2):
              e.tensor_tensor(out=a, in0=b, in1=b, op=ALU.mult),
              reads=[r_statsD], writes=[r_stats], tag=f'D.sm2{t0}')
        S_.op('vector', lambda e, a=st[:, 16:20], b=st[:, 8:12], c=st[:, 12:16]:
              e.tensor_tensor(out=a, in0=b, in1=c, op=ALU.subtract),
              reads=[r_stats], writes=[r_stats], tag=f'D.sv{t0}')
        S_.op('vector', lambda e, a=st[:, 16:20], b=st[:, 16:20]: e.tensor_scalar(
            out=a, in0=b, scalar1=EPS, scalar2=None, op0=ALU.add),
            reads=[r_stats], writes=[r_stats], tag=f'D.sve{t0}')
        S_.op('vector', lambda e, a=st[:, 20:24], b=st[:, 16:20]: e.reciprocal(
            out=a, in_=b), reads=[r_stats], writes=[r_stats], tag=f'D.sr{t0}')
        S_.op('scalar', lambda e, a=st[:, 24:28], b=st[:, 20:24]: e.activation(
            out=a, in_=b, func=AF.Sqrt),
            reads=[r_stats], writes=[r_stats], tag=f'D.ssq{t0}')
        S_.op('vector', lambda e, a=st[:, 8:12], b=st[:, 24:28]: e.tensor_tensor(
            out=a, in0=b, in1=b, op=ALU.mult),
            reads=[r_stats], writes=[r_stats], tag=f'D.sn1{t0}')
        S_.op('vector', lambda e, a=st[:, 12:16], b=st[:, 8:12], c=st[:, 16:20]:
              e.tensor_tensor(out=a, in0=b, in1=c, op=ALU.mult),
              reads=[r_stats], writes=[r_stats], tag=f'D.sn2{t0}')
        S_.op('vector', lambda e, a=st[:, 12:16], b=st[:, 12:16]: e.tensor_scalar(
            out=a, in0=b, scalar1=-0.5, scalar2=1.5, op0=ALU.mult, op1=ALU.add),
            reads=[r_stats], writes=[r_stats], tag=f'D.sn3{t0}')
        S_.op('vector', lambda e, a=sD[:, :, 1:2].squeeze(2), b=st[:, 24:28],
              c=st[:, 12:16]: e.tensor_tensor(out=a, in0=b, in1=c, op=ALU.mult),
              reads=[r_stats], writes=[r_statsD], tag=f'D.srstd{t0}')

    def d1_prep(t):
        xt, r_xt = dxb[t % 2]
        S_.dma('sync', lambda e, a=xt, b=d_x2[t * 128:(t + 1) * 128, :]:
               e.dma_start(out=a, in_=b),
               reads=[r_x2[t]], writes=[r_xt], tag=f'D.xl{t}')
        S_.op('vector', lambda e, a=dq1, b=xt, c=statsD[:, 2 * t:2 * t + 1],
              d=statsD[:, 2 * t + 1:2 * t + 2]: e.tensor_scalar(
              out=a, in0=b, scalar1=c, scalar2=d, op0=ALU.subtract,
              op1=ALU.mult),
              reads=[r_xt, r_statsD], writes=[r_dq1], tag=f'D.nrm{t}')
        S_.op('vector', lambda e, a=dtr, b=dq1, c=modDa[:, 0:H]:
              e.tensor_tensor(out=a, in0=b, in1=c, op=ALU.mult),
              reads=[r_dq1, r_modDa], writes=[r_dtr], tag=f'D.h1{t}')
        S_.op('vector', lambda e, a=dq1, b=dtr, c=modDa[:, H:2 * H]:
              e.tensor_tensor(out=a, in0=b, in1=c, op=ALU.add),
              reads=[r_dtr, r_modDa], writes=[r_dq1], tag=f'D.h2{t}')
        quant(dq1, r_dq1, H, tq, dtr, r_dtr, f'D.q{t}')
        for half in range(2):
            bank, r_bank = ps[5 + half]
            for j4 in range(4):
                j = half * 4 + j4
                S_.op('tensor', lambda e, a=bank[:, j4 * 128:(j4 + 1) * 128],
                      b=dtr[:, j * 128:(j + 1) * 128]: e.transpose(a, b, idf),
                      reads=[r_dtr, r_idf], writes=[r_bank], tag=f'D.tr{t}.{j}')
            dv = h2q3[:, half * 4:half * 4 + 4, t * 128:t * 128 + 128]
            S_.op('scalar', lambda e, a=dv, b=bank[:, 0:512].rearrange(
                'p (a b) -> p a b', b=128): e.copy(out=a, in_=b),
                reads=[r_bank], writes=[r_h2qt[t]], tag=f'D.h2e{t}.{half}')

    wqstg, r_wqstg = sb('wqstg', [128, 512], BF16)

    def gu_subjob(d_w, d_q, r_q, c, s_, nm):
        c0, wc = NSL[c]
        row = c0 + s_ * 128
        S_.dma('sync', lambda e, a=wldst1,
               b=d_w[row:row + 128, :]: e.dma_start(out=a, in_=b),
               writes=[r_wldst1], tag=f'{nm}.l{c}.{s_}')
        quant(wldst1, r_wldst1, H, tq, tq['xn'][0][:, 0:H],
              tq['xn'][1], f'{nm}.q{c}.{s_}')
        dqw, r_dqw = tq['xn']
        for half in range(2):
            bank, r_bank = ps[5 + half]
            for j4 in range(4):
                j = half * 4 + j4
                S_.op('tensor', lambda e, a=bank[:, j4 * 128:(j4 + 1) * 128],
                      b=dqw[:, j * 128:(j + 1) * 128]: e.transpose(a, b, idf),
                      reads=[r_dqw, r_idf], writes=[r_bank],
                      tag=f'{nm}.t{c}.{s_}.{j}')
            S_.op('scalar', lambda e, a=wqstg, b=bank[:, 0:512]:
                  e.copy(out=a, in_=b),
                  reads=[r_bank], writes=[r_wqstg], tag=f'{nm}.e{c}.{s_}.{half}')
            dvo = d_q[half * 4:half * 4 + 4, :, row:row + 128].transpose([1, 0, 2])
            S_.dma('scalar', lambda e, a=dvo,
                   b=wqstg.rearrange('p (a b) -> p a b', b=128):
                   e.dma_start(out=a, in_=b),
                   reads=[r_wqstg], writes=[r_q], tag=f'{nm}.s{c}.{s_}.{half}')

    def d1_head(cp, t):
        ca, cb = 2 * cp, 2 * cp + 1
        wca, wcb = NSL[ca][1], NSL[cb][1]
        wtot = wca + wcb
        par = t % 2
        silb, r_silb = silb2[par]
        hmb, r_hmb = hmb2[par]
        dqh, r_dqh = dqh2[par]
        pg = [ps[0], ps[1]]
        pu = [ps[2], ps[3]]
        for kc in range(KC):
            for ci, wc in ((0, wca), (1, wcb)):
                S_.op('tensor', lambda e, a=pg[ci][0][:, 0:wc],
                      b=h2q[:, kc * S + t * 128:kc * S + t * 128 + 128],
                      c=wgqc2[ci][0][:, kc * 512:kc * 512 + wc],
                      st=(kc == 0), sp=(kc == KC - 1):
                      e.matmul(a, b, c, start=st, stop=sp),
                      reads=[r_h2qt[t], wgqc2[ci][1]], writes=[pg[ci][1]],
                      tag=f'D.gm{ca + ci}.{t}.{kc}')
                S_.op('tensor', lambda e, a=pu[ci][0][:, 0:wc],
                      b=h2q[:, kc * S + t * 128:kc * S + t * 128 + 128],
                      c=wuqc2[ci][0][:, kc * 512:kc * 512 + wc],
                      st=(kc == 0), sp=(kc == KC - 1):
                      e.matmul(a, b, c, start=st, stop=sp),
                      reads=[r_h2qt[t], wuqc2[ci][1]], writes=[pu[ci][1]],
                      tag=f'D.um{ca + ci}.{t}.{kc}')
        for ci, off, wc in ((0, 0, wca), (1, wca, wcb)):
            S_.op('scalar', lambda e, a=silb[:, off:off + wc],
                  b=pg[ci][0][:, 0:wc]: e.activation(out=a, in_=b, func=AF.Silu),
                  reads=[pg[ci][1]], writes=[r_silb], tag=f'D.si{ca + ci}.{t}')
            S_.op('vector', lambda e, a=hmb[:, off:off + wc],
                  b=silb[:, off:off + wc], c=pu[ci][0][:, 0:wc]:
                  e.tensor_tensor(out=a, in0=b, in1=c, op=ALU.mult),
                  reads=[r_silb, pu[ci][1]], writes=[r_hmb],
                  tag=f'D.hm{ca + ci}.{t}')
        quant(hmb[:, 0:wtot], r_hmb, wtot, tqs[par], dqh[:, 0:wtot], r_dqh,
              f'D.qh{cp}.{t}')

    def d1_tail(cp, t):
        ca = 2 * cp
        c0 = NSL[ca][0]
        wtot = NSL[ca][1] + NSL[ca + 1][1]
        subs = wtot // 128
        par = t % 2
        dqh, r_dqh = dqh2[par]
        for half in range(2):
            bank, r_bank = ps[(7, 4)[half]]
            jn = min(4, subs - half * 4)
            if jn <= 0:
                continue
            for j4 in range(jn):
                j = half * 4 + j4
                S_.op('tensor', lambda e, a=bank[:, j4 * 128:(j4 + 1) * 128],
                      b=dqh[:, j * 128:(j + 1) * 128]: e.transpose(a, b, idb),
                      reads=[r_dqh, r_idb], writes=[r_bank],
                      tag=f'D.ht{cp}.{t}.{j}')
            S_.op('scalar', lambda e, a=hqst[:, half * 512:half * 512 + jn * 128],
                  b=bank[:, 0:jn * 128]: e.copy(out=a, in_=b),
                  reads=[r_bank], writes=[r_hqst], tag=f'D.he{cp}.{t}.{half}')
        hoo = d_hmq[t, :, c0:c0 + wtot]
        S_.dma('scalar', lambda e, a=hoo, b=hqst[:, 0:wtot]:
               e.dma_start(out=a, in_=b),
               reads=[r_hqst], writes=[r_hmq[t]], tag=f'D.hs{cp}.{t}')

    for t0 in range(0, NT, 4):
        d1_stats4(t0)
    wldst, r_wldst = sb('wldst', [128, IH], F32)

    def wd_subjob(r, hh):
        dqw, r_dqw = wldst, r_wldst
        S_.dma('sync', lambda e, a=wldst,
               b=d_wd[r * 128:(r + 1) * 128, hh * IH:(hh + 1) * IH]:
               e.dma_start(out=a, in_=b), writes=[r_wldst],
               tag=f'D.wdl{r}.{hh}')
        for hv in range(2):
            sl = wldst[:, hv * 704:(hv + 1) * 704]
            quant(sl, r_wldst, 704, tq, sl, r_wldst, f'D.wdq{r}.{hh}.{hv}')
        for half in range(3):
            bank, r_bank = ps[5 + (half % 2)]
            jn = 4 if half < 2 else 3
            for j4 in range(jn):
                j = half * 4 + j4
                jg = hh * 11 + j
                S_.op('tensor', lambda e,
                      a=bank[:, j4 * 128:(j4 + 1) * 128],
                      b=dqw[:, j * 128:(j + 1) * 128]:
                      e.transpose(a, b, idf),
                      reads=[r_dqw, r_idf], writes=[r_bank],
                      tag=f'D.wdt{r}.{jg}')
            g0 = hh * 11 + half * 4
            dv = wdt3[:, g0:g0 + jn, r * 128:r * 128 + 128]
            S_.op('scalar', lambda e, a=dv, b=bank[:, 0:jn * 128].rearrange(
                'p (a b) -> p a b', b=128): e.copy(out=a, in_=b),
                reads=[r_bank], writes=[r_wdt], tag=f'D.wde{r}.{hh}.{half}')

    d2bufs = {}
    silb2 = [sb(f'silb{i}', [128, 1024], F32) for i in range(2)]
    hmb2 = [sb(f'hmb{i}', [128, 1024], F32) for i in range(2)]
    dqh2 = [sb(f'dqh{i}', [128, 1024], BF16) for i in range(2)]
    hqst, r_hqst = sb('hqst', [128, 1024], BF16)
    wgqc2 = [sb(f'wgqc{i}', [128, KC * 512], BF16) for i in range(2)]
    wuqc2 = [sb(f'wuqc{i}', [128, KC * 512], BF16) for i in range(2)]
    tqs = []
    for i in range(2):
        tqs.append({
            'bm': sb(f'tqs{i}_bm', [128, 64], F32),
            'sc': sb(f'tqs{i}_sc', [128, 64], F32),
            'rsc': sb(f'tqs{i}_rsc', [128, 64], F32),
            'xn': silb2[i],
            'z': tq['z'],
        })

    def d2_tile(t):
        hmqt, (outb, r_outb), (eb2, r_eb2) = (
            d2bufs['hmqt'], d2bufs['outb'], d2bufs['eb2'])
        hq, r_hq = hmqt[t % 2]
        S_.dma('sync', lambda e, a=hq, b=d_hmq[t]: e.dma_start(out=a, in_=b),
               reads=[r_hmq[t]], writes=[r_hq], tag=f'D2.hl{t}')
        xt, r_xt = dxb[t % 2]
        S_.dma('sync', lambda e, a=xt, b=d_x2[t * 128:(t + 1) * 128, :]:
               e.dma_start(out=a, in_=b),
               reads=[r_x2[t]], writes=[r_xt], tag=f'D2.xl{t}')
        for hs in range(2):
            bank, r_bank = ps[0 + hs]
            for kc in range(IC):
                S_.op('tensor', lambda e, a=bank,
                      b=hq[:, kc * 128:(kc + 1) * 128],
                      c=wdt[:, kc * H + hs * 512:kc * H + (hs + 1) * 512],
                      st=(kc == 0), sp=(kc == IC - 1):
                      e.matmul(a, b, c, start=st, stop=sp),
                      reads=[r_hq, r_wdt], writes=[r_bank],
                      tag=f'D2.dm{t}.{hs}.{kc}')
            S_.op('vector', lambda e, a=eb2, b=bank,
                  c=modDg[:, hs * 512:(hs + 1) * 512]:
                  e.tensor_tensor(out=a, in0=b, in1=c, op=ALU.mult),
                  reads=[r_bank, r_modDg], writes=[r_eb2], tag=f'D2.e1{t}.{hs}')
            S_.op('vector', lambda e, a=outb[:, hs * 512:(hs + 1) * 512],
                  b=eb2, c=xt[:, hs * 512:(hs + 1) * 512]:
                  e.tensor_tensor(out=a, in0=b, in1=c, op=ALU.add),
                  reads=[r_eb2, r_xt], writes=[r_outb], tag=f'D2.e2{t}.{hs}')
        S_.dma('scalar', lambda e, a=d_out[t * 128:(t + 1) * 128, :], b=outb:
               e.dma_start(out=a, in_=b),
               reads=[r_outb], writes=[r_out], tag=f'D2.os{t}')

    def load_guq(c):
        c0, wc = NSL[c]
        wgqc, r_wgqcc = wgqc2[c % 2]
        wuqc, r_wuqcc = wuqc2[c % 2]
        for (dq, rq, dst, rdst, nm) in ((d_wgq, r_wgq[c], wgqc, r_wgqcc, 'g'),
                                        (d_wuq, r_wuq[c], wuqc, r_wuqcc, 'u')):
            hi = dq[:, :, c0:c0 + wc].transpose([1, 0, 2])
            S_.dma('sync', lambda e, a=dst.rearrange(
                'p (a b) -> p a b', b=512)[:, :, 0:wc], b=hi:
                e.dma_start(out=a, in_=b),
                reads=[rq], writes=[rdst], tag=f'D.wl{nm}{c}')

    gu_jobs = []
    for c in range(6):
        for s_ in range(NSL[c][1] // 128):
            gu_jobs.append(lambda c=c, s_=s_: gu_subjob(d_wg, d_wgq, r_wgq[c],
                                                        c, s_, 'D.g'))
            gu_jobs.append(lambda c=c, s_=s_: gu_subjob(d_wu, d_wuq, r_wuq[c],
                                                        c, s_, 'D.u'))
    wd_jobs = []
    for r in range(8):
        for hh in range(2):
            wd_jobs.append(lambda r=r, hh=hh: wd_subjob(r, hh))

    LAG = 9
    for t in range(NT):
        u = t - LAG
        d1_prep(t)
        if u >= 0:
            d1_head(0, u)
            if u > 0:
                d1_tail(0, u - 1)
        for _ in range(2):
            if gu_jobs:
                gu_jobs.pop(0)()
        if t == 7:
            load_guq(0)
            load_guq(1)
    wdi = 0
    for u in range(NT - LAG, NT):
        d1_head(0, u)
        d1_tail(0, u - 1)
        if wdi % 2 == 0 and wd_jobs:
            wd_jobs.pop(0)()
        wdi += 1
    d1_tail(0, NT - 1)

    load_guq(2)
    load_guq(3)
    for t in range(NT):
        d1_head(1, t)
        if t > 0:
            d1_tail(1, t - 1)
        if wdi % 2 == 0 and wd_jobs:
            wd_jobs.pop(0)()
        wdi += 1
    d1_tail(1, NT - 1)

    al.free((dq1, r_dq1)); al.free((dtr, r_dtr)); al.free((modDa, r_modDa))
    al.free((wldst, r_wldst)); al.free((wldst1, r_wldst1))
    al.free((wqstg, r_wqstg))
    for k in ('bm', 'sc', 'rsc', 'xn'):
        al.free(tq[k])
    d2bufs['hmqt'] = [sb(f'hmqt{i}', [128, IC * 128], BF16) for i in range(2)]
    d2bufs['outb'] = sb('outb', [128, H], F32)
    d2bufs['eb2'] = sb('eb2', [128, 512], F32)

    load_guq(4)
    load_guq(5)
    for t in range(NT):
        d1_head(2, t)
        if t > 0:
            d1_tail(2, t - 1)
        if t >= 3:
            d2_tile(t - 3)
    d1_tail(2, NT - 1)
    for t in range(NT - 3, NT):
        d2_tile(t)

    if upto == 'D1':
        with nc.Block() as block:
            S_.finalize(nc, block)
        mybir.codegen_inst_isa_subclasses(nc)
        _BUILD_CACHE[key] = nc
        return nc


    with nc.Block() as block:
        S_.finalize(nc, block)
    mybir.codegen_inst_isa_subclasses(nc)
    _BUILD_CACHE[key] = nc
    return nc



def kernel(L0_attn_x, L0_attn_Wq, L0_attn_Wk, L0_attn_Wv, L0_ffn_w_gate,
           L0_ffn_w_up, L0_ffn_w_down, L0_c, L0_wmod):
    nc = build(debug=False)
    x = np.ascontiguousarray(np.asarray(L0_attn_x, dtype=np.float32))
    base = {
        'wq': np.ascontiguousarray(np.asarray(L0_attn_Wq, np.float32)),
        'wk': np.ascontiguousarray(np.asarray(L0_attn_Wk, np.float32)),
        'wv': np.ascontiguousarray(np.asarray(L0_attn_Wv, np.float32)),
        'wg': np.ascontiguousarray(np.asarray(L0_ffn_w_gate, np.float32)),
        'wu': np.ascontiguousarray(np.asarray(L0_ffn_w_up, np.float32)),
        'wd': np.ascontiguousarray(np.asarray(L0_ffn_w_down, np.float32)),
        'wmod': np.ascontiguousarray(np.asarray(L0_wmod, np.float32)),
    }
    carr = np.ascontiguousarray(np.asarray(L0_c, np.float32))
    in_maps = []
    for b in range(NCORES):
        m = dict(base)
        m['x'] = np.ascontiguousarray(x[b])
        m['cvec'] = np.ascontiguousarray(carr[b:b + 1])
        in_maps.append(m)
    res = run_bass_kernel_spmd(nc, in_maps, core_ids=list(range(NCORES)))
    out = np.stack([res.results[b]['out'] for b in range(NCORES)], axis=0)
    return out


# version 17
# speedup vs baseline: 1.0388x; 1.0061x over previous
import sys
import numpy as np

sys.path.insert(0, '/opt/trn_rl_repo')

import concourse.bass as bass
import concourse.mybir as mybir
from concourse.bass_utils import run_bass_kernel_spmd

F32 = mybir.dt.float32
BF16 = mybir.dt.bfloat16
AF = mybir.ActivationFunctionType
ALU = mybir.AluOpType
AXX = mybir.AxisListType.X

S = 4096
H = 1024
I = 2816
NT = S // 128
KC = H // 128
IC = I // 128
NSL = [(0, 512), (512, 512), (1024, 512), (1536, 512), (2048, 512), (2560, 256)]
HMAGIC = float(1.5 * 2 ** 22)
EPS = 1e-5
INV_H = 1.0 / H
SMSCALE = float(1.0 / 32.0)
NCORES = 8

from concourse import dve_ops as _dve_ops
from concourse.dve_op import DveOp as _DveOp
from concourse.dve_spec import (Spec as _Spec, lower as _dve_lower, Src0 as _S0,
                                Src1 as _S1, C0 as _C0, C1 as _C1, C2 as _C2,
                                One as _One, sq as _sq)


def _mk_mg(src):
    t2 = _sq(src)
    g2 = t2 >= _C0
    g4 = t2 >= _C1
    s3 = ((g2 + g4) + g4) + _One
    return s3 * _C2


def _np_step(x):
    t2 = x * x
    return np.where(t2 >= 16.0, 2.0, np.where(t2 >= 4.0, 1.0, 0.5))


def _ref_fp4rnd(in0, in1, s0, s1, imm2):
    st = _np_step(in0)
    mg = imm2 * (st / 0.5)
    return np.round(in0 / st) * st + mg


def _ref_fp4sub(in0, in1, s0, s1, imm2):
    st = _np_step(in1)
    mg = imm2 * (st / 0.5)
    return in0 - mg


def _register_fp4_ops():
    have = {o.name: o for o in _dve_ops.OPS}
    if 'FP4RND' in have:
        return have['FP4RND'], have['FP4SUB']
    specA = _Spec(body=_S0 + _mk_mg(_S0))
    specB = _Spec(body=_S0 - _mk_mg(_S1))
    shaA = _dve_lower(specA, ver='v3', perf_en=False, name='FP4RND').sha('v3')
    shaB = _dve_lower(specB, ver='v3', perf_en=False, name='FP4SUB').sha('v3')
    opA = _DveOp('FP4RND', specA, subdim=False, uops_sha={'v3': shaA},
                 reference=_ref_fp4rnd)
    opB = _DveOp('FP4SUB', specB, subdim=False, uops_sha={'v3': shaB},
                 reference=_ref_fp4sub)
    for op in (opA, opB):
        _dve_ops.OPS.append(op)
        _dve_ops._SUB_OPCODE_FOR_NAME[op.name] = (
            _dve_ops._CUSTOM_DVE_ROW_BASE + len(_dve_ops.OPS) - 1)
        _dve_ops.CUSTOM_DVE_REFS[op.name] = op.reference
        _dve_ops.CUSTOM_DVE_SPECS[op.name] = op.spec
    assert max(_dve_ops._SUB_OPCODE_FOR_NAME.values()) < 0x20
    return opA, opB


OP_FP4RND, OP_FP4SUB = _register_fp4_ops()


ENGINES = ('tensor', 'vector', 'scalar', 'gpsimd', 'sync')
DRAIN_ENGINES = {'vector', 'scalar', 'gpsimd'}


class Res:
    __slots__ = ('name', 'writers', 'readers', 'accessors', 'is_psum', 'interval',
                 'dma_writes', 'sem')

    def __init__(self, name, is_psum=False, interval=None):
        self.name = name
        self.writers = {}
        self.readers = {}
        self.accessors = {}
        self.is_psum = is_psum
        self.interval = interval
        self.dma_writes = 0
        self.sem = None


class Item:
    __slots__ = ('eng', 'idx', 'emit', 'deps', 'kind', 'need_inc', 'tag',
                 'dma_res', 'dma_count')

    def __init__(self, eng, idx, emit, deps, kind, tag):
        self.eng = eng
        self.idx = idx
        self.emit = emit
        self.deps = deps
        self.kind = kind
        self.need_inc = False
        self.tag = tag
        self.dma_res = None
        self.dma_count = 0


class Sched:
    def __init__(self):
        self.items = {e: [] for e in ENGINES}
        self.retired = []

    def res(self, name, is_psum=False, interval=None):
        return Res(name, is_psum, interval)

    @staticmethod
    def _merge(d, eng, idx, kind):
        cur = d.get(eng)
        if cur is None or idx > cur[0]:
            d[eng] = (idx, kind)

    def op(self, eng, emit, reads=(), writes=(), kind='op', tag=''):
        idx = len(self.items[eng])
        deps = []
        for r in reads:
            for e2, (j, k2) in r.writers.items():
                deps.append((e2, j, k2))
            if r.is_psum:
                for e2, (j, k2) in r.accessors.items():
                    if e2 != eng:
                        deps.append((e2, j, k2))
        for w in writes:
            for e2, (j, k2) in w.writers.items():
                deps.append((e2, j, k2))
            for e2, (j, k2) in w.readers.items():
                deps.append((e2, j, k2))
            if w.is_psum:
                for e2, (j, k2) in w.accessors.items():
                    if e2 != eng:
                        deps.append((e2, j, k2))
        it = Item(eng, idx, emit, deps, kind, tag)
        if kind == 'dma':
            assert len(writes) == 1, f'dma must write exactly one res: {tag}'
            w0 = writes[0]
            it.dma_res = w0
            it.dma_count = getattr(w0, 'dma_writes', 0) + 1
            w0.dma_writes = it.dma_count
        self.items[eng].append(it)
        for r in reads:
            self._merge(r.readers, eng, idx, kind)
            if r.is_psum:
                self._merge(r.accessors, eng, idx, kind)
        for w in writes:
            w.writers = {}
            w.readers = {}
            self._merge(w.writers, eng, idx, kind)
            if w.is_psum:
                self._merge(w.accessors, eng, idx, kind)
        return it

    def dma(self, eng, emit, reads=(), writes=(), tag=''):
        return self.op(eng, emit, reads, writes, kind='dma', tag=tag)

    def finalize(self, nc, block):
        waits = {e: [] for e in ENGINES}
        drain_req = {e: [None] * len(self.items[e]) for e in ENGINES}
        for e in ENGINES:
            for it in self.items[e]:
                wl = []
                for (e2, j, k2) in it.deps:
                    if e2 == e and k2 != 'dma' and it.kind != 'dma':
                        if e in DRAIN_ENGINES:
                            cur = drain_req[e][it.idx]
                            drain_req[e][it.idx] = j if cur is None else max(cur, j)
                    elif e2 == e and k2 != 'dma' and it.kind == 'dma':
                        self.items[e2][j].need_inc = True
                        wl.append((e2, j, k2))
                    elif e2 == e and k2 == 'dma':
                        if not (it.kind == 'dma'
                                and self.items[e2][j].dma_res is it.dma_res):
                            wl.append((e2, j, k2))
                    else:
                        self.items[e2][j].need_inc = True
                        wl.append((e2, j, k2))
                waits[e].append(wl)
        prefix = {e: [] for e in ENGINES}
        for e in ENGINES:
            c_op = 0
            for it in self.items[e]:
                if it.need_inc and it.kind != 'dma':
                    c_op += 1
                prefix[e].append(c_op)
        semC = {e: nc.alloc_semaphore(name=f'semC_{e}') for e in ENGINES}
        dma_resources = []
        for e in ENGINES:
            for it in self.items[e]:
                if it.kind == 'dma' and it.dma_res.sem is None:
                    it.dma_res.sem = nc.alloc_semaphore(
                        name=f'dsem{len(dma_resources)}')
                    dma_resources.append(it.dma_res)
        finals = {e: [] for e in ENGINES}
        seen = set()
        for e in ENGINES:
            for it in reversed(self.items[e]):
                if it.kind == 'dma' and id(it.dma_res) not in seen:
                    seen.add(id(it.dma_res))
                    finals[e].append((it.dma_res.sem, 16 * it.dma_res.dma_writes))

        def emit_stream(e, eng):
            watermark = {}
            drained_to = -1
            for it in self.items[e]:
                dr = drain_req[e][it.idx]
                if dr is not None and dr > drained_to:
                    eng.drain()
                    drained_to = it.idx - 1
                for (e2, j, k2) in waits[e][it.idx]:
                    tgt = self.items[e2][j]
                    if k2 == 'dma':
                        sem = tgt.dma_res.sem
                        val = 16 * tgt.dma_count
                        if watermark.get(id(sem), 0) < val:
                            eng.wait_ge(sem, val)
                            watermark[id(sem)] = val
                    else:
                        c_op = prefix[e2][j]
                        if watermark.get((e2, 'o'), 0) < c_op:
                            eng.wait_ge(semC[e2], c_op)
                            watermark[(e2, 'o')] = c_op
                inst = it.emit(eng)
                if it.kind == 'dma':
                    inst.then_inc(it.dma_res.sem, 16)
                elif it.need_inc:
                    inst.then_inc(semC[e], 1)
            for sem, val in finals[e]:
                eng.wait_ge(sem, val)

        for e in ENGINES:
            if not self.items[e]:
                continue
            getattr(block, e)(lambda eng, e=e: emit_stream(e, eng))


class SBAlloc:
    def __init__(self, sched, nc):
        self.s = sched
        self.base = (nc.sbuf_base + 63) // 64 * 64
        self.limit = nc.sbuf_top - self.base
        self.live = []
        self.hwm = 0

    def _find(self, size, align):
        spans = sorted((off, off + sz) for off, sz, _ in self.live)
        pos = 0
        for a, b in spans:
            p = (pos + align - 1) // align * align
            if p + size <= a:
                return p
            pos = max(pos, b)
        p = (pos + align - 1) // align * align
        if p + size > self.limit:
            raise MemoryError(f'SBUF overflow: need {size}B at {p}')
        return p

    def alloc(self, nc, name, shape, dtype, align=64):
        per_part = int(np.prod(shape[1:])) * mybir.dt.size(dtype)
        off = self._find(per_part, align)
        h = nc.alloc_sbuf_tensor_at(name, list(shape), dtype,
                                    offset=self.base + off, align_bytes=align)
        r = self.s.res(name, interval=(off, per_part))
        for (o2, s2, old) in self.s.retired:
            if not (off + per_part <= o2 or o2 + s2 <= off):
                for e2, v in old.writers.items():
                    Sched._merge(r.readers, e2, v[0], v[1])
                for e2, v in old.readers.items():
                    Sched._merge(r.readers, e2, v[0], v[1])
        self.live.append((off, per_part, r))
        self.hwm = max(self.hwm, off + per_part)
        return h.ap(), r

    def free(self, ap_res):
        _, r = ap_res
        self.live = [(o, s, rr) for (o, s, rr) in self.live if rr is not r]
        self.s.retired.append((r.interval[0], r.interval[1], r))



_BUILD_CACHE = {}


def build(debug=False, upto='D2'):
    key = ('k', debug, upto)
    if key in _BUILD_CACHE:
        return _BUILD_CACHE[key]

    nc = bass.Bass('TRN2', target_bir_lowering=False)
    S_ = Sched()
    al = SBAlloc(S_, nc)

    def sb(name, shape, dtype, align=64):
        return al.alloc(nc, name, shape, dtype, align)

    d_x = nc.dram_tensor('x', [S, H], F32, kind='ExternalInput').ap()
    d_wq = nc.dram_tensor('wq', [H, H], F32, kind='ExternalInput').ap()
    d_wk = nc.dram_tensor('wk', [H, H], F32, kind='ExternalInput').ap()
    d_wv = nc.dram_tensor('wv', [H, H], F32, kind='ExternalInput').ap()
    d_wg = nc.dram_tensor('wg', [I, H], F32, kind='ExternalInput').ap()
    d_wu = nc.dram_tensor('wu', [I, H], F32, kind='ExternalInput').ap()
    d_wd = nc.dram_tensor('wd', [H, I], F32, kind='ExternalInput').ap()
    d_c = nc.dram_tensor('cvec', [1, H], F32, kind='ExternalInput').ap()
    d_wm = nc.dram_tensor('wmod', [H, 6 * H], F32, kind='ExternalInput').ap()
    d_out = nc.dram_tensor('out', [S, H], F32, kind='ExternalOutput').ap()
    dbg = 'ExternalOutput' if debug else 'Internal'
    d_hfm = nc.dram_tensor('hfm_s', [KC, 128, S], BF16, kind=dbg).ap()
    d_x2 = nc.dram_tensor('x2_s', [S, H], F32, kind=dbg).ap()
    d_hmq = nc.dram_tensor('hmq_s', [NT, 128, IC * 128], BF16, kind=dbg).ap()
    d_mod = nc.dram_tensor('mod_s', [6, H], F32, kind=dbg).ap()
    d_wgq = nc.dram_tensor('wgq_s', [KC, 128, I], BF16, kind='Internal').ap()
    d_wuq = nc.dram_tensor('wuq_s', [KC, 128, I], BF16, kind='Internal').ap()

    r_hfm = [S_.res(f'd.hfm{t}') for t in range(8)]
    _x2g = [S_.res(f'd.x2g{i}') for i in range(4)]
    r_x2 = [_x2g[t // 8] for t in range(NT)]
    r_hmq = [S_.res(f'd.hmq{t}') for t in range(NT)]
    r_mod_d = S_.res('d.mod')
    _wgq1 = S_.res('d.wgq')
    _wuq1 = S_.res('d.wuq')
    r_wgq = [_wgq1] * 6
    r_wuq = [_wuq1] * 6
    r_out = S_.res('d.out')

    ps = []
    for i in range(8):
        dt_i = BF16 if i in (4, 7) else F32
        h = nc.alloc_psum_tensor(f'pb{i}', [128, 512], dt_i)
        ps.append((h.ap(), S_.res(f'pb{i}', is_psum=True)))

    idf, r_idf = sb('identF', [128, 128], F32)
    idb, r_idb = sb('identB', [128, 128], BF16)
    ca1f, r_ca1f = sb('ca1f', [128, H], BF16)
    saf, r_saf = sb('saf', [128, H], BF16)
    gaf, r_gaf = sb('gaf', [128, H], BF16)
    modBC_parts = [(ca1f, r_ca1f), (saf, r_saf), (gaf, r_gaf)]
    kfm, r_kfm = sb('K_fm', [128, KC * S], BF16)
    vtm, r_vtm = sb('V_tm', [128, NT * H], BF16)
    stats, r_stats = sb('stats', [128, 32], F32)
    onesb, r_onesb = sb('onesb', [1, 128], F32)
    S_.op('vector', lambda e: e.memset(onesb, 1.0), writes=[r_onesb],
          tag='p0.ones')

    def bcast_row(vec_ap, vec_res, dst_ap, dst_res, bank_i, tag):
        bank, r_bank = ps[bank_i]
        for sl in range(2):
            S_.op('tensor', lambda e, a=bank, b=onesb,
                  c=vec_ap[0:1, sl * 512:(sl + 1) * 512]:
                  e.matmul(a, b, c, start=True, stop=True),
                  reads=[r_onesb, vec_res], writes=[r_bank],
                  tag=tag + f'.mm{sl}')
            S_.op('vector', lambda e, a=dst_ap[:, sl * 512:(sl + 1) * 512],
                  b=bank: e.tensor_copy(out=a, in_=b),
                  reads=[r_bank], writes=[dst_res], tag=tag + f'.ev{sl}')

    def quant(x_ap, x_res, n, tmp, deq_out, deq_res, tag, deq_engine='vector',
              xn_engine='vector'):
        nb = n // 16
        (bm, bm_r), (sc, sc_r), (rsc, rsc_r) = tmp['bm'], tmp['sc'], tmp['rsc']
        (xn, xn_r), (z, z_r) = tmp['xn'], tmp['z']
        (y, y_r) = tmp['y'] if 'y' in tmp else (x_ap, x_res)
        x3 = x_ap.rearrange('p (b k) -> p b k', k=16)
        S_.op('vector', lambda e, a=bm[:, 0:nb], b=x3: e.tensor_reduce(
            out=a, in_=b, axis=AXX, op=ALU.max, apply_absolute_value=True),
            reads=[x_res], writes=[bm_r], tag=tag + '.bm')
        S_.op('vector', lambda e, a=sc[:, 0:nb], b=bm[:, 0:nb]: e.tensor_scalar(
            out=a, in0=b, scalar1=float(np.float32(1.0 / 6.0)), scalar2=1e-12,
            op0=ALU.mult, op1=ALU.max),
            reads=[bm_r], writes=[sc_r], tag=tag + '.sc')
        S_.op('vector', lambda e, a=rsc[:, 0:nb], b=sc[:, 0:nb]: e.reciprocal(
            out=a, in_=b), reads=[sc_r], writes=[rsc_r], tag=tag + '.rsc')
        rb = rsc[:, 0:nb].unsqueeze(2).broadcast_to([128, nb, 16])
        S_.op(xn_engine, lambda e, a=xn[:, 0:n].rearrange('p (b k) -> p b k', k=16),
              b=x3, c=rb: e.tensor_tensor(out=a, in0=b, in1=c, op=ALU.mult),
              reads=[x_res, rsc_r], writes=[xn_r], tag=tag + '.xn')
        S_.op('vector', lambda e, a=y[:, 0:n], b=xn[:, 0:n]: e._custom_dve(
            OP_FP4RND, out=a, in0=b, s0=4.0, s1=16.0, imm2=HMAGIC),
            reads=[xn_r], writes=[y_r], tag=tag + '.opA')
        S_.op('vector', lambda e, a=z[:, 0:n], b=y[:, 0:n], c=xn[:, 0:n]:
              e._custom_dve(OP_FP4SUB, out=a, in0=b, in1=c, s0=4.0, s1=16.0,
                            imm2=HMAGIC),
              reads=[y_r, xn_r], writes=[z_r], tag=tag + '.opB')
        sbb = sc[:, 0:nb].unsqueeze(2).broadcast_to([128, nb, 16])
        S_.op(deq_engine, lambda e,
              a=deq_out.rearrange('p (b k) -> p b k', k=16),
              b=z[:, 0:n].rearrange('p (b k) -> p b k', k=16), c=sbb:
              e.tensor_tensor(out=a, in0=b, in1=c, op=ALU.mult),
              reads=[z_r, sc_r], writes=[deq_res], tag=tag + '.deq')

    def ln(x_ap, x_res, trash, trash_r, y_ap, y_res, tag):
        st = stats
        S_.op('vector', lambda e, a=st[:, 0:1], b=x_ap: e.reduce_sum(
            out=a, in_=b, axis=AXX), reads=[x_res], writes=[r_stats],
            tag=tag + '.sum')
        S_.op('scalar', lambda e, a=trash, b=x_ap, c=st[:, 1:2]: e.activation(
            out=a, in_=b, func=AF.Square, accum_out=c),
            reads=[x_res], writes=[trash_r, r_stats], tag=tag + '.sq')
        S_.op('vector', lambda e, a=st[:, 2:3], b=st[:, 0:1]: e.tensor_scalar(
            out=a, in0=b, scalar1=INV_H, scalar2=None, op0=ALU.mult),
            reads=[r_stats], writes=[r_stats], tag=tag + '.m')
        S_.op('vector', lambda e, a=st[:, 3:4], b=st[:, 1:2]: e.tensor_scalar(
            out=a, in0=b, scalar1=INV_H, scalar2=None, op0=ALU.mult),
            reads=[r_stats], writes=[r_stats], tag=tag + '.ex2')
        S_.op('vector', lambda e, a=st[:, 4:5], b=st[:, 2:3]: e.tensor_tensor(
            out=a, in0=b, in1=b, op=ALU.mult),
            reads=[r_stats], writes=[r_stats], tag=tag + '.m2')
        S_.op('vector', lambda e, a=st[:, 5:6], b=st[:, 3:4], c=st[:, 4:5]:
              e.tensor_tensor(out=a, in0=b, in1=c, op=ALU.subtract),
              reads=[r_stats], writes=[r_stats], tag=tag + '.var')
        S_.op('vector', lambda e, a=st[:, 6:7], b=st[:, 5:6]: e.tensor_scalar(
            out=a, in0=b, scalar1=EPS, scalar2=None, op0=ALU.add),
            reads=[r_stats], writes=[r_stats], tag=tag + '.ve')
        S_.op('vector', lambda e, a=st[:, 7:8], b=st[:, 6:7]: e.reciprocal(
            out=a, in_=b), reads=[r_stats], writes=[r_stats], tag=tag + '.r')
        S_.op('scalar', lambda e, a=st[:, 8:9], b=st[:, 7:8]: e.activation(
            out=a, in_=b, func=AF.Sqrt),
            reads=[r_stats], writes=[r_stats], tag=tag + '.s0')
        S_.op('vector', lambda e, a=st[:, 9:10], b=st[:, 8:9]: e.tensor_tensor(
            out=a, in0=b, in1=b, op=ALU.mult),
            reads=[r_stats], writes=[r_stats], tag=tag + '.nsq')
        S_.op('vector', lambda e, a=st[:, 10:11], b=st[:, 9:10], c=st[:, 6:7]:
              e.tensor_tensor(out=a, in0=b, in1=c, op=ALU.mult),
              reads=[r_stats], writes=[r_stats], tag=tag + '.nt2')
        S_.op('vector', lambda e, a=st[:, 10:11], b=st[:, 10:11]: e.tensor_scalar(
            out=a, in0=b, scalar1=-0.5, scalar2=1.5, op0=ALU.mult, op1=ALU.add),
            reads=[r_stats], writes=[r_stats], tag=tag + '.nt3')
        S_.op('vector', lambda e, a=st[:, 11:12], b=st[:, 8:9], c=st[:, 10:11]:
              e.tensor_tensor(out=a, in0=b, in1=c, op=ALU.mult),
              reads=[r_stats], writes=[r_stats], tag=tag + '.rstd')
        S_.op('vector', lambda e, a=y_ap, b=x_ap, c=st[:, 2:3], d=st[:, 11:12]:
              e.tensor_scalar(out=a, in0=b, scalar1=c, scalar2=d,
                              op0=ALU.subtract, op1=ALU.mult),
              reads=[x_res, r_stats], writes=[y_res], tag=tag + '.norm')

    def prep_wT(d_w, dst, r_dst, stg, nm, banks=(1, 2)):
        dst3 = dst.rearrange('p (a b) -> p a b', b=H)
        for r in range(KC):
            wst, r_wst = stg[r % 2]
            S_.dma('sync', lambda e, a=wst, b=d_w[r * 128:(r + 1) * 128, :]:
                   e.dma_start(out=a, in_=b), writes=[r_wst], tag=f'{nm}.ld{r}')
            for half in range(2):
                bank, r_bank = ps[banks[half]]
                for j4 in range(4):
                    j = half * 4 + j4
                    S_.op('tensor', lambda e, a=bank[:, j4 * 128:(j4 + 1) * 128],
                          b=wst[:, j * 128:(j + 1) * 128]: e.transpose(a, b, idf),
                          reads=[r_wst, r_idf], writes=[r_bank],
                          tag=f'{nm}.tr{r}.{j}')
                dv = dst3[:, half * 4:half * 4 + 4, r * 128:r * 128 + 128]
                S_.op('scalar', lambda e, a=dv, b=bank[:, 0:512].rearrange(
                    'p (a b) -> p a b', b=128): e.copy(out=a, in_=b),
                    reads=[r_bank], writes=[r_dst], tag=f'{nm}.ev{r}.{half}')

    S_.op('gpsimd', lambda e: e.memset(idf, 0.0), writes=[r_idf], tag='p0.i0')
    S_.op('gpsimd', lambda e: e.affine_select(
        out=idf, in_=idf, compare_op=ALU.not_equal, fill=1.0, base=0,
        pattern=[[-1, 128]], channel_multiplier=1),
        reads=[r_idf], writes=[r_idf], tag='p0.i1')
    S_.op('gpsimd', lambda e: e.memset(idb, 0.0), writes=[r_idb], tag='p0.i2')
    S_.op('gpsimd', lambda e: e.affine_select(
        out=idb, in_=idb, compare_op=ALU.not_equal, fill=1.0, base=0,
        pattern=[[-1, 128]], channel_multiplier=1),
        reads=[r_idb], writes=[r_idb], tag='p0.i3')

    modstage, r_modstage = sb('modstage', [1, 6 * H], F32)
    csb, r_csb = sb('csb', [128, 8], F32)
    csbf, r_csbf = sb('csbf', [128, 8], BF16)
    wmst = [sb(f'wmst{i}', [128, 512], F32) for i in range(2)]
    wmbf = [sb(f'wmbf{i}', [128, 512], BF16) for i in range(2)]
    t1k = [sb(f't1k{i}', [1, H], F32) for i in range(3)]

    for kc in range(KC):
        S_.dma('sync', lambda e, a=csb[:, kc:kc + 1],
               b=d_c[0:1, kc * 128:(kc + 1) * 128].transpose([1, 0]):
               e.dma_start(out=a, in_=b), writes=[r_csb], tag=f'p0.c{kc}')
    S_.op('scalar', lambda e: e.activation(out=csbf, in_=csb, func=AF.Silu),
          reads=[r_csb], writes=[r_csbf], tag='p0.sc')
    pm, r_pm = ps[0]
    for sl in range(12):
        for kc in range(KC):
            i = (sl * KC + kc) % 2
            wst, r_wst = wmst[i]
            wbf, r_wbf = wmbf[i]
            S_.dma('sync', lambda e, a=wst,
                   b=d_wm[kc * 128:(kc + 1) * 128, sl * 512:(sl + 1) * 512]:
                   e.dma_start(out=a, in_=b), writes=[r_wst],
                   tag=f'p0.wl{sl}.{kc}')
            S_.op('scalar', lambda e, a=wbf, b=wst: e.copy(out=a, in_=b),
                  reads=[r_wst], writes=[r_wbf], tag=f'p0.wc{sl}.{kc}')
            S_.op('tensor', lambda e, a=pm[0:1, :], b=csbf[:, kc:kc + 1], c=wbf,
                  st=(kc == 0), sp=(kc == KC - 1):
                  e.matmul(a, b, c, start=st, stop=sp),
                  reads=[r_csbf, r_wbf], writes=[r_pm], tag=f'p0.wm{sl}.{kc}')
        S_.op('vector', lambda e, a=modstage[0:1, sl * 512:(sl + 1) * 512],
              b=pm[0:1, :]: e.tensor_copy(out=a, in_=b),
              reads=[r_pm], writes=[r_modstage], tag=f'p0.we{sl}')

    tv, r_tv = t1k[0]
    ta, r_ta = t1k[1]
    tb, r_tb = t1k[2]

    def derive(ji, jo, kind, bc_col):
        src = modstage[0:1, ji * H:(ji + 1) * H]
        if kind == 'copy':
            S_.op('vector', lambda e, a=tv, b=src: e.tensor_copy(out=a, in_=b),
                  reads=[r_modstage], writes=[r_tv], tag=f'p0.d{jo}a')
        elif kind == 'plus1':
            S_.op('vector', lambda e, a=tv, b=src: e.tensor_scalar(
                out=a, in0=b, scalar1=1.0, scalar2=None, op0=ALU.add),
                reads=[r_modstage], writes=[r_tv], tag=f'p0.d{jo}a')
        else:
            S_.op('vector', lambda e, a=ta, b=src: e.tensor_scalar(
                out=a, in0=b, scalar1=0.0, scalar2=None, op0=ALU.is_ge),
                reads=[r_modstage], writes=[r_ta], tag=f'p0.d{jo}b')
            S_.op('vector', lambda e, a=ta: e.tensor_scalar(
                out=a, in0=a, scalar1=2.0, scalar2=-1.0, op0=ALU.mult,
                op1=ALU.add), reads=[r_ta], writes=[r_ta], tag=f'p0.d{jo}c')
            S_.op('vector', lambda e, a=tb, b=src: e.tensor_scalar(
                out=a, in0=b, scalar1=-1.0, scalar2=None, op0=ALU.mult),
                reads=[r_modstage], writes=[r_tb], tag=f'p0.d{jo}d')
            S_.op('vector', lambda e, a=tb, b=src: e.tensor_tensor(
                out=a, in0=b, in1=a, op=ALU.max),
                reads=[r_modstage, r_tb], writes=[r_tb], tag=f'p0.d{jo}e')
            S_.op('vector', lambda e, a=tb: e.tensor_scalar(
                out=a, in0=a, scalar1=0.25, scalar2=None, op0=ALU.max),
                reads=[r_tb], writes=[r_tb], tag=f'p0.d{jo}f')
            S_.op('vector', lambda e, a=tv, b=ta, c=tb: e.tensor_tensor(
                out=a, in0=b, in1=c, op=ALU.mult),
                reads=[r_ta, r_tb], writes=[r_tv], tag=f'p0.d{jo}g')
        S_.dma('scalar', lambda e, a=d_mod[jo:jo + 1, :], b=tv:
               e.dma_start(out=a, in_=b), reads=[r_tv], writes=[r_mod_d],
               tag=f'p0.d{jo}s')
        if bc_col is not None:
            dap, dres = modBC_parts[bc_col]
            bcast_row(tv, r_tv, dap, dres, 1, f'p0.d{jo}bc')

    derive(1, 0, 'plus1', 0)
    derive(0, 1, 'copy', 1)
    derive(2, 2, 'gate', 2)
    derive(4, 3, 'plus1', None)
    derive(3, 4, 'copy', None)
    derive(5, 5, 'gate', None)

    al.free((modstage, r_modstage)); al.free((csb, r_csb))
    al.free((csbf, r_csbf))
    al.free(wmst[0]); al.free(wmst[1]); al.free(wmbf[0]); al.free(wmbf[1])
    al.free(t1k[0]); al.free(t1k[1]); al.free(t1k[2])

    wkt, r_wkt = sb('WkT', [128, KC * H], BF16)
    wvt, r_wvt = sb('WvT', [128, KC * H], BF16)
    wstg = [sb(f'wstg{i}', [128, H], F32) for i in range(2)]
    prep_wT(d_wk, wkt, r_wkt, wstg, 'p0.wk')
    prep_wT(d_wv, wvt, r_wvt, wstg, 'p0.wv')
    al.free(wstg[0]); al.free(wstg[1])

    if upto == 'P0':
        with nc.Block() as block:
            S_.finalize(nc, block)
        mybir.codegen_inst_isa_subclasses(nc)
        _BUILD_CACHE[key] = nc
        return nc

    xb = [sb(f'xb{i}', [128, H], F32) for i in range(4)]
    hb, r_hb = sb('hb', [128, H], F32)
    yb, r_yb = sb('yb', [128, H], F32)
    sqtr, r_sqtr = sb('sqtr', [128, H], F32)
    statsB, r_statsB = sb('statsB', [128, 64], F32)
    hfmg, r_hfmg = sb('hfmg', [128, KC * 512], BF16)
    hfmg3 = hfmg.rearrange('p (a b) -> p a b', b=512)

    def b_stats4(t0):
        st = stats
        for j in range(4):
            t = t0 + j
            xt, r_xt = xb[t % 4]
            S_.dma('sync', lambda e, a=xt, b=d_x[t * 128:(t + 1) * 128, :]:
                   e.dma_start(out=a, in_=b), writes=[r_xt], tag=f'B.x{t}')
            S_.op('vector', lambda e, a=st[:, j:j + 1], b=xt: e.reduce_sum(
                out=a, in_=b, axis=AXX), reads=[r_xt], writes=[r_stats],
                tag=f'B.ss{t}')
            S_.op('scalar', lambda e, a=sqtr, b=xt, c=st[:, 4 + j:5 + j]:
                  e.activation(out=a, in_=b, func=AF.Square, accum_out=c),
                  reads=[r_xt], writes=[r_sqtr, r_stats], tag=f'B.sq{t}')
        sB = statsB[:, 2 * (t0 % 16):2 * (t0 % 16) + 8].rearrange(
            'p (a b) -> p a b', b=2)
        S_.op('vector', lambda e, a=sB[:, :, 0:1].squeeze(2), b=st[:, 0:4]:
              e.tensor_scalar(out=a, in0=b, scalar1=INV_H, scalar2=None,
                              op0=ALU.mult),
              reads=[r_stats], writes=[r_statsB], tag=f'B.sm{t0}')
        S_.op('vector', lambda e, a=st[:, 8:12], b=st[:, 4:8]: e.tensor_scalar(
            out=a, in0=b, scalar1=INV_H, scalar2=None, op0=ALU.mult),
            reads=[r_stats], writes=[r_stats], tag=f'B.se{t0}')
        S_.op('vector', lambda e, a=st[:, 12:16], b=sB[:, :, 0:1].squeeze(2):
              e.tensor_tensor(out=a, in0=b, in1=b, op=ALU.mult),
              reads=[r_statsB], writes=[r_stats], tag=f'B.sm2{t0}')
        S_.op('vector', lambda e, a=st[:, 16:20], b=st[:, 8:12], c=st[:, 12:16]:
              e.tensor_tensor(out=a, in0=b, in1=c, op=ALU.subtract),
              reads=[r_stats], writes=[r_stats], tag=f'B.sv{t0}')
        S_.op('vector', lambda e, a=st[:, 16:20], b=st[:, 16:20]: e.tensor_scalar(
            out=a, in0=b, scalar1=EPS, scalar2=None, op0=ALU.add),
            reads=[r_stats], writes=[r_stats], tag=f'B.sve{t0}')
        S_.op('vector', lambda e, a=st[:, 20:24], b=st[:, 16:20]: e.reciprocal(
            out=a, in_=b), reads=[r_stats], writes=[r_stats], tag=f'B.sr{t0}')
        S_.op('scalar', lambda e, a=st[:, 24:28], b=st[:, 20:24]: e.activation(
            out=a, in_=b, func=AF.Sqrt),
            reads=[r_stats], writes=[r_stats], tag=f'B.ssq{t0}')
        S_.op('vector', lambda e, a=st[:, 8:12], b=st[:, 24:28]: e.tensor_tensor(
            out=a, in0=b, in1=b, op=ALU.mult),
            reads=[r_stats], writes=[r_stats], tag=f'B.sn1{t0}')
        S_.op('vector', lambda e, a=st[:, 12:16], b=st[:, 8:12], c=st[:, 16:20]:
              e.tensor_tensor(out=a, in0=b, in1=c, op=ALU.mult),
              reads=[r_stats], writes=[r_stats], tag=f'B.sn2{t0}')
        S_.op('vector', lambda e, a=st[:, 12:16], b=st[:, 12:16]: e.tensor_scalar(
            out=a, in0=b, scalar1=-0.5, scalar2=1.5, op0=ALU.mult, op1=ALU.add),
            reads=[r_stats], writes=[r_stats], tag=f'B.sn3{t0}')
        S_.op('vector', lambda e, a=sB[:, :, 1:2].squeeze(2), b=st[:, 24:28],
              c=st[:, 12:16]: e.tensor_tensor(out=a, in0=b, in1=c, op=ALU.mult),
              reads=[r_stats], writes=[r_statsB], tag=f'B.srstd{t0}')

    def b_head(t):
        xt, r_xt = xb[t % 4]
        col = 2 * (t % 16)
        S_.op('vector', lambda e, a=yb, b=xt, c=statsB[:, col:col + 1],
              d=statsB[:, col + 1:col + 2]: e.tensor_scalar(
              out=a, in0=b, scalar1=c, scalar2=d, op0=ALU.subtract,
              op1=ALU.mult),
              reads=[r_xt, r_statsB], writes=[r_yb], tag=f'B.nrm{t}')
        S_.op('vector', lambda e, a=hb, b=yb, c=ca1f:
              e.tensor_tensor(out=a, in0=b, in1=c, op=ALU.mult),
              reads=[r_yb, r_ca1f], writes=[r_hb], tag=f'B.h1{t}')
        S_.op('vector', lambda e, a=xt, b=hb, c=saf:
              e.tensor_tensor(out=a, in0=b, in1=c, op=ALU.add),
              reads=[r_hb, r_saf], writes=[r_xt], tag=f'B.h2{t}')

    def b_tail(t):
        tl = t % 4
        xt, r_xt = xb[t % 4]
        for half in range(2):
            bank, r_bank = ps[2 + half]
            for j4 in range(4):
                j = half * 4 + j4
                S_.op('tensor', lambda e, a=bank[:, j4 * 128:(j4 + 1) * 128],
                      b=xt[:, j * 128:(j + 1) * 128]: e.transpose(a, b, idf),
                      reads=[r_xt, r_idf], writes=[r_bank],
                      tag=f'B.tr{t}.{j}')
            dv = hfmg3[:, half * 4:half * 4 + 4, tl * 128:tl * 128 + 128]
            S_.op('scalar', lambda e, a=dv, b=bank[:, 0:512].rearrange(
                'p (a b) -> p a b', b=128): e.copy(out=a, in_=b),
                reads=[r_bank], writes=[r_hfmg], tag=f'B.he{t}.{half}')

    for g in range(8):
        b_stats4(g * 4)
        for tl in range(4):
            t = g * 4 + tl
            b_head(t)
            if tl > 0:
                b_tail(t - 1)
        b_tail(g * 4 + 3)
        for oc in range(KC):
            bank, r_bank = ps[5 + (oc % 2)]
            for kc in range(KC):
                S_.op('tensor', lambda e, a=bank,
                      b=wkt[:, kc * H + oc * 128:kc * H + oc * 128 + 128],
                      c=hfmg[:, kc * 512:(kc + 1) * 512],
                      st=(kc == 0), sp=(kc == KC - 1):
                      e.matmul(a, b, c, start=st, stop=sp),
                      reads=[r_wkt, r_hfmg], writes=[r_bank],
                      tag=f'B.km{g}.{oc}.{kc}')
            S_.op('scalar', lambda e,
                  a=kfm[:, oc * S + g * 512:oc * S + (g + 1) * 512], b=bank:
                  e.copy(out=a, in_=b),
                  reads=[r_bank], writes=[r_kfm], tag=f'B.ke{g}.{oc}')
        for tl in range(4):
            t = g * 4 + tl
            for hs in range(2):
                bank, r_bank = ps[5 + ((tl * 2 + hs) % 2)]
                for kc in range(KC):
                    S_.op('tensor', lambda e, a=bank,
                          b=hfmg[:, kc * 512 + tl * 128:kc * 512 + tl * 128 + 128],
                          c=wvt[:, kc * H + hs * 512:kc * H + (hs + 1) * 512],
                          st=(kc == 0), sp=(kc == KC - 1):
                          e.matmul(a, b, c, start=st, stop=sp),
                          reads=[r_wvt, r_hfmg], writes=[r_bank],
                          tag=f'B.vm{t}.{hs}.{kc}')
                S_.op('scalar', lambda e,
                      a=vtm[:, t * H + hs * 512:t * H + (hs + 1) * 512], b=bank:
                      e.copy(out=a, in_=b),
                      reads=[r_bank], writes=[r_vtm], tag=f'B.ve{t}.{hs}')
        ho = d_hfm[:, :, g * 512:(g + 1) * 512].transpose([1, 0, 2])
        S_.dma('scalar', lambda e, a=ho, b=hfmg3: e.dma_start(out=a, in_=b),
               reads=[r_hfmg], writes=[r_hfm[g]], tag=f'B.hs{g}')

    al.free((wkt, r_wkt)); al.free((wvt, r_wvt)); al.free((hfmg, r_hfmg))
    for i in range(4):
        al.free(xb[i])
    al.free((hb, r_hb)); al.free((yb, r_yb)); al.free((sqtr, r_sqtr))
    al.free((statsB, r_statsB))
    al.free((ca1f, r_ca1f)); al.free((saf, r_saf))

    if upto == 'B':
        with nc.Block() as block:
            S_.finalize(nc, block)
        mybir.codegen_inst_isa_subclasses(nc)
        _BUILD_CACHE[key] = nc
        return nc

    wqt, r_wqt = sb('WqT', [128, KC * H], BF16)
    wstg2 = [sb(f'wstg2_{i}', [128, H], F32) for i in range(2)]
    prep_wT(d_wq, wqt, r_wqt, wstg2, 'C.wq', banks=(0, 1))
    al.free(wstg2[0]); al.free(wstg2[1])
    msk, r_msk = sb('masks', [128, 4 * 512], BF16)
    for r in range(4):
        slm = msk[:, r * 512:(r + 1) * 512]
        S_.op('gpsimd', lambda e, a=slm: e.memset(a, 1.0), writes=[r_msk],
              tag=f'p0.m0{r}')
        S_.op('gpsimd', lambda e, a=slm, base=128 * r: e.affine_select(
            out=a, in_=a, compare_op=ALU.is_ge, fill=0.0, base=base,
            pattern=[[-1, 512]], channel_multiplier=1),
            reads=[r_msk], writes=[r_msk], tag=f'p0.m1{r}')
    hfmc, r_hfmc = sb('hfmc', [128, KC * 512], BF16)
    qg, r_qg = sb('qg', [128, KC * 512], BF16)
    pbuf2 = [sb(f'pbuf{i}', [128, S - 512], BF16) for i in range(2)]
    dscr, r_dscr = sb('dscr', [128, 512], BF16)
    pmb2 = [sb(f'pmb{i}', [128, 512], BF16) for i in range(2)]
    ptb = [sb(f'ptb{i}', [128, 512], BF16) for i in range(2)]
    cxb = [sb(f'cxb{i}', [128, H], F32) for i in range(2)]
    x2b2 = [sb('x2b0', [128, H], F32)] * 2
    ebuf, r_ebuf = sb('ebuf', [128, 512], F32)
    ebufB, r_ebufB = sb('ebufB', [128, 512], F32)
    obuf, r_obuf = sb('obuf', [128, H], BF16)
    stc2 = [sb(f'stc{i}', [128, 32], F32) for i in range(2)]

    def c_rounds(qc):
        full = list(range(qc))
        if len(full) > 4:
            return [full[:4], full[4:] + [qc]]
        if full:
            return [full, [qc]]
        return [[qc]]

    def c_part1(qi):
        par = qi % 2
        st, r_st = stc2[par]
        pbuf, r_pbuf = pbuf2[par]
        pmb, r_pmb = pmb2[par]
        xt, r_xt = cxb[par]
        qc = qi // 4
        rr = qi % 4
        S_.dma('sync', lambda e, a=xt, b=d_x[qi * 128:(qi + 1) * 128, :]:
               e.dma_start(out=a, in_=b), writes=[r_xt], tag=f'C.x{qi}')
        rounds = c_rounds(qc)
        nrounds = len(rounds)
        for ri, cs in enumerate(rounds):
            for k_, ci in enumerate(cs):
                bank, r_bank = ps[k_]
                for hc in range(KC):
                    S_.op('tensor', lambda e, a=bank,
                          b=qg[:, hc * 512 + rr * 128:hc * 512 + rr * 128 + 128],
                          c=kfm[:, hc * S + ci * 512:hc * S + (ci + 1) * 512],
                          st=(hc == 0), sp=(hc == KC - 1):
                          e.matmul(a, b, c, start=st, stop=sp),
                          reads=[r_qg, r_kfm], writes=[r_bank],
                          tag=f'C.sm{qi}.{ci}.{hc}')
            for k_, ci in enumerate(cs):
                bank, r_bank = ps[k_]
                S_.op('vector', lambda e, a=st[:, 16 + ci:17 + ci], b=bank:
                      e.reduce_max(out=a, in_=b, axis=AXX),
                      reads=[r_bank], writes=[r_st], tag=f'C.mx{qi}.{ci}')
            c0, c1 = cs[0], cs[-1]
            S_.op('vector', lambda e, a=st[:, 12 + ri:13 + ri],
                  b=st[:, 16 + c0:17 + c1]: e.reduce_max(
                  out=a, in_=b, axis=AXX),
                  reads=[r_st], writes=[r_st], tag=f'C.rm{qi}.{ri}')
            S_.op('vector', lambda e, a=st[:, 14 + ri:15 + ri],
                  b=st[:, 12 + ri:13 + ri]: e.tensor_scalar(
                  out=a, in0=b, scalar1=-SMSCALE, scalar2=None, op0=ALU.mult),
                  reads=[r_st], writes=[r_st], tag=f'C.nb{qi}.{ri}')
            for k_, ci in enumerate(cs):
                bank, r_bank = ps[k_]
                dst = dscr if ci == qc else pbuf[:, ci * 512:(ci + 1) * 512]
                r_dst = r_dscr if ci == qc else r_pbuf
                S_.op('scalar', lambda e,
                      a=dst, b=bank,
                      c=st[:, 14 + ri:15 + ri], d=st[:, 24 + ci:25 + ci]:
                      e.activation(out=a, in_=b, func=AF.Exp, scale=SMSCALE,
                                   bias=c, accum_out=d),
                      reads=[r_bank, r_st], writes=[r_dst, r_st],
                      tag=f'C.ex{qi}.{ci}')
        if nrounds == 2:
            S_.op('vector', lambda e, a=st[:, 11:12], b=st[:, 12:13],
                  c=st[:, 13:14]: e.tensor_tensor(
                  out=a, in0=b, in1=c, op=ALU.max),
                  reads=[r_st], writes=[r_st], tag=f'C.gm{qi}')
            S_.op('vector', lambda e, a=st[:, 18:20], b=st[:, 12:14],
                  c=st[:, 11:12]: e.tensor_scalar(
                  out=a, in0=b, scalar1=c, scalar2=SMSCALE,
                  op0=ALU.subtract, op1=ALU.mult),
                  reads=[r_st], writes=[r_st], tag=f'C.fd{qi}')
            S_.op('scalar', lambda e, a=st[:, 20:22], b=st[:, 18:20]:
                  e.activation(out=a, in_=b, func=AF.Exp),
                  reads=[r_st], writes=[r_st], tag=f'C.fe{qi}')
        S_.op('vector', lambda e, a=pmb, b=dscr,
              c=msk[:, (qi % 4) * 512:(qi % 4 + 1) * 512]:
              e.tensor_tensor(out=a, in0=b, in1=c, op=ALU.mult),
              reads=[r_dscr, r_msk], writes=[r_pmb], tag=f'C.pm{qi}')
        S_.op('vector', lambda e, a=st[:, 24 + qc:25 + qc], b=pmb:
              e.reduce_sum(out=a, in_=b, axis=AXX),
              reads=[r_pmb], writes=[r_st], tag=f'C.ls{qi}')
        if nrounds == 2:
            na = len(rounds[0])
            S_.op('vector', lambda e, a=st[:, 22:23],
                  b=st[:, 24:24 + na]: e.reduce_sum(out=a, in_=b, axis=AXX),
                  reads=[r_st], writes=[r_st], tag=f'C.la{qi}')
            S_.op('vector', lambda e, a=st[:, 23:24],
                  b=st[:, 24 + na:25 + qc]: e.reduce_sum(
                  out=a, in_=b, axis=AXX),
                  reads=[r_st], writes=[r_st], tag=f'C.lb{qi}')
            S_.op('vector', lambda e, a=st[:, 22:24], b=st[:, 22:24],
                  c=st[:, 20:22]: e.tensor_tensor(
                  out=a, in0=b, in1=c, op=ALU.mult),
                  reads=[r_st], writes=[r_st], tag=f'C.lw{qi}')
            S_.op('vector', lambda e, a=st[:, 15:16], b=st[:, 22:23],
                  c=st[:, 23:24]: e.tensor_tensor(
                  out=a, in0=b, in1=c, op=ALU.add),
                  reads=[r_st], writes=[r_st], tag=f'C.l{qi}')
        else:
            S_.op('vector', lambda e, a=st[:, 15:16], b=st[:, 17:18],
                  c=st[:, 24 + qc:25 + qc]: e.tensor_copy(out=a, in_=c),
                  reads=[r_st], writes=[r_st], tag=f'C.l{qi}')
        S_.op('vector', lambda e, a=st[:, 10:11], b=st[:, 15:16]:
              e.reciprocal(out=a, in_=b),
              reads=[r_st], writes=[r_st], tag=f'C.rl{qi}')

    def c_part2(qi):
        par = qi % 2
        st, r_st = stc2[par]
        pbuf, r_pbuf = pbuf2[par]
        pmb, r_pmb = pmb2[par]
        xt, r_xt = cxb[par]
        x2b, r_x2b = x2b2[par]
        qc = qi // 4
        rounds = c_rounds(qc)
        nrounds = len(rounds)
        trbanks = [ps[7], ps[4]]
        ob = [ps[5], ps[6]]
        rbound = []
        if nrounds == 2:
            na = len(rounds[0])
            rbound.append((0, 4 * na - 1, 0))
            rbound.append((4 * na, qi, 1))
        else:
            rbound.append((0, qi, 0))
        batches = []
        for (j0, j1, ri) in rbound:
            for jq in range(j0, j1 + 1, 4):
                batches.append((jq, min(4, j1 + 1 - jq), j0, j1, ri))

        def emit_tr(bi):
            jq, jn, j0, j1, ri = batches[bi]
            half = (jq // 4) % 2
            trbank_full, r_trb = trbanks[half]
            trbank = trbank_full[:, 0:jn * 128]
            pt, r_pt = ptb[half]
            for j4 in range(jn):
                j = jq + j4
                ci = j // 4
                if ci == qc:
                    src = pmb[:, (j % 4) * 128:(j % 4 + 1) * 128]
                    r_src = r_pmb
                else:
                    src = pbuf[:, j * 128:(j + 1) * 128]
                    r_src = r_pbuf
                S_.op('tensor', lambda e,
                      a=trbank_full[:, j4 * 128:(j4 + 1) * 128],
                      b=src, c=idb: e.transpose(a, b, c),
                      reads=[r_src, r_idb], writes=[r_trb],
                      tag=f'C.pt{qi}.{j}')
            ev_eng = 'scalar' if half == 0 else 'vector'
            ev = ((lambda e, a=pt[:, 0:jn * 128], b=trbank: e.copy(out=a, in_=b))
                  if ev_eng == 'scalar' else
                  (lambda e, a=pt[:, 0:jn * 128], b=trbank: e.tensor_copy(out=a, in_=b)))
            S_.op(ev_eng, ev,
                  reads=[r_trb], writes=[r_pt],
                  tag=f'C.pe{qi}.{jq}')

        def emit_av(bi):
            jq, jn, j0, j1, ri = batches[bi]
            half = (jq // 4) % 2
            pt, r_pt = ptb[half]
            for j4 in range(jn):
                j = jq + j4
                for hs in range(2):
                    bank, r_bank = ob[hs]
                    S_.op('tensor', lambda e, a=bank,
                          b=pt[:, j4 * 128:(j4 + 1) * 128],
                          c=vtm[:, j * H + hs * 512:j * H + (hs + 1) * 512],
                          st=(j == j0), sp=(j == j1):
                          e.matmul(a, b, c, start=st, stop=sp),
                          reads=[r_pt, r_vtm], writes=[r_bank],
                          tag=f'C.av{qi}.{j}.{hs}')
            if nrounds == 2 and ri == 0 and jq + jn == j1 + 1:
                for hs in range(2):
                    bank, r_bank = ob[hs]
                    S_.op('scalar', lambda e,
                          a=obuf[:, hs * 512:(hs + 1) * 512], b=bank,
                          c=st[:, 20:21]: e.activation(
                          out=a, in_=b, func=AF.Copy, scale=c),
                          reads=[r_bank, r_st], writes=[r_obuf],
                          tag=f'C.oe{qi}.{hs}')

        for s in range(len(batches) + 1):
            if s < len(batches):
                emit_tr(s)
            if s >= 1:
                emit_av(s - 1)
        for hs in range(2):
            bank, r_bank = ob[hs]
            if nrounds == 2:
                S_.op('vector', lambda e, a=ebuf, b=bank, c=st[:, 21:22],
                      d=obuf[:, hs * 512:(hs + 1) * 512]:
                      e.scalar_tensor_tensor(out=a, in0=b, scalar=c, in1=d,
                                             op0=ALU.mult, op1=ALU.add),
                      reads=[r_bank, r_st, r_obuf], writes=[r_ebuf],
                      tag=f'C.e0{qi}.{hs}')
                S_.op('vector', lambda e, a=ebufB, b=ebuf, c=st[:, 10:11],
                      d=gaf[:, hs * 512:(hs + 1) * 512]:
                      e.scalar_tensor_tensor(out=a, in0=b, scalar=c, in1=d,
                                             op0=ALU.mult, op1=ALU.mult),
                      reads=[r_ebuf, r_st, r_gaf], writes=[r_ebufB],
                      tag=f'C.e1{qi}.{hs}')
            else:
                S_.op('vector', lambda e, a=ebufB, b=bank, c=st[:, 10:11],
                      d=gaf[:, hs * 512:(hs + 1) * 512]:
                      e.scalar_tensor_tensor(out=a, in0=b, scalar=c, in1=d,
                                             op0=ALU.mult, op1=ALU.mult),
                      reads=[r_bank, r_st, r_gaf], writes=[r_ebufB],
                      tag=f'C.e1{qi}.{hs}')
            S_.op('vector', lambda e, a=x2b[:, hs * 512:(hs + 1) * 512],
                  b=ebufB, c=xt[:, hs * 512:(hs + 1) * 512]:
                  e.tensor_tensor(out=a, in0=b, in1=c, op=ALU.add),
                  reads=[r_ebufB, r_xt], writes=[r_x2b], tag=f'C.e2{qi}.{hs}')
        S_.dma('scalar', lambda e, a=d_x2[qi * 128:(qi + 1) * 128, :], b=x2b:
               e.dma_start(out=a, in_=b),
               reads=[r_x2b], writes=[r_x2[qi]], tag=f'C.xs{qi}')

    def load_hfmc(g):
        for qd, qu in enumerate(('sync', 'scalar', 'sync', 'scalar')):
            kc0 = qd * (KC // 4)
            hi = d_hfm[kc0:kc0 + KC // 4, :,
                       g * 512:(g + 1) * 512].transpose([1, 0, 2])
            S_.dma(qu, lambda e, a=hfmc[:, kc0 * 512:(kc0 + KC // 4) * 512]
                   .rearrange('p (a b) -> p a b', b=512),
                   b=hi: e.dma_start(out=a, in_=b),
                   reads=[r_hfm[g]], writes=[r_hfmc], tag=f'C.hl{g}.{qd}')

    load_hfmc(0)
    for g in range(8):
        for oc in range(KC):
            bank, r_bank = ps[5 + (oc % 2)]
            for kc in range(KC):
                S_.op('tensor', lambda e, a=bank,
                      b=wqt[:, kc * H + oc * 128:kc * H + oc * 128 + 128],
                      c=hfmc[:, kc * 512:(kc + 1) * 512],
                      st=(kc == 0), sp=(kc == KC - 1):
                      e.matmul(a, b, c, start=st, stop=sp),
                      reads=[r_wqt, r_hfmc], writes=[r_bank],
                      tag=f'C.qm{g}.{oc}.{kc}')
            S_.op('scalar', lambda e, a=qg[:, oc * 512:(oc + 1) * 512], b=bank:
                  e.copy(out=a, in_=b),
                  reads=[r_bank], writes=[r_qg], tag=f'C.qe{g}.{oc}')
        if g + 1 < 8:
            load_hfmc(g + 1)
        for rr in range(4):
            qi = g * 4 + rr
            c_part1(qi)
            if qi > 0:
                c_part2(qi - 1)
    c_part2(NT - 1)

    al.free((wqt, r_wqt))
    al.free((hfmc, r_hfmc)); al.free((qg, r_qg))
    al.free(pbuf2[0]); al.free(pbuf2[1]); al.free((dscr, r_dscr)); al.free(pmb2[0]); al.free(pmb2[1]); al.free(ptb[0]); al.free(ptb[1])
    al.free(cxb[0]); al.free(cxb[1]); al.free(x2b2[0])
    al.free(stc2[0]); al.free(stc2[1])
    al.free((ebuf, r_ebuf)); al.free((ebufB, r_ebufB)); al.free((obuf, r_obuf))
    al.free((kfm, r_kfm)); al.free((vtm, r_vtm)); al.free((gaf, r_gaf))
    al.free((msk, r_msk))

    if upto == 'C':
        with nc.Block() as block:
            S_.finalize(nc, block)
        mybir.codegen_inst_isa_subclasses(nc)
        _BUILD_CACHE[key] = nc
        return nc

    modDa, r_modDa = sb('modDa', [128, 2 * H], BF16)
    modDg, r_modDg = sb('modDg', [128, H], BF16)
    mdt, r_mdt = sb('mdt', [1, H], F32)
    for j in range(3):
        S_.dma('sync', lambda e, a=mdt, b=d_mod[3 + j:4 + j, :]:
               e.dma_start(out=a, in_=b), reads=[r_mod_d], writes=[r_mdt],
               tag=f'D.ml{j}')
        if j < 2:
            bcast_row(mdt, r_mdt, modDa[:, j * H:(j + 1) * H], r_modDa,
                      6, f'D.mb{j}')
        else:
            bcast_row(mdt, r_mdt, modDg, r_modDg, 6, f'D.mb{j}')
    al.free((mdt, r_mdt)); al.free((onesb, r_onesb))

    h2q, r_h2q = sb('h2q', [128, KC * S], BF16)
    r_h2qt = [S_.res(f'h2qt{t}') for t in range(NT)]
    wdt, r_wdt = sb('wdT', [128, IC * H], BF16)
    wdt3 = wdt.rearrange('p (a b) -> p a b', b=H)
    IH = I // 2
    tq = {
        'bm': sb('tq_bm', [128, 64], F32),
        'sc': sb('tq_sc', [128, 64], F32),
        'rsc': sb('tq_rsc', [128, 64], F32),
        'xn': sb('tq_xn', [128, H], F32),
        'z': sb('tq_z', [128, H], F32),
    }
    dxb = [sb(f'dxb{i}', [128, H], F32) for i in range(2)]
    dq1, r_dq1 = sb('dq1', [128, H], F32)
    dtr, r_dtr = sb('dtr', [128, H], F32)
    h2q3 = h2q.rearrange('p (a b) -> p a b', b=S)
    wldst1_2 = [sb(f'wldst1{i}', [128, H], F32) for i in range(2)]

    statsD, r_statsD = sb('statsD', [128, 64], F32)

    def d1_stats4(t0):
        st = stats
        for j in range(4):
            t = t0 + j
            xt, r_xt = dxb[t % 2]
            S_.dma('sync', lambda e, a=xt, b=d_x2[t * 128:(t + 1) * 128, :]:
                   e.dma_start(out=a, in_=b),
                   reads=[r_x2[t]], writes=[r_xt], tag=f'D.sxl{t}')
            S_.op('vector', lambda e, a=st[:, j:j + 1], b=xt: e.reduce_sum(
                out=a, in_=b, axis=AXX), reads=[r_xt], writes=[r_stats],
                tag=f'D.ss{t}')
            S_.op('scalar', lambda e, a=dtr, b=xt, c=st[:, 4 + j:5 + j]:
                  e.activation(out=a, in_=b, func=AF.Square, accum_out=c),
                  reads=[r_xt], writes=[r_dtr, r_stats], tag=f'D.sq{t}')
        sD = statsD[:, 2 * t0:2 * t0 + 8].rearrange('p (a b) -> p a b', b=2)
        S_.op('vector', lambda e, a=sD[:, :, 0:1].squeeze(2), b=st[:, 0:4]:
              e.tensor_scalar(out=a, in0=b, scalar1=INV_H, scalar2=None,
                              op0=ALU.mult),
              reads=[r_stats], writes=[r_statsD], tag=f'D.sm{t0}')
        S_.op('vector', lambda e, a=st[:, 8:12], b=st[:, 4:8]: e.tensor_scalar(
            out=a, in0=b, scalar1=INV_H, scalar2=None, op0=ALU.mult),
            reads=[r_stats], writes=[r_stats], tag=f'D.se{t0}')
        S_.op('vector', lambda e, a=st[:, 12:16], b=sD[:, :, 0:1].squeeze(2):
              e.tensor_tensor(out=a, in0=b, in1=b, op=ALU.mult),
              reads=[r_statsD], writes=[r_stats], tag=f'D.sm2{t0}')
        S_.op('vector', lambda e, a=st[:, 16:20], b=st[:, 8:12], c=st[:, 12:16]:
              e.tensor_tensor(out=a, in0=b, in1=c, op=ALU.subtract),
              reads=[r_stats], writes=[r_stats], tag=f'D.sv{t0}')
        S_.op('vector', lambda e, a=st[:, 16:20], b=st[:, 16:20]: e.tensor_scalar(
            out=a, in0=b, scalar1=EPS, scalar2=None, op0=ALU.add),
            reads=[r_stats], writes=[r_stats], tag=f'D.sve{t0}')
        S_.op('vector', lambda e, a=st[:, 20:24], b=st[:, 16:20]: e.reciprocal(
            out=a, in_=b), reads=[r_stats], writes=[r_stats], tag=f'D.sr{t0}')
        S_.op('scalar', lambda e, a=st[:, 24:28], b=st[:, 20:24]: e.activation(
            out=a, in_=b, func=AF.Sqrt),
            reads=[r_stats], writes=[r_stats], tag=f'D.ssq{t0}')
        S_.op('vector', lambda e, a=st[:, 8:12], b=st[:, 24:28]: e.tensor_tensor(
            out=a, in0=b, in1=b, op=ALU.mult),
            reads=[r_stats], writes=[r_stats], tag=f'D.sn1{t0}')
        S_.op('vector', lambda e, a=st[:, 12:16], b=st[:, 8:12], c=st[:, 16:20]:
              e.tensor_tensor(out=a, in0=b, in1=c, op=ALU.mult),
              reads=[r_stats], writes=[r_stats], tag=f'D.sn2{t0}')
        S_.op('vector', lambda e, a=st[:, 12:16], b=st[:, 12:16]: e.tensor_scalar(
            out=a, in0=b, scalar1=-0.5, scalar2=1.5, op0=ALU.mult, op1=ALU.add),
            reads=[r_stats], writes=[r_stats], tag=f'D.sn3{t0}')
        S_.op('vector', lambda e, a=sD[:, :, 1:2].squeeze(2), b=st[:, 24:28],
              c=st[:, 12:16]: e.tensor_tensor(out=a, in0=b, in1=c, op=ALU.mult),
              reads=[r_stats], writes=[r_statsD], tag=f'D.srstd{t0}')

    def d1_prep(t):
        xt, r_xt = dxb[t % 2]
        S_.dma('sync', lambda e, a=xt, b=d_x2[t * 128:(t + 1) * 128, :]:
               e.dma_start(out=a, in_=b),
               reads=[r_x2[t]], writes=[r_xt], tag=f'D.xl{t}')
        S_.op('vector', lambda e, a=dq1, b=xt, c=statsD[:, 2 * t:2 * t + 1],
              d=statsD[:, 2 * t + 1:2 * t + 2]: e.tensor_scalar(
              out=a, in0=b, scalar1=c, scalar2=d, op0=ALU.subtract,
              op1=ALU.mult),
              reads=[r_xt, r_statsD], writes=[r_dq1], tag=f'D.nrm{t}')
        S_.op('vector', lambda e, a=dtr, b=dq1, c=modDa[:, 0:H]:
              e.tensor_tensor(out=a, in0=b, in1=c, op=ALU.mult),
              reads=[r_dq1, r_modDa], writes=[r_dtr], tag=f'D.h1{t}')
        S_.op('vector', lambda e, a=dq1, b=dtr, c=modDa[:, H:2 * H]:
              e.tensor_tensor(out=a, in0=b, in1=c, op=ALU.add),
              reads=[r_dtr, r_modDa], writes=[r_dq1], tag=f'D.h2{t}')
        quant(dq1, r_dq1, H, tq, dtr, r_dtr, f'D.q{t}')
        for half in range(2):
            bank, r_bank = ps[5 + half]
            for j4 in range(4):
                j = half * 4 + j4
                S_.op('tensor', lambda e, a=bank[:, j4 * 128:(j4 + 1) * 128],
                      b=dtr[:, j * 128:(j + 1) * 128]: e.transpose(a, b, idf),
                      reads=[r_dtr, r_idf], writes=[r_bank], tag=f'D.tr{t}.{j}')
            dv = h2q3[:, half * 4:half * 4 + 4, t * 128:t * 128 + 128]
            S_.op('scalar', lambda e, a=dv, b=bank[:, 0:512].rearrange(
                'p (a b) -> p a b', b=128): e.copy(out=a, in_=b),
                reads=[r_bank], writes=[r_h2qt[t]], tag=f'D.h2e{t}.{half}')

    wqstg, r_wqstg = sb('wqstg', [128, 512], BF16)

    def gu_subjob(d_w, d_q, r_q, c, s_, wb, nm):
        c0, wc = NSL[c]
        row = c0 + s_ * 128
        wldst1, r_wldst1 = wldst1_2[wb]
        S_.dma('sync', lambda e, a=wldst1,
               b=d_w[row:row + 128, :]: e.dma_start(out=a, in_=b),
               writes=[r_wldst1], tag=f'{nm}.l{c}.{s_}')
        quant(wldst1, r_wldst1, H, tq, tq['xn'][0][:, 0:H],
              tq['xn'][1], f'{nm}.q{c}.{s_}')
        dqw, r_dqw = tq['xn']
        for half in range(2):
            bank, r_bank = ps[5 + half]
            for j4 in range(4):
                j = half * 4 + j4
                S_.op('tensor', lambda e, a=bank[:, j4 * 128:(j4 + 1) * 128],
                      b=dqw[:, j * 128:(j + 1) * 128]: e.transpose(a, b, idf),
                      reads=[r_dqw, r_idf], writes=[r_bank],
                      tag=f'{nm}.t{c}.{s_}.{j}')
            S_.op('scalar', lambda e, a=wqstg, b=bank[:, 0:512]:
                  e.copy(out=a, in_=b),
                  reads=[r_bank], writes=[r_wqstg], tag=f'{nm}.e{c}.{s_}.{half}')
            dvo = d_q[half * 4:half * 4 + 4, :, row:row + 128].transpose([1, 0, 2])
            S_.dma('scalar', lambda e, a=dvo,
                   b=wqstg.rearrange('p (a b) -> p a b', b=128):
                   e.dma_start(out=a, in_=b),
                   reads=[r_wqstg], writes=[r_q], tag=f'{nm}.s{c}.{s_}.{half}')

    def d1_head(cp, t):
        ca, cb = 2 * cp, 2 * cp + 1
        wca, wcb = NSL[ca][1], NSL[cb][1]
        wtot = wca + wcb
        par = t % 2
        silb, r_silb = silb2[par]
        hmb, r_hmb = hmb2[par]
        dqh, r_dqh = dqh2[par]
        pg = [ps[0], ps[1]]
        pu = [ps[2], ps[3]]
        for kc in range(KC):
            for ci, wc in ((0, wca), (1, wcb)):
                S_.op('tensor', lambda e, a=pg[ci][0][:, 0:wc],
                      b=h2q[:, kc * S + t * 128:kc * S + t * 128 + 128],
                      c=wgqc2[ci][0][:, kc * 512:kc * 512 + wc],
                      st=(kc == 0), sp=(kc == KC - 1):
                      e.matmul(a, b, c, start=st, stop=sp),
                      reads=[r_h2qt[t], wgqc2[ci][1]], writes=[pg[ci][1]],
                      tag=f'D.gm{ca + ci}.{t}.{kc}')
                S_.op('tensor', lambda e, a=pu[ci][0][:, 0:wc],
                      b=h2q[:, kc * S + t * 128:kc * S + t * 128 + 128],
                      c=wuqc2[ci][0][:, kc * 512:kc * 512 + wc],
                      st=(kc == 0), sp=(kc == KC - 1):
                      e.matmul(a, b, c, start=st, stop=sp),
                      reads=[r_h2qt[t], wuqc2[ci][1]], writes=[pu[ci][1]],
                      tag=f'D.um{ca + ci}.{t}.{kc}')
        for ci, off, wc in ((0, 0, wca), (1, wca, wcb)):
            S_.op('scalar', lambda e, a=silb[:, off:off + wc],
                  b=pg[ci][0][:, 0:wc]: e.activation(out=a, in_=b, func=AF.Silu),
                  reads=[pg[ci][1]], writes=[r_silb], tag=f'D.si{ca + ci}.{t}')
            S_.op('vector', lambda e, a=hmb[:, off:off + wc],
                  b=silb[:, off:off + wc], c=pu[ci][0][:, 0:wc]:
                  e.tensor_tensor(out=a, in0=b, in1=c, op=ALU.mult),
                  reads=[r_silb, pu[ci][1]], writes=[r_hmb],
                  tag=f'D.hm{ca + ci}.{t}')
        quant(hmb[:, 0:wtot], r_hmb, wtot, tqs[par], dqh[:, 0:wtot], r_dqh,
              f'D.qh{cp}.{t}')

    def d1_tail(cp, t):
        ca = 2 * cp
        c0 = NSL[ca][0]
        wtot = NSL[ca][1] + NSL[ca + 1][1]
        subs = wtot // 128
        par = t % 2
        dqh, r_dqh = dqh2[par]
        for half in range(2):
            bank, r_bank = ps[(7, 4)[half]]
            jn = min(4, subs - half * 4)
            if jn <= 0:
                continue
            for j4 in range(jn):
                j = half * 4 + j4
                S_.op('tensor', lambda e, a=bank[:, j4 * 128:(j4 + 1) * 128],
                      b=dqh[:, j * 128:(j + 1) * 128]: e.transpose(a, b, idb),
                      reads=[r_dqh, r_idb], writes=[r_bank],
                      tag=f'D.ht{cp}.{t}.{j}')
            S_.op('scalar', lambda e, a=hqst[:, half * 512:half * 512 + jn * 128],
                  b=bank[:, 0:jn * 128]: e.copy(out=a, in_=b),
                  reads=[r_bank], writes=[r_hqst], tag=f'D.he{cp}.{t}.{half}')
        hoo = d_hmq[t, :, c0:c0 + wtot]
        S_.dma('scalar', lambda e, a=hoo, b=hqst[:, 0:wtot]:
               e.dma_start(out=a, in_=b),
               reads=[r_hqst], writes=[r_hmq[t]], tag=f'D.hs{cp}.{t}')

    for t0 in range(0, NT, 4):
        d1_stats4(t0)

    def wd_subjob(r, hh):
        for hv, n_hv in ((0, 768), (1, 640)):
            buf, r_buf = wdld2[hv]
            S_.dma(('sync', 'scalar')[hv], lambda e, a=buf[:, 0:n_hv],
                   b=d_wd[r * 128:(r + 1) * 128,
                          hh * IH + hv * 768:hh * IH + hv * 768 + n_hv]:
                   e.dma_start(out=a, in_=b), writes=[r_buf],
                   tag=f'D.wdl{r}.{hh}.{hv}')
            quant(buf[:, 0:n_hv], r_buf, n_hv, tq, buf[:, 0:n_hv], r_buf,
                  f'D.wdq{r}.{hh}.{hv}')
        for half in range(3):
            bank, r_bank = ps[5 + (half % 2)]
            jn = 4 if half < 2 else 3
            for j4 in range(jn):
                j = half * 4 + j4
                jg = hh * 11 + j
                sbuf, r_sbuf = wdld2[0] if j < 6 else wdld2[1]
                col = j * 128 if j < 6 else (j - 6) * 128
                S_.op('tensor', lambda e,
                      a=bank[:, j4 * 128:(j4 + 1) * 128],
                      b=sbuf[:, col:col + 128]:
                      e.transpose(a, b, idf),
                      reads=[r_sbuf, r_idf], writes=[r_bank],
                      tag=f'D.wdt{r}.{jg}')
            g0 = hh * 11 + half * 4
            dv = wdt3[:, g0:g0 + jn, r * 128:r * 128 + 128]
            S_.op('scalar', lambda e, a=dv, b=bank[:, 0:jn * 128].rearrange(
                'p (a b) -> p a b', b=128): e.copy(out=a, in_=b),
                reads=[r_bank], writes=[r_wdt], tag=f'D.wde{r}.{hh}.{half}')

    d2bufs = {}
    silb2 = [sb(f'silb{i}', [128, 1024], F32) for i in range(2)]
    hmb2 = [sb(f'hmb{i}', [128, 1024], F32) for i in range(2)]
    dqh2 = [sb(f'dqh{i}', [128, 1024], BF16) for i in range(2)]
    hqst, r_hqst = sb('hqst', [128, 1024], BF16)
    wgqc2 = [sb(f'wgqc{i}', [128, KC * 512], BF16) for i in range(2)]
    wuqc2 = [sb(f'wuqc{i}', [128, KC * 512], BF16) for i in range(2)]
    tqs = []
    for i in range(2):
        tqs.append({
            'bm': sb(f'tqs{i}_bm', [128, 64], F32),
            'sc': sb(f'tqs{i}_sc', [128, 64], F32),
            'rsc': sb(f'tqs{i}_rsc', [128, 64], F32),
            'xn': silb2[i],
            'z': tq['z'],
        })

    def d2_tile(t):
        hmqt, (outb, r_outb), (eb2, r_eb2) = (
            d2bufs['hmqt'], d2bufs['outb'], d2bufs['eb2'])
        hq, r_hq = hmqt[t % 2]
        S_.dma('sync', lambda e, a=hq, b=d_hmq[t]: e.dma_start(out=a, in_=b),
               reads=[r_hmq[t]], writes=[r_hq], tag=f'D2.hl{t}')
        xt, r_xt = dxb[t % 2]
        S_.dma('sync', lambda e, a=xt, b=d_x2[t * 128:(t + 1) * 128, :]:
               e.dma_start(out=a, in_=b),
               reads=[r_x2[t]], writes=[r_xt], tag=f'D2.xl{t}')
        for hs in range(2):
            bank, r_bank = ps[0 + hs]
            for kc in range(IC):
                S_.op('tensor', lambda e, a=bank,
                      b=hq[:, kc * 128:(kc + 1) * 128],
                      c=wdt[:, kc * H + hs * 512:kc * H + (hs + 1) * 512],
                      st=(kc == 0), sp=(kc == IC - 1):
                      e.matmul(a, b, c, start=st, stop=sp),
                      reads=[r_hq, r_wdt], writes=[r_bank],
                      tag=f'D2.dm{t}.{hs}.{kc}')
            S_.op('vector', lambda e, a=eb2, b=bank,
                  c=modDg[:, hs * 512:(hs + 1) * 512]:
                  e.tensor_tensor(out=a, in0=b, in1=c, op=ALU.mult),
                  reads=[r_bank, r_modDg], writes=[r_eb2], tag=f'D2.e1{t}.{hs}')
            S_.op('vector', lambda e, a=outb[:, hs * 512:(hs + 1) * 512],
                  b=eb2, c=xt[:, hs * 512:(hs + 1) * 512]:
                  e.tensor_tensor(out=a, in0=b, in1=c, op=ALU.add),
                  reads=[r_eb2, r_xt], writes=[r_outb], tag=f'D2.e2{t}.{hs}')
        S_.dma('scalar', lambda e, a=d_out[t * 128:(t + 1) * 128, :], b=outb:
               e.dma_start(out=a, in_=b),
               reads=[r_outb], writes=[r_out], tag=f'D2.os{t}')

    def load_guq(c):
        c0, wc = NSL[c]
        wgqc, r_wgqcc = wgqc2[c % 2]
        wuqc, r_wuqcc = wuqc2[c % 2]
        for (dq, rq, dst, rdst, nm) in ((d_wgq, r_wgq[c], wgqc, r_wgqcc, 'g'),
                                        (d_wuq, r_wuq[c], wuqc, r_wuqcc, 'u')):
            hi = dq[:, :, c0:c0 + wc].transpose([1, 0, 2])
            S_.dma('sync', lambda e, a=dst.rearrange(
                'p (a b) -> p a b', b=512)[:, :, 0:wc], b=hi:
                e.dma_start(out=a, in_=b),
                reads=[rq], writes=[rdst], tag=f'D.wl{nm}{c}')

    gu_jobs = []
    for c in range(6):
        for s_ in range(NSL[c][1] // 128):
            gu_jobs.append(lambda c=c, s_=s_: gu_subjob(
                d_wg, d_wgq, r_wgq[c], c, s_, 0, 'D.g'))
            gu_jobs.append(lambda c=c, s_=s_: gu_subjob(
                d_wu, d_wuq, r_wuq[c], c, s_, 1, 'D.u'))
    wd_jobs = []
    for r in range(8):
        for hh in range(2):
            wd_jobs.append(lambda r=r, hh=hh: wd_subjob(r, hh))

    LAG = 9
    for t in range(NT):
        u = t - LAG
        d1_prep(t)
        if u >= 0:
            d1_head(0, u)
            if u > 0:
                d1_tail(0, u - 1)
        for _ in range(2):
            if gu_jobs:
                gu_jobs.pop(0)()
        if t == 7:
            load_guq(0)
            load_guq(1)
    al.free(wldst1_2[0]); al.free(wldst1_2[1])
    wdld2 = [sb(f'wdld{i}', [128, 768], F32) for i in range(2)]
    wdi = 0
    for u in range(NT - LAG, NT):
        d1_head(0, u)
        d1_tail(0, u - 1)
        if wdi % 2 == 0 and wd_jobs:
            wd_jobs.pop(0)()
        wdi += 1
    d1_tail(0, NT - 1)

    load_guq(2)
    load_guq(3)
    for t in range(NT):
        d1_head(1, t)
        if t > 0:
            d1_tail(1, t - 1)
        if wdi % 2 == 0 and wd_jobs:
            wd_jobs.pop(0)()
        wdi += 1
    d1_tail(1, NT - 1)

    al.free((dq1, r_dq1)); al.free((dtr, r_dtr)); al.free((modDa, r_modDa))
    al.free(wdld2[0]); al.free(wdld2[1])
    al.free((wqstg, r_wqstg))
    for k in ('bm', 'sc', 'rsc', 'xn'):
        al.free(tq[k])
    d2bufs['hmqt'] = [sb(f'hmqt{i}', [128, IC * 128], BF16) for i in range(2)]
    d2bufs['outb'] = sb('outb', [128, H], F32)
    d2bufs['eb2'] = sb('eb2', [128, 512], F32)

    load_guq(4)
    load_guq(5)
    for t in range(NT):
        d1_head(2, t)
        if t > 0:
            d1_tail(2, t - 1)
        if t >= 3:
            d2_tile(t - 3)
    d1_tail(2, NT - 1)
    for t in range(NT - 3, NT):
        d2_tile(t)

    if upto == 'D1':
        with nc.Block() as block:
            S_.finalize(nc, block)
        mybir.codegen_inst_isa_subclasses(nc)
        _BUILD_CACHE[key] = nc
        return nc


    with nc.Block() as block:
        S_.finalize(nc, block)
    mybir.codegen_inst_isa_subclasses(nc)
    _BUILD_CACHE[key] = nc
    return nc



def kernel(L0_attn_x, L0_attn_Wq, L0_attn_Wk, L0_attn_Wv, L0_ffn_w_gate,
           L0_ffn_w_up, L0_ffn_w_down, L0_c, L0_wmod):
    nc = build(debug=False)
    x = np.ascontiguousarray(np.asarray(L0_attn_x, dtype=np.float32))
    base = {
        'wq': np.ascontiguousarray(np.asarray(L0_attn_Wq, np.float32)),
        'wk': np.ascontiguousarray(np.asarray(L0_attn_Wk, np.float32)),
        'wv': np.ascontiguousarray(np.asarray(L0_attn_Wv, np.float32)),
        'wg': np.ascontiguousarray(np.asarray(L0_ffn_w_gate, np.float32)),
        'wu': np.ascontiguousarray(np.asarray(L0_ffn_w_up, np.float32)),
        'wd': np.ascontiguousarray(np.asarray(L0_ffn_w_down, np.float32)),
        'wmod': np.ascontiguousarray(np.asarray(L0_wmod, np.float32)),
    }
    carr = np.ascontiguousarray(np.asarray(L0_c, np.float32))
    in_maps = []
    for b in range(NCORES):
        m = dict(base)
        m['x'] = np.ascontiguousarray(x[b])
        m['cvec'] = np.ascontiguousarray(carr[b:b + 1])
        in_maps.append(m)
    res = run_bass_kernel_spmd(nc, in_maps, core_ids=list(range(NCORES)))
    out = np.stack([res.results[b]['out'] for b in range(NCORES)], axis=0)
    return out


# version 18 (final)
# speedup vs baseline: 1.1423x; 1.0996x over previous
import sys
import numpy as np

sys.path.insert(0, '/opt/trn_rl_repo')

import concourse.bass as bass
import concourse.mybir as mybir
from concourse.bass_utils import run_bass_kernel_spmd

F32 = mybir.dt.float32
BF16 = mybir.dt.bfloat16
AF = mybir.ActivationFunctionType
ALU = mybir.AluOpType
AXX = mybir.AxisListType.X

S = 4096
H = 1024
I = 2816
NT = S // 128
KC = H // 128
IC = I // 128
NSL = [(0, 512), (512, 512), (1024, 512), (1536, 512), (2048, 512), (2560, 256)]
HMAGIC = float(1.5 * 2 ** 22)
EPS = 1e-5
INV_H = 1.0 / H
SMSCALE = float(1.0 / 32.0)
NCORES = 8

from concourse import dve_ops as _dve_ops
from concourse.dve_op import DveOp as _DveOp
from concourse.dve_spec import (Spec as _Spec, lower as _dve_lower, Src0 as _S0,
                                Src1 as _S1, C0 as _C0, C1 as _C1, C2 as _C2,
                                One as _One, sq as _sq)


def _mk_mg(src):
    t2 = _sq(src)
    g2 = t2 >= _C0
    g4 = t2 >= _C1
    s3 = ((g2 + g4) + g4) + _One
    return s3 * _C2


def _np_step(x):
    t2 = x * x
    return np.where(t2 >= 16.0, 2.0, np.where(t2 >= 4.0, 1.0, 0.5))


def _ref_fp4rnd(in0, in1, s0, s1, imm2):
    st = _np_step(in0)
    mg = imm2 * (st / 0.5)
    return np.round(in0 / st) * st + mg


def _ref_fp4sub(in0, in1, s0, s1, imm2):
    st = _np_step(in1)
    mg = imm2 * (st / 0.5)
    return in0 - mg


def _register_fp4_ops():
    have = {o.name: o for o in _dve_ops.OPS}
    if 'FP4RND' in have:
        return have['FP4RND'], have['FP4SUB']
    specA = _Spec(body=_S0 + _mk_mg(_S0))
    specB = _Spec(body=_S0 - _mk_mg(_S1))
    shaA = _dve_lower(specA, ver='v3', perf_en=False, name='FP4RND').sha('v3')
    shaB = _dve_lower(specB, ver='v3', perf_en=False, name='FP4SUB').sha('v3')
    opA = _DveOp('FP4RND', specA, subdim=False, uops_sha={'v3': shaA},
                 reference=_ref_fp4rnd)
    opB = _DveOp('FP4SUB', specB, subdim=False, uops_sha={'v3': shaB},
                 reference=_ref_fp4sub)
    for op in (opA, opB):
        _dve_ops.OPS.append(op)
        _dve_ops._SUB_OPCODE_FOR_NAME[op.name] = (
            _dve_ops._CUSTOM_DVE_ROW_BASE + len(_dve_ops.OPS) - 1)
        _dve_ops.CUSTOM_DVE_REFS[op.name] = op.reference
        _dve_ops.CUSTOM_DVE_SPECS[op.name] = op.spec
    assert max(_dve_ops._SUB_OPCODE_FOR_NAME.values()) < 0x20
    return opA, opB


OP_FP4RND, OP_FP4SUB = _register_fp4_ops()


ENGINES = ('tensor', 'vector', 'scalar', 'gpsimd', 'sync')
DRAIN_ENGINES = {'vector', 'scalar', 'gpsimd'}


class Res:
    __slots__ = ('name', 'writers', 'readers', 'accessors', 'is_psum', 'interval',
                 'dma_writes', 'sem')

    def __init__(self, name, is_psum=False, interval=None):
        self.name = name
        self.writers = {}
        self.readers = {}
        self.accessors = {}
        self.is_psum = is_psum
        self.interval = interval
        self.dma_writes = 0
        self.sem = None


class Item:
    __slots__ = ('eng', 'idx', 'emit', 'deps', 'kind', 'need_inc', 'tag',
                 'dma_res', 'dma_count')

    def __init__(self, eng, idx, emit, deps, kind, tag):
        self.eng = eng
        self.idx = idx
        self.emit = emit
        self.deps = deps
        self.kind = kind
        self.need_inc = False
        self.tag = tag
        self.dma_res = None
        self.dma_count = 0


class Sched:
    def __init__(self):
        self.items = {e: [] for e in ENGINES}
        self.retired = []

    def res(self, name, is_psum=False, interval=None):
        return Res(name, is_psum, interval)

    @staticmethod
    def _merge(d, eng, idx, kind):
        cur = d.get(eng)
        if cur is None or idx > cur[0]:
            d[eng] = (idx, kind)

    def op(self, eng, emit, reads=(), writes=(), kind='op', tag=''):
        idx = len(self.items[eng])
        deps = []
        for r in reads:
            for e2, (j, k2) in r.writers.items():
                deps.append((e2, j, k2))
            if r.is_psum:
                for e2, (j, k2) in r.accessors.items():
                    if e2 != eng:
                        deps.append((e2, j, k2))
        for w in writes:
            for e2, (j, k2) in w.writers.items():
                deps.append((e2, j, k2))
            for e2, (j, k2) in w.readers.items():
                deps.append((e2, j, k2))
            if w.is_psum:
                for e2, (j, k2) in w.accessors.items():
                    if e2 != eng:
                        deps.append((e2, j, k2))
        it = Item(eng, idx, emit, deps, kind, tag)
        if kind == 'dma':
            assert len(writes) == 1, f'dma must write exactly one res: {tag}'
            w0 = writes[0]
            it.dma_res = w0
            it.dma_count = getattr(w0, 'dma_writes', 0) + 1
            w0.dma_writes = it.dma_count
        self.items[eng].append(it)
        for r in reads:
            self._merge(r.readers, eng, idx, kind)
            if r.is_psum:
                self._merge(r.accessors, eng, idx, kind)
        for w in writes:
            w.writers = {}
            w.readers = {}
            self._merge(w.writers, eng, idx, kind)
            if w.is_psum:
                self._merge(w.accessors, eng, idx, kind)
        return it

    def dma(self, eng, emit, reads=(), writes=(), tag=''):
        return self.op(eng, emit, reads, writes, kind='dma', tag=tag)

    def finalize(self, nc, block):
        waits = {e: [] for e in ENGINES}
        drain_req = {e: [None] * len(self.items[e]) for e in ENGINES}
        for e in ENGINES:
            for it in self.items[e]:
                wl = []
                for (e2, j, k2) in it.deps:
                    if e2 == e and k2 != 'dma' and it.kind != 'dma':
                        if e in DRAIN_ENGINES:
                            cur = drain_req[e][it.idx]
                            drain_req[e][it.idx] = j if cur is None else max(cur, j)
                    elif e2 == e and k2 != 'dma' and it.kind == 'dma':
                        self.items[e2][j].need_inc = True
                        wl.append((e2, j, k2))
                    elif e2 == e and k2 == 'dma':
                        if not (it.kind == 'dma'
                                and self.items[e2][j].dma_res is it.dma_res):
                            wl.append((e2, j, k2))
                    else:
                        self.items[e2][j].need_inc = True
                        wl.append((e2, j, k2))
                waits[e].append(wl)
        prefix = {e: [] for e in ENGINES}
        for e in ENGINES:
            c_op = 0
            for it in self.items[e]:
                if it.need_inc and it.kind != 'dma':
                    c_op += 1
                prefix[e].append(c_op)
        semC = {e: nc.alloc_semaphore(name=f'semC_{e}') for e in ENGINES}
        dma_resources = []
        for e in ENGINES:
            for it in self.items[e]:
                if it.kind == 'dma' and it.dma_res.sem is None:
                    it.dma_res.sem = nc.alloc_semaphore(
                        name=f'dsem{len(dma_resources)}')
                    dma_resources.append(it.dma_res)
        finals = {e: [] for e in ENGINES}
        seen = set()
        for e in ENGINES:
            for it in reversed(self.items[e]):
                if it.kind == 'dma' and id(it.dma_res) not in seen:
                    seen.add(id(it.dma_res))
                    finals[e].append((it.dma_res.sem, 16 * it.dma_res.dma_writes))

        def emit_stream(e, eng):
            watermark = {}
            drained_to = -1
            for it in self.items[e]:
                dr = drain_req[e][it.idx]
                if dr is not None and dr > drained_to:
                    eng.drain()
                    drained_to = it.idx - 1
                for (e2, j, k2) in waits[e][it.idx]:
                    tgt = self.items[e2][j]
                    if k2 == 'dma':
                        sem = tgt.dma_res.sem
                        val = 16 * tgt.dma_count
                        if watermark.get(id(sem), 0) < val:
                            eng.wait_ge(sem, val)
                            watermark[id(sem)] = val
                    else:
                        c_op = prefix[e2][j]
                        if watermark.get((e2, 'o'), 0) < c_op:
                            eng.wait_ge(semC[e2], c_op)
                            watermark[(e2, 'o')] = c_op
                inst = it.emit(eng)
                if it.kind == 'dma':
                    inst.then_inc(it.dma_res.sem, 16)
                elif it.need_inc:
                    inst.then_inc(semC[e], 1)
            for sem, val in finals[e]:
                eng.wait_ge(sem, val)

        for e in ENGINES:
            if not self.items[e]:
                continue
            getattr(block, e)(lambda eng, e=e: emit_stream(e, eng))


class SBAlloc:
    def __init__(self, sched, nc):
        self.s = sched
        self.base = (nc.sbuf_base + 63) // 64 * 64
        self.limit = nc.sbuf_top - self.base
        self.live = []
        self.hwm = 0

    def _find(self, size, align):
        spans = sorted((off, off + sz) for off, sz, _ in self.live)
        pos = 0
        for a, b in spans:
            p = (pos + align - 1) // align * align
            if p + size <= a:
                return p
            pos = max(pos, b)
        p = (pos + align - 1) // align * align
        if p + size > self.limit:
            raise MemoryError(f'SBUF overflow: need {size}B at {p}')
        return p

    def alloc(self, nc, name, shape, dtype, align=64):
        per_part = int(np.prod(shape[1:])) * mybir.dt.size(dtype)
        off = self._find(per_part, align)
        h = nc.alloc_sbuf_tensor_at(name, list(shape), dtype,
                                    offset=self.base + off, align_bytes=align)
        r = self.s.res(name, interval=(off, per_part))
        for (o2, s2, old) in self.s.retired:
            if not (off + per_part <= o2 or o2 + s2 <= off):
                for e2, v in old.writers.items():
                    Sched._merge(r.readers, e2, v[0], v[1])
                for e2, v in old.readers.items():
                    Sched._merge(r.readers, e2, v[0], v[1])
        self.live.append((off, per_part, r))
        self.hwm = max(self.hwm, off + per_part)
        return h.ap(), r

    def free(self, ap_res):
        _, r = ap_res
        self.live = [(o, s, rr) for (o, s, rr) in self.live if rr is not r]
        self.s.retired.append((r.interval[0], r.interval[1], r))



_BUILD_CACHE = {}


def build(debug=False, upto='D2'):
    key = ('k', debug, upto)
    if key in _BUILD_CACHE:
        return _BUILD_CACHE[key]

    nc = bass.Bass('TRN2', target_bir_lowering=False)
    S_ = Sched()
    al = SBAlloc(S_, nc)

    def sb(name, shape, dtype, align=64):
        return al.alloc(nc, name, shape, dtype, align)

    d_x = nc.dram_tensor('x', [S, H], F32, kind='ExternalInput').ap()
    d_wq = nc.dram_tensor('wq', [H, H], F32, kind='ExternalInput').ap()
    d_wk = nc.dram_tensor('wk', [H, H], F32, kind='ExternalInput').ap()
    d_wv = nc.dram_tensor('wv', [H, H], F32, kind='ExternalInput').ap()
    d_wg = nc.dram_tensor('wg', [I, H], F32, kind='ExternalInput').ap()
    d_wu = nc.dram_tensor('wu', [I, H], F32, kind='ExternalInput').ap()
    d_wd = nc.dram_tensor('wd', [H, I], F32, kind='ExternalInput').ap()
    d_c = nc.dram_tensor('cvec', [1, H], F32, kind='ExternalInput').ap()
    d_wm = nc.dram_tensor('wmod', [H, 6 * H], F32, kind='ExternalInput').ap()
    d_out = nc.dram_tensor('out', [S, H], F32, kind='ExternalOutput').ap()
    dbg = 'ExternalOutput' if debug else 'Internal'
    d_hfm = nc.dram_tensor('hfm_s', [KC, 128, S], BF16, kind=dbg).ap()
    d_x2 = nc.dram_tensor('x2_s', [S, H], F32, kind=dbg).ap()
    d_hmq = nc.dram_tensor('hmq_s', [NT, 128, IC * 128], BF16, kind=dbg).ap()
    d_mod = nc.dram_tensor('mod_s', [6, H], F32, kind=dbg).ap()
    d_wgq = nc.dram_tensor('wgq_s', [KC, 128, I], BF16, kind='Internal').ap()
    d_wuq = nc.dram_tensor('wuq_s', [KC, 128, I], BF16, kind='Internal').ap()

    r_hfm = [S_.res(f'd.hfm{t}') for t in range(8)]
    _x2g = [S_.res(f'd.x2g{i}') for i in range(4)]
    r_x2 = [_x2g[t // 8] for t in range(NT)]
    r_hmq = [S_.res(f'd.hmq{t}') for t in range(NT)]
    r_mod_d = S_.res('d.mod')
    _wgq1 = S_.res('d.wgq')
    _wuq1 = S_.res('d.wuq')
    r_wgq = [_wgq1] * 6
    r_wuq = [_wuq1] * 6
    r_out = S_.res('d.out')

    ps = []
    for i in range(8):
        dt_i = BF16 if i in (4, 7) else F32
        h = nc.alloc_psum_tensor(f'pb{i}', [128, 512], dt_i)
        ps.append((h.ap(), S_.res(f'pb{i}', is_psum=True)))

    idf, r_idf = sb('identF', [128, 128], F32)
    idb, r_idb = sb('identB', [128, 128], BF16)
    ca1f, r_ca1f = sb('ca1f', [128, H], BF16)
    saf, r_saf = sb('saf', [128, H], BF16)
    gaf, r_gaf = sb('gaf', [128, H], BF16)
    modBC_parts = [(ca1f, r_ca1f), (saf, r_saf), (gaf, r_gaf)]
    kfm, r_kfm = sb('K_fm', [128, KC * S], BF16)
    vtm, r_vtm = sb('V_tm', [128, NT * H], BF16)
    stats, r_stats = sb('stats', [128, 32], F32)
    onesb, r_onesb = sb('onesb', [1, 128], F32)
    S_.op('vector', lambda e: e.memset(onesb, 1.0), writes=[r_onesb],
          tag='p0.ones')

    def bcast_row(vec_ap, vec_res, dst_ap, dst_res, bank_i, tag):
        bank, r_bank = ps[bank_i]
        for sl in range(2):
            S_.op('tensor', lambda e, a=bank, b=onesb,
                  c=vec_ap[0:1, sl * 512:(sl + 1) * 512]:
                  e.matmul(a, b, c, start=True, stop=True),
                  reads=[r_onesb, vec_res], writes=[r_bank],
                  tag=tag + f'.mm{sl}')
            S_.op('vector', lambda e, a=dst_ap[:, sl * 512:(sl + 1) * 512],
                  b=bank: e.tensor_copy(out=a, in_=b),
                  reads=[r_bank], writes=[dst_res], tag=tag + f'.ev{sl}')

    def quant(x_ap, x_res, n, tmp, deq_out, deq_res, tag, deq_engine='vector',
              xn_engine='vector'):
        nb = n // 16
        (bm, bm_r), (sc, sc_r), (rsc, rsc_r) = tmp['bm'], tmp['sc'], tmp['rsc']
        (xn, xn_r), (y, y_r), (z, z_r) = tmp['xn'], tmp['y'], tmp['z']
        x3 = x_ap.rearrange('p (b k) -> p b k', k=16)
        S_.op('vector', lambda e, a=bm[:, 0:nb], b=x3: e.tensor_reduce(
            out=a, in_=b, axis=AXX, op=ALU.max, apply_absolute_value=True),
            reads=[x_res], writes=[bm_r], tag=tag + '.bm')
        S_.op('vector', lambda e, a=sc[:, 0:nb], b=bm[:, 0:nb]: e.reciprocal(
            out=a, in_=b), reads=[bm_r], writes=[sc_r], tag=tag + '.r1')
        S_.op('vector', lambda e, a=rsc[:, 0:nb], b=sc[:, 0:nb]: e.tensor_scalar(
            out=a, in0=b, scalar1=6.0, scalar2=1e12,
            op0=ALU.mult, op1=ALU.min),
            reads=[sc_r], writes=[rsc_r], tag=tag + '.rsc')
        S_.op('vector', lambda e, a=sc[:, 0:nb], b=rsc[:, 0:nb]: e.reciprocal(
            out=a, in_=b), reads=[rsc_r], writes=[sc_r], tag=tag + '.sc')
        rb = rsc[:, 0:nb].unsqueeze(2).broadcast_to([128, nb, 16])
        S_.op(xn_engine, lambda e, a=xn[:, 0:n].rearrange('p (b k) -> p b k', k=16),
              b=x3, c=rb: e.tensor_tensor(out=a, in0=b, in1=c, op=ALU.mult),
              reads=[x_res, rsc_r], writes=[xn_r], tag=tag + '.xn')
        S_.op('vector', lambda e, a=y[:, 0:n], b=xn[:, 0:n]: e._custom_dve(
            OP_FP4RND, out=a, in0=b, s0=4.0, s1=16.0, imm2=HMAGIC),
            reads=[xn_r], writes=[y_r], tag=tag + '.opA')
        S_.op('vector', lambda e, a=z[:, 0:n], b=y[:, 0:n], c=xn[:, 0:n]:
              e._custom_dve(OP_FP4SUB, out=a, in0=b, in1=c, s0=4.0, s1=16.0,
                            imm2=HMAGIC),
              reads=[y_r, xn_r], writes=[z_r], tag=tag + '.opB')
        sbb = sc[:, 0:nb].unsqueeze(2).broadcast_to([128, nb, 16])
        S_.op(deq_engine, lambda e,
              a=deq_out.rearrange('p (b k) -> p b k', k=16),
              b=z[:, 0:n].rearrange('p (b k) -> p b k', k=16), c=sbb:
              e.tensor_tensor(out=a, in0=b, in1=c, op=ALU.mult),
              reads=[z_r, sc_r], writes=[deq_res], tag=tag + '.deq')

    def ln(x_ap, x_res, trash, trash_r, y_ap, y_res, tag):
        st = stats
        S_.op('vector', lambda e, a=st[:, 0:1], b=x_ap: e.reduce_sum(
            out=a, in_=b, axis=AXX), reads=[x_res], writes=[r_stats],
            tag=tag + '.sum')
        S_.op('scalar', lambda e, a=trash, b=x_ap, c=st[:, 1:2]: e.activation(
            out=a, in_=b, func=AF.Square, accum_out=c),
            reads=[x_res], writes=[trash_r, r_stats], tag=tag + '.sq')
        S_.op('vector', lambda e, a=st[:, 2:3], b=st[:, 0:1]: e.tensor_scalar(
            out=a, in0=b, scalar1=INV_H, scalar2=None, op0=ALU.mult),
            reads=[r_stats], writes=[r_stats], tag=tag + '.m')
        S_.op('vector', lambda e, a=st[:, 3:4], b=st[:, 1:2]: e.tensor_scalar(
            out=a, in0=b, scalar1=INV_H, scalar2=None, op0=ALU.mult),
            reads=[r_stats], writes=[r_stats], tag=tag + '.ex2')
        S_.op('vector', lambda e, a=st[:, 4:5], b=st[:, 2:3]: e.tensor_tensor(
            out=a, in0=b, in1=b, op=ALU.mult),
            reads=[r_stats], writes=[r_stats], tag=tag + '.m2')
        S_.op('vector', lambda e, a=st[:, 5:6], b=st[:, 3:4], c=st[:, 4:5]:
              e.tensor_tensor(out=a, in0=b, in1=c, op=ALU.subtract),
              reads=[r_stats], writes=[r_stats], tag=tag + '.var')
        S_.op('vector', lambda e, a=st[:, 6:7], b=st[:, 5:6]: e.tensor_scalar(
            out=a, in0=b, scalar1=EPS, scalar2=None, op0=ALU.add),
            reads=[r_stats], writes=[r_stats], tag=tag + '.ve')
        S_.op('vector', lambda e, a=st[:, 7:8], b=st[:, 6:7]: e.reciprocal(
            out=a, in_=b), reads=[r_stats], writes=[r_stats], tag=tag + '.r')
        S_.op('scalar', lambda e, a=st[:, 8:9], b=st[:, 7:8]: e.activation(
            out=a, in_=b, func=AF.Sqrt),
            reads=[r_stats], writes=[r_stats], tag=tag + '.s0')
        S_.op('vector', lambda e, a=st[:, 9:10], b=st[:, 8:9]: e.tensor_tensor(
            out=a, in0=b, in1=b, op=ALU.mult),
            reads=[r_stats], writes=[r_stats], tag=tag + '.nsq')
        S_.op('vector', lambda e, a=st[:, 10:11], b=st[:, 9:10], c=st[:, 6:7]:
              e.tensor_tensor(out=a, in0=b, in1=c, op=ALU.mult),
              reads=[r_stats], writes=[r_stats], tag=tag + '.nt2')
        S_.op('vector', lambda e, a=st[:, 10:11], b=st[:, 10:11]: e.tensor_scalar(
            out=a, in0=b, scalar1=-0.5, scalar2=1.5, op0=ALU.mult, op1=ALU.add),
            reads=[r_stats], writes=[r_stats], tag=tag + '.nt3')
        S_.op('vector', lambda e, a=st[:, 11:12], b=st[:, 8:9], c=st[:, 10:11]:
              e.tensor_tensor(out=a, in0=b, in1=c, op=ALU.mult),
              reads=[r_stats], writes=[r_stats], tag=tag + '.rstd')
        S_.op('vector', lambda e, a=y_ap, b=x_ap, c=st[:, 2:3], d=st[:, 11:12]:
              e.tensor_scalar(out=a, in0=b, scalar1=c, scalar2=d,
                              op0=ALU.subtract, op1=ALU.mult),
              reads=[x_res, r_stats], writes=[y_res], tag=tag + '.norm')

    def prep_wT(d_w, dst, r_dst, stg, nm, banks=(1, 2)):
        dst3 = dst.rearrange('p (a b) -> p a b', b=H)
        for r in range(KC):
            wst, r_wst = stg[r % 2]
            S_.dma('sync', lambda e, a=wst, b=d_w[r * 128:(r + 1) * 128, :]:
                   e.dma_start(out=a, in_=b), writes=[r_wst], tag=f'{nm}.ld{r}')
            for half in range(2):
                bank, r_bank = ps[banks[half]]
                for j4 in range(4):
                    j = half * 4 + j4
                    S_.op('tensor', lambda e, a=bank[:, j4 * 128:(j4 + 1) * 128],
                          b=wst[:, j * 128:(j + 1) * 128]: e.transpose(a, b, idf),
                          reads=[r_wst, r_idf], writes=[r_bank],
                          tag=f'{nm}.tr{r}.{j}')
                dv = dst3[:, half * 4:half * 4 + 4, r * 128:r * 128 + 128]
                S_.op('scalar', lambda e, a=dv, b=bank[:, 0:512].rearrange(
                    'p (a b) -> p a b', b=128): e.copy(out=a, in_=b),
                    reads=[r_bank], writes=[r_dst], tag=f'{nm}.ev{r}.{half}')

    S_.op('gpsimd', lambda e: e.memset(idf, 0.0), writes=[r_idf], tag='p0.i0')
    S_.op('gpsimd', lambda e: e.affine_select(
        out=idf, in_=idf, compare_op=ALU.not_equal, fill=1.0, base=0,
        pattern=[[-1, 128]], channel_multiplier=1),
        reads=[r_idf], writes=[r_idf], tag='p0.i1')
    S_.op('gpsimd', lambda e: e.memset(idb, 0.0), writes=[r_idb], tag='p0.i2')
    S_.op('gpsimd', lambda e: e.affine_select(
        out=idb, in_=idb, compare_op=ALU.not_equal, fill=1.0, base=0,
        pattern=[[-1, 128]], channel_multiplier=1),
        reads=[r_idb], writes=[r_idb], tag='p0.i3')

    modstage, r_modstage = sb('modstage', [1, 6 * H], F32)
    csb, r_csb = sb('csb', [128, 8], F32)
    csbf, r_csbf = sb('csbf', [128, 8], BF16)
    wmst = [sb(f'wmst{i}', [128, 512], F32) for i in range(2)]
    wmbf = [sb(f'wmbf{i}', [128, 512], BF16) for i in range(2)]
    t1k = [sb(f't1k{i}', [1, H], F32) for i in range(3)]

    for kc in range(KC):
        S_.dma('sync', lambda e, a=csb[:, kc:kc + 1],
               b=d_c[0:1, kc * 128:(kc + 1) * 128].transpose([1, 0]):
               e.dma_start(out=a, in_=b), writes=[r_csb], tag=f'p0.c{kc}')
    S_.op('scalar', lambda e: e.activation(out=csbf, in_=csb, func=AF.Silu),
          reads=[r_csb], writes=[r_csbf], tag='p0.sc')
    pm, r_pm = ps[0]
    for sl in range(12):
        for kc in range(KC):
            i = (sl * KC + kc) % 2
            wst, r_wst = wmst[i]
            wbf, r_wbf = wmbf[i]
            S_.dma('sync', lambda e, a=wst,
                   b=d_wm[kc * 128:(kc + 1) * 128, sl * 512:(sl + 1) * 512]:
                   e.dma_start(out=a, in_=b), writes=[r_wst],
                   tag=f'p0.wl{sl}.{kc}')
            S_.op('scalar', lambda e, a=wbf, b=wst: e.copy(out=a, in_=b),
                  reads=[r_wst], writes=[r_wbf], tag=f'p0.wc{sl}.{kc}')
            S_.op('tensor', lambda e, a=pm[0:1, :], b=csbf[:, kc:kc + 1], c=wbf,
                  st=(kc == 0), sp=(kc == KC - 1):
                  e.matmul(a, b, c, start=st, stop=sp),
                  reads=[r_csbf, r_wbf], writes=[r_pm], tag=f'p0.wm{sl}.{kc}')
        S_.op('vector', lambda e, a=modstage[0:1, sl * 512:(sl + 1) * 512],
              b=pm[0:1, :]: e.tensor_copy(out=a, in_=b),
              reads=[r_pm], writes=[r_modstage], tag=f'p0.we{sl}')

    tv, r_tv = t1k[0]
    ta, r_ta = t1k[1]
    tb, r_tb = t1k[2]

    def derive(ji, jo, kind, bc_col):
        src = modstage[0:1, ji * H:(ji + 1) * H]
        if kind == 'copy':
            S_.op('vector', lambda e, a=tv, b=src: e.tensor_copy(out=a, in_=b),
                  reads=[r_modstage], writes=[r_tv], tag=f'p0.d{jo}a')
        elif kind == 'plus1':
            S_.op('vector', lambda e, a=tv, b=src: e.tensor_scalar(
                out=a, in0=b, scalar1=1.0, scalar2=None, op0=ALU.add),
                reads=[r_modstage], writes=[r_tv], tag=f'p0.d{jo}a')
        else:
            S_.op('vector', lambda e, a=ta, b=src: e.tensor_scalar(
                out=a, in0=b, scalar1=0.0, scalar2=None, op0=ALU.is_ge),
                reads=[r_modstage], writes=[r_ta], tag=f'p0.d{jo}b')
            S_.op('vector', lambda e, a=ta: e.tensor_scalar(
                out=a, in0=a, scalar1=2.0, scalar2=-1.0, op0=ALU.mult,
                op1=ALU.add), reads=[r_ta], writes=[r_ta], tag=f'p0.d{jo}c')
            S_.op('vector', lambda e, a=tb, b=src: e.tensor_scalar(
                out=a, in0=b, scalar1=-1.0, scalar2=None, op0=ALU.mult),
                reads=[r_modstage], writes=[r_tb], tag=f'p0.d{jo}d')
            S_.op('vector', lambda e, a=tb, b=src: e.tensor_tensor(
                out=a, in0=b, in1=a, op=ALU.max),
                reads=[r_modstage, r_tb], writes=[r_tb], tag=f'p0.d{jo}e')
            S_.op('vector', lambda e, a=tb: e.tensor_scalar(
                out=a, in0=a, scalar1=0.25, scalar2=None, op0=ALU.max),
                reads=[r_tb], writes=[r_tb], tag=f'p0.d{jo}f')
            S_.op('vector', lambda e, a=tv, b=ta, c=tb: e.tensor_tensor(
                out=a, in0=b, in1=c, op=ALU.mult),
                reads=[r_ta, r_tb], writes=[r_tv], tag=f'p0.d{jo}g')
        S_.dma('scalar', lambda e, a=d_mod[jo:jo + 1, :], b=tv:
               e.dma_start(out=a, in_=b), reads=[r_tv], writes=[r_mod_d],
               tag=f'p0.d{jo}s')
        if bc_col is not None:
            dap, dres = modBC_parts[bc_col]
            bcast_row(tv, r_tv, dap, dres, 1, f'p0.d{jo}bc')

    derive(1, 0, 'plus1', 0)
    derive(0, 1, 'copy', 1)
    derive(2, 2, 'gate', 2)
    derive(4, 3, 'plus1', None)
    derive(3, 4, 'copy', None)
    derive(5, 5, 'gate', None)

    al.free((modstage, r_modstage)); al.free((csb, r_csb))
    al.free((csbf, r_csbf))
    al.free(wmst[0]); al.free(wmst[1]); al.free(wmbf[0]); al.free(wmbf[1])
    al.free(t1k[0]); al.free(t1k[1]); al.free(t1k[2])

    wkt, r_wkt = sb('WkT', [128, KC * H], BF16)
    wvt, r_wvt = sb('WvT', [128, KC * H], BF16)
    wstg = [sb(f'wstg{i}', [128, H], F32) for i in range(2)]
    prep_wT(d_wk, wkt, r_wkt, wstg, 'p0.wk')
    prep_wT(d_wv, wvt, r_wvt, wstg, 'p0.wv')
    al.free(wstg[0]); al.free(wstg[1])

    if upto == 'P0':
        with nc.Block() as block:
            S_.finalize(nc, block)
        mybir.codegen_inst_isa_subclasses(nc)
        _BUILD_CACHE[key] = nc
        return nc

    xb = [sb(f'xb{i}', [128, H], F32) for i in range(4)]
    hb, r_hb = sb('hb', [128, H], F32)
    yb, r_yb = sb('yb', [128, H], F32)
    sqtr, r_sqtr = sb('sqtr', [128, H], F32)
    statsB, r_statsB = sb('statsB', [128, 64], F32)
    hfmg, r_hfmg = sb('hfmg', [128, KC * 512], BF16)
    hfmg3 = hfmg.rearrange('p (a b) -> p a b', b=512)

    def b_stats4(t0):
        st = stats
        for j in range(4):
            t = t0 + j
            xt, r_xt = xb[t % 4]
            S_.dma('sync', lambda e, a=xt, b=d_x[t * 128:(t + 1) * 128, :]:
                   e.dma_start(out=a, in_=b), writes=[r_xt], tag=f'B.x{t}')
            S_.op('vector', lambda e, a=st[:, j:j + 1], b=xt: e.reduce_sum(
                out=a, in_=b, axis=AXX), reads=[r_xt], writes=[r_stats],
                tag=f'B.ss{t}')
            S_.op('scalar', lambda e, a=sqtr, b=xt, c=st[:, 4 + j:5 + j]:
                  e.activation(out=a, in_=b, func=AF.Square, accum_out=c),
                  reads=[r_xt], writes=[r_sqtr, r_stats], tag=f'B.sq{t}')
        sB = statsB[:, 2 * (t0 % 16):2 * (t0 % 16) + 8].rearrange(
            'p (a b) -> p a b', b=2)
        S_.op('vector', lambda e, a=sB[:, :, 0:1].squeeze(2), b=st[:, 0:4]:
              e.tensor_scalar(out=a, in0=b, scalar1=INV_H, scalar2=None,
                              op0=ALU.mult),
              reads=[r_stats], writes=[r_statsB], tag=f'B.sm{t0}')
        S_.op('vector', lambda e, a=st[:, 8:12], b=st[:, 4:8]: e.tensor_scalar(
            out=a, in0=b, scalar1=INV_H, scalar2=None, op0=ALU.mult),
            reads=[r_stats], writes=[r_stats], tag=f'B.se{t0}')
        S_.op('vector', lambda e, a=st[:, 12:16], b=sB[:, :, 0:1].squeeze(2):
              e.tensor_tensor(out=a, in0=b, in1=b, op=ALU.mult),
              reads=[r_statsB], writes=[r_stats], tag=f'B.sm2{t0}')
        S_.op('vector', lambda e, a=st[:, 16:20], b=st[:, 8:12], c=st[:, 12:16]:
              e.tensor_tensor(out=a, in0=b, in1=c, op=ALU.subtract),
              reads=[r_stats], writes=[r_stats], tag=f'B.sv{t0}')
        S_.op('vector', lambda e, a=st[:, 16:20], b=st[:, 16:20]: e.tensor_scalar(
            out=a, in0=b, scalar1=EPS, scalar2=None, op0=ALU.add),
            reads=[r_stats], writes=[r_stats], tag=f'B.sve{t0}')
        S_.op('vector', lambda e, a=st[:, 20:24], b=st[:, 16:20]: e.reciprocal(
            out=a, in_=b), reads=[r_stats], writes=[r_stats], tag=f'B.sr{t0}')
        S_.op('scalar', lambda e, a=st[:, 24:28], b=st[:, 20:24]: e.activation(
            out=a, in_=b, func=AF.Sqrt),
            reads=[r_stats], writes=[r_stats], tag=f'B.ssq{t0}')
        S_.op('vector', lambda e, a=st[:, 8:12], b=st[:, 24:28]: e.tensor_tensor(
            out=a, in0=b, in1=b, op=ALU.mult),
            reads=[r_stats], writes=[r_stats], tag=f'B.sn1{t0}')
        S_.op('vector', lambda e, a=st[:, 12:16], b=st[:, 8:12], c=st[:, 16:20]:
              e.tensor_tensor(out=a, in0=b, in1=c, op=ALU.mult),
              reads=[r_stats], writes=[r_stats], tag=f'B.sn2{t0}')
        S_.op('vector', lambda e, a=st[:, 12:16], b=st[:, 12:16]: e.tensor_scalar(
            out=a, in0=b, scalar1=-0.5, scalar2=1.5, op0=ALU.mult, op1=ALU.add),
            reads=[r_stats], writes=[r_stats], tag=f'B.sn3{t0}')
        S_.op('vector', lambda e, a=sB[:, :, 1:2].squeeze(2), b=st[:, 24:28],
              c=st[:, 12:16]: e.tensor_tensor(out=a, in0=b, in1=c, op=ALU.mult),
              reads=[r_stats], writes=[r_statsB], tag=f'B.srstd{t0}')

    def b_head(t):
        xt, r_xt = xb[t % 4]
        col = 2 * (t % 16)
        S_.op('vector', lambda e, a=yb, b=xt, c=statsB[:, col:col + 1],
              d=statsB[:, col + 1:col + 2]: e.tensor_scalar(
              out=a, in0=b, scalar1=c, scalar2=d, op0=ALU.subtract,
              op1=ALU.mult),
              reads=[r_xt, r_statsB], writes=[r_yb], tag=f'B.nrm{t}')
        S_.op('vector', lambda e, a=hb, b=yb, c=ca1f:
              e.tensor_tensor(out=a, in0=b, in1=c, op=ALU.mult),
              reads=[r_yb, r_ca1f], writes=[r_hb], tag=f'B.h1{t}')
        S_.op('vector', lambda e, a=xt, b=hb, c=saf:
              e.tensor_tensor(out=a, in0=b, in1=c, op=ALU.add),
              reads=[r_hb, r_saf], writes=[r_xt], tag=f'B.h2{t}')

    def b_tail(t):
        tl = t % 4
        xt, r_xt = xb[t % 4]
        for half in range(2):
            bank, r_bank = ps[2 + half]
            for j4 in range(4):
                j = half * 4 + j4
                S_.op('tensor', lambda e, a=bank[:, j4 * 128:(j4 + 1) * 128],
                      b=xt[:, j * 128:(j + 1) * 128]: e.transpose(a, b, idf),
                      reads=[r_xt, r_idf], writes=[r_bank],
                      tag=f'B.tr{t}.{j}')
            dv = hfmg3[:, half * 4:half * 4 + 4, tl * 128:tl * 128 + 128]
            S_.op('scalar', lambda e, a=dv, b=bank[:, 0:512].rearrange(
                'p (a b) -> p a b', b=128): e.copy(out=a, in_=b),
                reads=[r_bank], writes=[r_hfmg], tag=f'B.he{t}.{half}')

    for g in range(8):
        b_stats4(g * 4)
        for tl in range(4):
            t = g * 4 + tl
            b_head(t)
            if tl > 0:
                b_tail(t - 1)
        b_tail(g * 4 + 3)
        for oc in range(KC):
            bank, r_bank = ps[5 + (oc % 2)]
            for kc in range(KC):
                S_.op('tensor', lambda e, a=bank,
                      b=wkt[:, kc * H + oc * 128:kc * H + oc * 128 + 128],
                      c=hfmg[:, kc * 512:(kc + 1) * 512],
                      st=(kc == 0), sp=(kc == KC - 1):
                      e.matmul(a, b, c, start=st, stop=sp),
                      reads=[r_wkt, r_hfmg], writes=[r_bank],
                      tag=f'B.km{g}.{oc}.{kc}')
            S_.op('scalar', lambda e,
                  a=kfm[:, oc * S + g * 512:oc * S + (g + 1) * 512], b=bank:
                  e.copy(out=a, in_=b),
                  reads=[r_bank], writes=[r_kfm], tag=f'B.ke{g}.{oc}')
        for tl in range(4):
            t = g * 4 + tl
            for hs in range(2):
                bank, r_bank = ps[5 + ((tl * 2 + hs) % 2)]
                for kc in range(KC):
                    S_.op('tensor', lambda e, a=bank,
                          b=hfmg[:, kc * 512 + tl * 128:kc * 512 + tl * 128 + 128],
                          c=wvt[:, kc * H + hs * 512:kc * H + (hs + 1) * 512],
                          st=(kc == 0), sp=(kc == KC - 1):
                          e.matmul(a, b, c, start=st, stop=sp),
                          reads=[r_wvt, r_hfmg], writes=[r_bank],
                          tag=f'B.vm{t}.{hs}.{kc}')
                S_.op('scalar', lambda e,
                      a=vtm[:, t * H + hs * 512:t * H + (hs + 1) * 512], b=bank:
                      e.copy(out=a, in_=b),
                      reads=[r_bank], writes=[r_vtm], tag=f'B.ve{t}.{hs}')
        ho = d_hfm[:, :, g * 512:(g + 1) * 512].transpose([1, 0, 2])
        S_.dma('scalar', lambda e, a=ho, b=hfmg3: e.dma_start(out=a, in_=b),
               reads=[r_hfmg], writes=[r_hfm[g]], tag=f'B.hs{g}')

    al.free((wkt, r_wkt)); al.free((wvt, r_wvt)); al.free((hfmg, r_hfmg))
    for i in range(4):
        al.free(xb[i])
    al.free((hb, r_hb)); al.free((yb, r_yb)); al.free((sqtr, r_sqtr))
    al.free((statsB, r_statsB))
    al.free((ca1f, r_ca1f)); al.free((saf, r_saf))

    if upto == 'B':
        with nc.Block() as block:
            S_.finalize(nc, block)
        mybir.codegen_inst_isa_subclasses(nc)
        _BUILD_CACHE[key] = nc
        return nc

    wqt, r_wqt = sb('WqT', [128, KC * H], BF16)
    wstg2 = [sb(f'wstg2_{i}', [128, H], F32) for i in range(2)]
    prep_wT(d_wq, wqt, r_wqt, wstg2, 'C.wq', banks=(0, 1))
    al.free(wstg2[0]); al.free(wstg2[1])
    msk, r_msk = sb('masks', [128, 4 * 512], BF16)
    for r in range(4):
        slm = msk[:, r * 512:(r + 1) * 512]
        S_.op('gpsimd', lambda e, a=slm: e.memset(a, 1.0), writes=[r_msk],
              tag=f'p0.m0{r}')
        S_.op('gpsimd', lambda e, a=slm, base=128 * r: e.affine_select(
            out=a, in_=a, compare_op=ALU.is_ge, fill=0.0, base=base,
            pattern=[[-1, 512]], channel_multiplier=1),
            reads=[r_msk], writes=[r_msk], tag=f'p0.m1{r}')
    hfmc, r_hfmc = sb('hfmc', [128, KC * 512], BF16)
    qg, r_qg = sb('qg', [128, KC * 512], BF16)
    pbuf2 = [sb(f'pbuf{i}', [128, S - 512], BF16) for i in range(2)]
    dscr, r_dscr = sb('dscr', [128, 512], BF16)
    pmb2 = [sb(f'pmb{i}', [128, 512], BF16) for i in range(2)]
    ptb = [sb(f'ptb{i}', [128, 512], BF16) for i in range(2)]
    cxb = [sb(f'cxb{i}', [128, H], F32) for i in range(2)]
    x2b2 = [sb('x2b0', [128, H], F32)] * 2
    ebuf, r_ebuf = sb('ebuf', [128, 512], F32)
    ebufB, r_ebufB = sb('ebufB', [128, 512], F32)
    obuf, r_obuf = sb('obuf', [128, H], BF16)
    stc2 = [sb(f'stc{i}', [128, 32], F32) for i in range(2)]

    def c_rounds(qc):
        full = list(range(qc))
        if len(full) > 4:
            return [full[:4], full[4:] + [qc]]
        if full:
            return [full, [qc]]
        return [[qc]]

    def c_part1(qi):
        par = qi % 2
        st, r_st = stc2[par]
        pbuf, r_pbuf = pbuf2[par]
        pmb, r_pmb = pmb2[par]
        xt, r_xt = cxb[par]
        qc = qi // 4
        rr = qi % 4
        S_.dma('sync', lambda e, a=xt, b=d_x[qi * 128:(qi + 1) * 128, :]:
               e.dma_start(out=a, in_=b), writes=[r_xt], tag=f'C.x{qi}')
        rounds = c_rounds(qc)
        nrounds = len(rounds)
        for ri, cs in enumerate(rounds):
            for k_, ci in enumerate(cs):
                bank, r_bank = ps[k_]
                for hc in range(KC):
                    S_.op('tensor', lambda e, a=bank,
                          b=qg[:, hc * 512 + rr * 128:hc * 512 + rr * 128 + 128],
                          c=kfm[:, hc * S + ci * 512:hc * S + (ci + 1) * 512],
                          st=(hc == 0), sp=(hc == KC - 1):
                          e.matmul(a, b, c, start=st, stop=sp),
                          reads=[r_qg, r_kfm], writes=[r_bank],
                          tag=f'C.sm{qi}.{ci}.{hc}')
            for k_, ci in enumerate(cs):
                bank, r_bank = ps[k_]
                S_.op('vector', lambda e, a=st[:, 16 + ci:17 + ci], b=bank:
                      e.reduce_max(out=a, in_=b, axis=AXX),
                      reads=[r_bank], writes=[r_st], tag=f'C.mx{qi}.{ci}')
            c0, c1 = cs[0], cs[-1]
            S_.op('vector', lambda e, a=st[:, 12 + ri:13 + ri],
                  b=st[:, 16 + c0:17 + c1]: e.reduce_max(
                  out=a, in_=b, axis=AXX),
                  reads=[r_st], writes=[r_st], tag=f'C.rm{qi}.{ri}')
            S_.op('vector', lambda e, a=st[:, 14 + ri:15 + ri],
                  b=st[:, 12 + ri:13 + ri]: e.tensor_scalar(
                  out=a, in0=b, scalar1=-SMSCALE, scalar2=None, op0=ALU.mult),
                  reads=[r_st], writes=[r_st], tag=f'C.nb{qi}.{ri}')
            for k_, ci in enumerate(cs):
                bank, r_bank = ps[k_]
                dst = dscr if ci == qc else pbuf[:, ci * 512:(ci + 1) * 512]
                r_dst = r_dscr if ci == qc else r_pbuf
                S_.op('scalar', lambda e,
                      a=dst, b=bank,
                      c=st[:, 14 + ri:15 + ri], d=st[:, 24 + ci:25 + ci]:
                      e.activation(out=a, in_=b, func=AF.Exp, scale=SMSCALE,
                                   bias=c, accum_out=d),
                      reads=[r_bank, r_st], writes=[r_dst, r_st],
                      tag=f'C.ex{qi}.{ci}')
        if nrounds == 2:
            S_.op('vector', lambda e, a=st[:, 11:12], b=st[:, 12:13],
                  c=st[:, 13:14]: e.tensor_tensor(
                  out=a, in0=b, in1=c, op=ALU.max),
                  reads=[r_st], writes=[r_st], tag=f'C.gm{qi}')
            S_.op('vector', lambda e, a=st[:, 18:20], b=st[:, 12:14],
                  c=st[:, 11:12]: e.tensor_scalar(
                  out=a, in0=b, scalar1=c, scalar2=SMSCALE,
                  op0=ALU.subtract, op1=ALU.mult),
                  reads=[r_st], writes=[r_st], tag=f'C.fd{qi}')
            S_.op('scalar', lambda e, a=st[:, 20:22], b=st[:, 18:20]:
                  e.activation(out=a, in_=b, func=AF.Exp),
                  reads=[r_st], writes=[r_st], tag=f'C.fe{qi}')
        S_.op('vector', lambda e, a=pmb, b=dscr,
              c=msk[:, (qi % 4) * 512:(qi % 4 + 1) * 512]:
              e.tensor_tensor(out=a, in0=b, in1=c, op=ALU.mult),
              reads=[r_dscr, r_msk], writes=[r_pmb], tag=f'C.pm{qi}')
        S_.op('vector', lambda e, a=st[:, 24 + qc:25 + qc], b=pmb:
              e.reduce_sum(out=a, in_=b, axis=AXX),
              reads=[r_pmb], writes=[r_st], tag=f'C.ls{qi}')
        if nrounds == 2:
            na = len(rounds[0])
            S_.op('vector', lambda e, a=st[:, 22:23],
                  b=st[:, 24:24 + na]: e.reduce_sum(out=a, in_=b, axis=AXX),
                  reads=[r_st], writes=[r_st], tag=f'C.la{qi}')
            S_.op('vector', lambda e, a=st[:, 23:24],
                  b=st[:, 24 + na:25 + qc]: e.reduce_sum(
                  out=a, in_=b, axis=AXX),
                  reads=[r_st], writes=[r_st], tag=f'C.lb{qi}')
            S_.op('vector', lambda e, a=st[:, 22:24], b=st[:, 22:24],
                  c=st[:, 20:22]: e.tensor_tensor(
                  out=a, in0=b, in1=c, op=ALU.mult),
                  reads=[r_st], writes=[r_st], tag=f'C.lw{qi}')
            S_.op('vector', lambda e, a=st[:, 15:16], b=st[:, 22:23],
                  c=st[:, 23:24]: e.tensor_tensor(
                  out=a, in0=b, in1=c, op=ALU.add),
                  reads=[r_st], writes=[r_st], tag=f'C.l{qi}')
        else:
            S_.op('vector', lambda e, a=st[:, 15:16], b=st[:, 17:18],
                  c=st[:, 24 + qc:25 + qc]: e.tensor_copy(out=a, in_=c),
                  reads=[r_st], writes=[r_st], tag=f'C.l{qi}')
        S_.op('vector', lambda e, a=st[:, 10:11], b=st[:, 15:16]:
              e.reciprocal(out=a, in_=b),
              reads=[r_st], writes=[r_st], tag=f'C.rl{qi}')

    def c_part2(qi):
        par = qi % 2
        st, r_st = stc2[par]
        pbuf, r_pbuf = pbuf2[par]
        pmb, r_pmb = pmb2[par]
        xt, r_xt = cxb[par]
        x2b, r_x2b = x2b2[par]
        qc = qi // 4
        rounds = c_rounds(qc)
        nrounds = len(rounds)
        trbanks = [ps[7], ps[4]]
        ob = [ps[5], ps[6]]
        rbound = []
        if nrounds == 2:
            na = len(rounds[0])
            rbound.append((0, 4 * na - 1, 0))
            rbound.append((4 * na, qi, 1))
        else:
            rbound.append((0, qi, 0))
        batches = []
        for (j0, j1, ri) in rbound:
            for jq in range(j0, j1 + 1, 4):
                batches.append((jq, min(4, j1 + 1 - jq), j0, j1, ri))

        def emit_tr(bi):
            jq, jn, j0, j1, ri = batches[bi]
            half = (jq // 4) % 2
            trbank_full, r_trb = trbanks[half]
            trbank = trbank_full[:, 0:jn * 128]
            pt, r_pt = ptb[half]
            for j4 in range(jn):
                j = jq + j4
                ci = j // 4
                if ci == qc:
                    src = pmb[:, (j % 4) * 128:(j % 4 + 1) * 128]
                    r_src = r_pmb
                else:
                    src = pbuf[:, j * 128:(j + 1) * 128]
                    r_src = r_pbuf
                S_.op('tensor', lambda e,
                      a=trbank_full[:, j4 * 128:(j4 + 1) * 128],
                      b=src, c=idb: e.transpose(a, b, c),
                      reads=[r_src, r_idb], writes=[r_trb],
                      tag=f'C.pt{qi}.{j}')
            ev_eng = 'scalar' if half == 0 else 'vector'
            ev = ((lambda e, a=pt[:, 0:jn * 128], b=trbank: e.copy(out=a, in_=b))
                  if ev_eng == 'scalar' else
                  (lambda e, a=pt[:, 0:jn * 128], b=trbank: e.tensor_copy(out=a, in_=b)))
            S_.op(ev_eng, ev,
                  reads=[r_trb], writes=[r_pt],
                  tag=f'C.pe{qi}.{jq}')

        def emit_av(bi):
            jq, jn, j0, j1, ri = batches[bi]
            half = (jq // 4) % 2
            pt, r_pt = ptb[half]
            for j4 in range(jn):
                j = jq + j4
                for hs in range(2):
                    bank, r_bank = ob[hs]
                    S_.op('tensor', lambda e, a=bank,
                          b=pt[:, j4 * 128:(j4 + 1) * 128],
                          c=vtm[:, j * H + hs * 512:j * H + (hs + 1) * 512],
                          st=(j == j0), sp=(j == j1):
                          e.matmul(a, b, c, start=st, stop=sp),
                          reads=[r_pt, r_vtm], writes=[r_bank],
                          tag=f'C.av{qi}.{j}.{hs}')
            if nrounds == 2 and ri == 0 and jq + jn == j1 + 1:
                for hs in range(2):
                    bank, r_bank = ob[hs]
                    S_.op('scalar', lambda e,
                          a=obuf[:, hs * 512:(hs + 1) * 512], b=bank,
                          c=st[:, 20:21]: e.activation(
                          out=a, in_=b, func=AF.Copy, scale=c),
                          reads=[r_bank, r_st], writes=[r_obuf],
                          tag=f'C.oe{qi}.{hs}')

        for s in range(len(batches) + 1):
            if s < len(batches):
                emit_tr(s)
            if s >= 1:
                emit_av(s - 1)
        for hs in range(2):
            bank, r_bank = ob[hs]
            if nrounds == 2:
                S_.op('vector', lambda e, a=ebuf, b=bank, c=st[:, 21:22],
                      d=obuf[:, hs * 512:(hs + 1) * 512]:
                      e.scalar_tensor_tensor(out=a, in0=b, scalar=c, in1=d,
                                             op0=ALU.mult, op1=ALU.add),
                      reads=[r_bank, r_st, r_obuf], writes=[r_ebuf],
                      tag=f'C.e0{qi}.{hs}')
                S_.op('vector', lambda e, a=ebufB, b=ebuf, c=st[:, 10:11],
                      d=gaf[:, hs * 512:(hs + 1) * 512]:
                      e.scalar_tensor_tensor(out=a, in0=b, scalar=c, in1=d,
                                             op0=ALU.mult, op1=ALU.mult),
                      reads=[r_ebuf, r_st, r_gaf], writes=[r_ebufB],
                      tag=f'C.e1{qi}.{hs}')
            else:
                S_.op('vector', lambda e, a=ebufB, b=bank, c=st[:, 10:11],
                      d=gaf[:, hs * 512:(hs + 1) * 512]:
                      e.scalar_tensor_tensor(out=a, in0=b, scalar=c, in1=d,
                                             op0=ALU.mult, op1=ALU.mult),
                      reads=[r_bank, r_st, r_gaf], writes=[r_ebufB],
                      tag=f'C.e1{qi}.{hs}')
            S_.op('vector', lambda e, a=x2b[:, hs * 512:(hs + 1) * 512],
                  b=ebufB, c=xt[:, hs * 512:(hs + 1) * 512]:
                  e.tensor_tensor(out=a, in0=b, in1=c, op=ALU.add),
                  reads=[r_ebufB, r_xt], writes=[r_x2b], tag=f'C.e2{qi}.{hs}')
        S_.dma('scalar', lambda e, a=d_x2[qi * 128:(qi + 1) * 128, :], b=x2b:
               e.dma_start(out=a, in_=b),
               reads=[r_x2b], writes=[r_x2[qi]], tag=f'C.xs{qi}')

    def load_hfmc(g):
        for qd, qu in enumerate(('sync', 'scalar', 'sync', 'scalar')):
            kc0 = qd * (KC // 4)
            hi = d_hfm[kc0:kc0 + KC // 4, :,
                       g * 512:(g + 1) * 512].transpose([1, 0, 2])
            S_.dma(qu, lambda e, a=hfmc[:, kc0 * 512:(kc0 + KC // 4) * 512]
                   .rearrange('p (a b) -> p a b', b=512),
                   b=hi: e.dma_start(out=a, in_=b),
                   reads=[r_hfm[g]], writes=[r_hfmc], tag=f'C.hl{g}.{qd}')

    load_hfmc(0)
    for g in range(8):
        for oc in range(KC):
            bank, r_bank = ps[5 + (oc % 2)]
            for kc in range(KC):
                S_.op('tensor', lambda e, a=bank,
                      b=wqt[:, kc * H + oc * 128:kc * H + oc * 128 + 128],
                      c=hfmc[:, kc * 512:(kc + 1) * 512],
                      st=(kc == 0), sp=(kc == KC - 1):
                      e.matmul(a, b, c, start=st, stop=sp),
                      reads=[r_wqt, r_hfmc], writes=[r_bank],
                      tag=f'C.qm{g}.{oc}.{kc}')
            S_.op('scalar', lambda e, a=qg[:, oc * 512:(oc + 1) * 512], b=bank:
                  e.copy(out=a, in_=b),
                  reads=[r_bank], writes=[r_qg], tag=f'C.qe{g}.{oc}')
        if g + 1 < 8:
            load_hfmc(g + 1)
        for rr in range(4):
            qi = g * 4 + rr
            c_part1(qi)
            if qi > 0:
                c_part2(qi - 1)
    c_part2(NT - 1)

    al.free((wqt, r_wqt))
    al.free((hfmc, r_hfmc)); al.free((qg, r_qg))
    al.free(pbuf2[0]); al.free(pbuf2[1]); al.free((dscr, r_dscr)); al.free(pmb2[0]); al.free(pmb2[1]); al.free(ptb[0]); al.free(ptb[1])
    al.free(cxb[0]); al.free(cxb[1]); al.free(x2b2[0])
    al.free(stc2[0]); al.free(stc2[1])
    al.free((ebuf, r_ebuf)); al.free((ebufB, r_ebufB)); al.free((obuf, r_obuf))
    al.free((kfm, r_kfm)); al.free((vtm, r_vtm)); al.free((gaf, r_gaf))
    al.free((msk, r_msk))

    if upto == 'C':
        with nc.Block() as block:
            S_.finalize(nc, block)
        mybir.codegen_inst_isa_subclasses(nc)
        _BUILD_CACHE[key] = nc
        return nc

    modDa, r_modDa = sb('modDa', [128, 2 * H], F32)
    modDg, r_modDg = sb('modDg', [128, H], F32)
    mdt, r_mdt = sb('mdt', [1, H], F32)
    for j in range(3):
        S_.dma('sync', lambda e, a=mdt, b=d_mod[3 + j:4 + j, :]:
               e.dma_start(out=a, in_=b), reads=[r_mod_d], writes=[r_mdt],
               tag=f'D.ml{j}')
        if j < 2:
            bcast_row(mdt, r_mdt, modDa[:, j * H:(j + 1) * H], r_modDa,
                      6, f'D.mb{j}')
        else:
            bcast_row(mdt, r_mdt, modDg, r_modDg, 6, f'D.mb{j}')
    al.free((mdt, r_mdt))

    h2q, r_h2q = sb('h2q', [128, KC * S], BF16)
    r_h2qt = [S_.res(f'h2qt{t}') for t in range(NT)]
    wdt, r_wdt = sb('wdT', [128, IC * H], BF16)
    wdt3 = wdt.rearrange('p (a b) -> p a b', b=H)
    IH = I // 2
    tq = {
        'bm': sb('tq_bm', [128, 96], F32),
        'sc': sb('tq_sc', [128, 96], F32),
        'rsc': sb('tq_rsc', [128, 96], F32),
        'xn': sb('tq_xn', [128, IH], F32),
        'y': sb('tq_y', [128, IH], F32),
        'z': sb('tq_z', [128, IH], F32),
    }
    dxb = [sb(f'dxb{i}', [128, H], F32) for i in range(2)]
    dq1, r_dq1 = sb('dq1', [128, H], F32)
    dtr, r_dtr = sb('dtr', [128, H], F32)
    h2q3 = h2q.rearrange('p (a b) -> p a b', b=S)
    wldst1, r_wldst1 = sb('wldst1', [128, H], F32)

    statsD, r_statsD = sb('statsD', [128, 64], F32)

    def d1_stats4(t0):
        st = stats
        for j in range(4):
            t = t0 + j
            xt, r_xt = dxb[t % 2]
            S_.dma('sync', lambda e, a=xt, b=d_x2[t * 128:(t + 1) * 128, :]:
                   e.dma_start(out=a, in_=b),
                   reads=[r_x2[t]], writes=[r_xt], tag=f'D.sxl{t}')
            S_.op('vector', lambda e, a=st[:, j:j + 1], b=xt: e.reduce_sum(
                out=a, in_=b, axis=AXX), reads=[r_xt], writes=[r_stats],
                tag=f'D.ss{t}')
            S_.op('scalar', lambda e, a=dtr, b=xt, c=st[:, 4 + j:5 + j]:
                  e.activation(out=a, in_=b, func=AF.Square, accum_out=c),
                  reads=[r_xt], writes=[r_dtr, r_stats], tag=f'D.sq{t}')
        sD = statsD[:, 2 * t0:2 * t0 + 8].rearrange('p (a b) -> p a b', b=2)
        S_.op('vector', lambda e, a=sD[:, :, 0:1].squeeze(2), b=st[:, 0:4]:
              e.tensor_scalar(out=a, in0=b, scalar1=INV_H, scalar2=None,
                              op0=ALU.mult),
              reads=[r_stats], writes=[r_statsD], tag=f'D.sm{t0}')
        S_.op('vector', lambda e, a=st[:, 8:12], b=st[:, 4:8]: e.tensor_scalar(
            out=a, in0=b, scalar1=INV_H, scalar2=None, op0=ALU.mult),
            reads=[r_stats], writes=[r_stats], tag=f'D.se{t0}')
        S_.op('vector', lambda e, a=st[:, 12:16], b=sD[:, :, 0:1].squeeze(2):
              e.tensor_tensor(out=a, in0=b, in1=b, op=ALU.mult),
              reads=[r_statsD], writes=[r_stats], tag=f'D.sm2{t0}')
        S_.op('vector', lambda e, a=st[:, 16:20], b=st[:, 8:12], c=st[:, 12:16]:
              e.tensor_tensor(out=a, in0=b, in1=c, op=ALU.subtract),
              reads=[r_stats], writes=[r_stats], tag=f'D.sv{t0}')
        S_.op('vector', lambda e, a=st[:, 16:20], b=st[:, 16:20]: e.tensor_scalar(
            out=a, in0=b, scalar1=EPS, scalar2=None, op0=ALU.add),
            reads=[r_stats], writes=[r_stats], tag=f'D.sve{t0}')
        S_.op('vector', lambda e, a=st[:, 20:24], b=st[:, 16:20]: e.reciprocal(
            out=a, in_=b), reads=[r_stats], writes=[r_stats], tag=f'D.sr{t0}')
        S_.op('scalar', lambda e, a=st[:, 24:28], b=st[:, 20:24]: e.activation(
            out=a, in_=b, func=AF.Sqrt),
            reads=[r_stats], writes=[r_stats], tag=f'D.ssq{t0}')
        S_.op('vector', lambda e, a=st[:, 8:12], b=st[:, 24:28]: e.tensor_tensor(
            out=a, in0=b, in1=b, op=ALU.mult),
            reads=[r_stats], writes=[r_stats], tag=f'D.sn1{t0}')
        S_.op('vector', lambda e, a=st[:, 12:16], b=st[:, 8:12], c=st[:, 16:20]:
              e.tensor_tensor(out=a, in0=b, in1=c, op=ALU.mult),
              reads=[r_stats], writes=[r_stats], tag=f'D.sn2{t0}')
        S_.op('vector', lambda e, a=st[:, 12:16], b=st[:, 12:16]: e.tensor_scalar(
            out=a, in0=b, scalar1=-0.5, scalar2=1.5, op0=ALU.mult, op1=ALU.add),
            reads=[r_stats], writes=[r_stats], tag=f'D.sn3{t0}')
        S_.op('vector', lambda e, a=sD[:, :, 1:2].squeeze(2), b=st[:, 24:28],
              c=st[:, 12:16]: e.tensor_tensor(out=a, in0=b, in1=c, op=ALU.mult),
              reads=[r_stats], writes=[r_statsD], tag=f'D.srstd{t0}')

    def d1_prep(t):
        xt, r_xt = dxb[t % 2]
        S_.dma('sync', lambda e, a=xt, b=d_x2[t * 128:(t + 1) * 128, :]:
               e.dma_start(out=a, in_=b),
               reads=[r_x2[t]], writes=[r_xt], tag=f'D.xl{t}')
        S_.op('vector', lambda e, a=dq1, b=xt, c=statsD[:, 2 * t:2 * t + 1],
              d=statsD[:, 2 * t + 1:2 * t + 2]: e.tensor_scalar(
              out=a, in0=b, scalar1=c, scalar2=d, op0=ALU.subtract,
              op1=ALU.mult),
              reads=[r_xt, r_statsD], writes=[r_dq1], tag=f'D.nrm{t}')
        S_.op('vector', lambda e, a=dtr, b=dq1, c=modDa[:, 0:H]:
              e.tensor_tensor(out=a, in0=b, in1=c, op=ALU.mult),
              reads=[r_dq1, r_modDa], writes=[r_dtr], tag=f'D.h1{t}')
        S_.op('vector', lambda e, a=dq1, b=dtr, c=modDa[:, H:2 * H]:
              e.tensor_tensor(out=a, in0=b, in1=c, op=ALU.add),
              reads=[r_dtr, r_modDa], writes=[r_dq1], tag=f'D.h2{t}')
        quant(dq1, r_dq1, H, tq, dtr, r_dtr, f'D.q{t}')
        for half in range(2):
            bank, r_bank = ps[5 + half]
            for j4 in range(4):
                j = half * 4 + j4
                S_.op('tensor', lambda e, a=bank[:, j4 * 128:(j4 + 1) * 128],
                      b=dtr[:, j * 128:(j + 1) * 128]: e.transpose(a, b, idf),
                      reads=[r_dtr, r_idf], writes=[r_bank], tag=f'D.tr{t}.{j}')
            dv = h2q3[:, half * 4:half * 4 + 4, t * 128:t * 128 + 128]
            S_.op('scalar', lambda e, a=dv, b=bank[:, 0:512].rearrange(
                'p (a b) -> p a b', b=128): e.copy(out=a, in_=b),
                reads=[r_bank], writes=[r_h2qt[t]], tag=f'D.h2e{t}.{half}')

    wqstg, r_wqstg = sb('wqstg', [128, 512], BF16)

    def gu_subjob(d_w, d_q, r_q, c, s_, nm):
        c0, wc = NSL[c]
        row = c0 + s_ * 128
        S_.dma('sync', lambda e, a=wldst1,
               b=d_w[row:row + 128, :]: e.dma_start(out=a, in_=b),
               writes=[r_wldst1], tag=f'{nm}.l{c}.{s_}')
        quant(wldst1, r_wldst1, H, tq, tq['xn'][0][:, 0:H],
              tq['xn'][1], f'{nm}.q{c}.{s_}')
        dqw, r_dqw = tq['xn']
        for half in range(2):
            bank, r_bank = ps[5 + half]
            for j4 in range(4):
                j = half * 4 + j4
                S_.op('tensor', lambda e, a=bank[:, j4 * 128:(j4 + 1) * 128],
                      b=dqw[:, j * 128:(j + 1) * 128]: e.transpose(a, b, idf),
                      reads=[r_dqw, r_idf], writes=[r_bank],
                      tag=f'{nm}.t{c}.{s_}.{j}')
            S_.op('scalar', lambda e, a=wqstg, b=bank[:, 0:512]:
                  e.copy(out=a, in_=b),
                  reads=[r_bank], writes=[r_wqstg], tag=f'{nm}.e{c}.{s_}.{half}')
            dvo = d_q[half * 4:half * 4 + 4, :, row:row + 128].transpose([1, 0, 2])
            S_.dma('scalar', lambda e, a=dvo,
                   b=wqstg.rearrange('p (a b) -> p a b', b=128):
                   e.dma_start(out=a, in_=b),
                   reads=[r_wqstg], writes=[r_q], tag=f'{nm}.s{c}.{s_}.{half}')

    def d1_head(cp, t):
        ca, cb = 2 * cp, 2 * cp + 1
        wca, wcb = NSL[ca][1], NSL[cb][1]
        wtot = wca + wcb
        par = t % 2
        silb, r_silb = silb2[par]
        hmb, r_hmb = hmb2[par]
        dqh, r_dqh = dqh2[par]
        pg = [ps[0], ps[1]]
        pu = [ps[2], ps[3]]
        for kc in range(KC):
            for ci, wc in ((0, wca), (1, wcb)):
                S_.op('tensor', lambda e, a=pg[ci][0][:, 0:wc],
                      b=h2q[:, kc * S + t * 128:kc * S + t * 128 + 128],
                      c=wgqc2[ci][0][:, kc * 512:kc * 512 + wc],
                      st=(kc == 0), sp=(kc == KC - 1):
                      e.matmul(a, b, c, start=st, stop=sp),
                      reads=[r_h2qt[t], wgqc2[ci][1]], writes=[pg[ci][1]],
                      tag=f'D.gm{ca + ci}.{t}.{kc}')
                S_.op('tensor', lambda e, a=pu[ci][0][:, 0:wc],
                      b=h2q[:, kc * S + t * 128:kc * S + t * 128 + 128],
                      c=wuqc2[ci][0][:, kc * 512:kc * 512 + wc],
                      st=(kc == 0), sp=(kc == KC - 1):
                      e.matmul(a, b, c, start=st, stop=sp),
                      reads=[r_h2qt[t], wuqc2[ci][1]], writes=[pu[ci][1]],
                      tag=f'D.um{ca + ci}.{t}.{kc}')
        for ci, off, wc in ((0, 0, wca), (1, wca, wcb)):
            S_.op('scalar', lambda e, a=silb[:, off:off + wc],
                  b=pg[ci][0][:, 0:wc]: e.activation(out=a, in_=b, func=AF.Silu),
                  reads=[pg[ci][1]], writes=[r_silb], tag=f'D.si{ca + ci}.{t}')
            S_.op('vector', lambda e, a=hmb[:, off:off + wc],
                  b=silb[:, off:off + wc], c=pu[ci][0][:, 0:wc]:
                  e.tensor_tensor(out=a, in0=b, in1=c, op=ALU.mult),
                  reads=[r_silb, pu[ci][1]], writes=[r_hmb],
                  tag=f'D.hm{ca + ci}.{t}')
        quant(hmb[:, 0:wtot], r_hmb, wtot, tqs[par], dqh[:, 0:wtot], r_dqh,
              f'D.qh{cp}.{t}')

    def d1_tail(cp, t):
        ca = 2 * cp
        c0 = NSL[ca][0]
        wtot = NSL[ca][1] + NSL[ca + 1][1]
        subs = wtot // 128
        par = t % 2
        dqh, r_dqh = dqh2[par]
        hqst, r_hqst = hqst2[par]
        for half in range(2):
            bank, r_bank = ps[(7, 4)[half]]
            jn = min(4, subs - half * 4)
            if jn <= 0:
                continue
            for j4 in range(jn):
                j = half * 4 + j4
                S_.op('tensor', lambda e, a=bank[:, j4 * 128:(j4 + 1) * 128],
                      b=dqh[:, j * 128:(j + 1) * 128]: e.transpose(a, b, idb),
                      reads=[r_dqh, r_idb], writes=[r_bank],
                      tag=f'D.ht{cp}.{t}.{j}')
            S_.op('scalar', lambda e, a=hqst[:, half * 512:half * 512 + jn * 128],
                  b=bank[:, 0:jn * 128]: e.copy(out=a, in_=b),
                  reads=[r_bank], writes=[r_hqst], tag=f'D.he{cp}.{t}.{half}')
        hoo = d_hmq[t, :, c0:c0 + wtot]
        S_.dma('scalar', lambda e, a=hoo, b=hqst[:, 0:wtot]:
               e.dma_start(out=a, in_=b),
               reads=[r_hqst], writes=[r_hmq[t]], tag=f'D.hs{cp}.{t}')

    for t0 in range(0, NT, 4):
        d1_stats4(t0)
    wldst, r_wldst = sb('wldst', [128, IH], F32)

    def wd_subjob(r, hh):
        dqw, r_dqw = tq['xn']
        S_.dma('sync', lambda e, a=wldst,
               b=d_wd[r * 128:(r + 1) * 128, hh * IH:(hh + 1) * IH]:
               e.dma_start(out=a, in_=b), writes=[r_wldst],
               tag=f'D.wdl{r}.{hh}')
        quant(wldst, r_wldst, IH, tq,
              dqw[:, 0:IH], r_dqw, f'D.wdq{r}.{hh}')
        for half in range(3):
            bank, r_bank = ps[5 + (half % 2)]
            jn = 4 if half < 2 else 3
            for j4 in range(jn):
                j = half * 4 + j4
                jg = hh * 11 + j
                S_.op('tensor', lambda e,
                      a=bank[:, j4 * 128:(j4 + 1) * 128],
                      b=dqw[:, j * 128:(j + 1) * 128]:
                      e.transpose(a, b, idf),
                      reads=[r_dqw, r_idf], writes=[r_bank],
                      tag=f'D.wdt{r}.{jg}')
            g0 = hh * 11 + half * 4
            dv = wdt3[:, g0:g0 + jn, r * 128:r * 128 + 128]
            S_.op('scalar', lambda e, a=dv, b=bank[:, 0:jn * 128].rearrange(
                'p (a b) -> p a b', b=128): e.copy(out=a, in_=b),
                reads=[r_bank], writes=[r_wdt], tag=f'D.wde{r}.{hh}.{half}')

    wjobs = []
    for c in range(6):
        for s_ in range(NSL[c][1] // 128):
            wjobs.append(lambda c=c, s_=s_: gu_subjob(d_wg, d_wgq, r_wgq[c],
                                                      c, s_, 'D.g'))
            wjobs.append(lambda c=c, s_=s_: gu_subjob(d_wu, d_wuq, r_wuq[c],
                                                      c, s_, 'D.u'))
    for r in range(8):
        for hh in range(2):
            wjobs.append(lambda r=r, hh=hh: wd_subjob(r, hh))

    for t in range(NT):
        d1_prep(t)
        for _ in range(2):
            if wjobs:
                wjobs.pop(0)()
    while wjobs:
        wjobs.pop(0)()

    al.free((dq1, r_dq1)); al.free((dtr, r_dtr)); al.free((modDa, r_modDa))
    al.free((wldst, r_wldst)); al.free((wldst1, r_wldst1))
    al.free((wqstg, r_wqstg))
    for k in ('bm', 'sc', 'rsc', 'xn', 'y', 'z'):
        al.free(tq[k])
    d2bufs = {}
    silb2 = [sb(f'silb{i}', [128, 1024], F32) for i in range(2)]
    hmb2 = [sb(f'hmb{i}', [128, 1024], F32) for i in range(2)]
    dqh2 = [sb(f'dqh{i}', [128, 1024], BF16) for i in range(2)]
    hqst2 = [sb(f'hqst{i}', [128, 1024], BF16) for i in range(2)]
    d2bufs['hmqt'] = [sb(f'hmqt{i}', [128, IC * 128], BF16) for i in range(2)]
    wgqc2 = [sb(f'wgqc{i}', [128, KC * 512], BF16) for i in range(2)]
    wuqc2 = [sb(f'wuqc{i}', [128, KC * 512], BF16) for i in range(2)]
    d2bufs['outb'] = sb('outb', [128, H], F32)
    d2bufs['eb2'] = sb('eb2', [128, 512], F32)
    tqs = []
    for i in range(2):
        tqs.append({
            'bm': sb(f'tqs{i}_bm', [128, 64], F32),
            'sc': sb(f'tqs{i}_sc', [128, 64], F32),
            'rsc': sb(f'tqs{i}_rsc', [128, 64], F32),
            'xn': silb2[i],
            'y': hmb2[i],
            'z': sb(f'tqs{i}_z', [128, 1024], F32),
        })

    def d2_tile(t):
        hmqt, (outb, r_outb), (eb2, r_eb2) = (
            d2bufs['hmqt'], d2bufs['outb'], d2bufs['eb2'])
        hq, r_hq = hmqt[t % 2]
        S_.dma('sync', lambda e, a=hq, b=d_hmq[t]: e.dma_start(out=a, in_=b),
               reads=[r_hmq[t]], writes=[r_hq], tag=f'D2.hl{t}')
        xt, r_xt = dxb[t % 2]
        S_.dma('sync', lambda e, a=xt, b=d_x2[t * 128:(t + 1) * 128, :]:
               e.dma_start(out=a, in_=b),
               reads=[r_x2[t]], writes=[r_xt], tag=f'D2.xl{t}')
        for hs in range(2):
            bank, r_bank = ps[0 + hs]
            for kc in range(IC):
                S_.op('tensor', lambda e, a=bank,
                      b=hq[:, kc * 128:(kc + 1) * 128],
                      c=wdt[:, kc * H + hs * 512:kc * H + (hs + 1) * 512],
                      st=(kc == 0), sp=(kc == IC - 1):
                      e.matmul(a, b, c, start=st, stop=sp),
                      reads=[r_hq, r_wdt], writes=[r_bank],
                      tag=f'D2.dm{t}.{hs}.{kc}')
            S_.op('vector', lambda e, a=eb2, b=bank,
                  c=modDg[:, hs * 512:(hs + 1) * 512]:
                  e.tensor_tensor(out=a, in0=b, in1=c, op=ALU.mult),
                  reads=[r_bank, r_modDg], writes=[r_eb2], tag=f'D2.e1{t}.{hs}')
            S_.op('vector', lambda e, a=outb[:, hs * 512:(hs + 1) * 512],
                  b=eb2, c=xt[:, hs * 512:(hs + 1) * 512]:
                  e.tensor_tensor(out=a, in0=b, in1=c, op=ALU.add),
                  reads=[r_eb2, r_xt], writes=[r_outb], tag=f'D2.e2{t}.{hs}')
        S_.dma('scalar', lambda e, a=d_out[t * 128:(t + 1) * 128, :], b=outb:
               e.dma_start(out=a, in_=b),
               reads=[r_outb], writes=[r_out], tag=f'D2.os{t}')

    def load_guq(c):
        c0, wc = NSL[c]
        wgqc, r_wgqcc = wgqc2[c % 2]
        wuqc, r_wuqcc = wuqc2[c % 2]
        for (dq, rq, dst, rdst, nm) in ((d_wgq, r_wgq[c], wgqc, r_wgqcc, 'g'),
                                        (d_wuq, r_wuq[c], wuqc, r_wuqcc, 'u')):
            hi = dq[:, :, c0:c0 + wc].transpose([1, 0, 2])
            S_.dma('sync', lambda e, a=dst.rearrange(
                'p (a b) -> p a b', b=512)[:, :, 0:wc], b=hi:
                e.dma_start(out=a, in_=b),
                reads=[rq], writes=[rdst], tag=f'D.wl{nm}{c}')

    load_guq(0)
    load_guq(1)
    for cp in range(3):
        for t in range(NT):
            d1_head(cp, t)
            if t > 0:
                d1_tail(cp, t - 1)
            if cp == 2 and t >= 3:
                d2_tile(t - 3)
        d1_tail(cp, NT - 1)
        if cp < 2:
            load_guq(2 * cp + 2)
            load_guq(2 * cp + 3)
    for t in range(NT - 3, NT):
        d2_tile(t)

    if upto == 'D1':
        with nc.Block() as block:
            S_.finalize(nc, block)
        mybir.codegen_inst_isa_subclasses(nc)
        _BUILD_CACHE[key] = nc
        return nc


    with nc.Block() as block:
        S_.finalize(nc, block)
    mybir.codegen_inst_isa_subclasses(nc)
    _BUILD_CACHE[key] = nc
    return nc



def kernel(L0_attn_x, L0_attn_Wq, L0_attn_Wk, L0_attn_Wv, L0_ffn_w_gate,
           L0_ffn_w_up, L0_ffn_w_down, L0_c, L0_wmod):
    nc = build(debug=False)
    x = np.ascontiguousarray(np.asarray(L0_attn_x, dtype=np.float32))
    base = {
        'wq': np.ascontiguousarray(np.asarray(L0_attn_Wq, np.float32)),
        'wk': np.ascontiguousarray(np.asarray(L0_attn_Wk, np.float32)),
        'wv': np.ascontiguousarray(np.asarray(L0_attn_Wv, np.float32)),
        'wg': np.ascontiguousarray(np.asarray(L0_ffn_w_gate, np.float32)),
        'wu': np.ascontiguousarray(np.asarray(L0_ffn_w_up, np.float32)),
        'wd': np.ascontiguousarray(np.asarray(L0_ffn_w_down, np.float32)),
        'wmod': np.ascontiguousarray(np.asarray(L0_wmod, np.float32)),
    }
    carr = np.ascontiguousarray(np.asarray(L0_c, np.float32))
    in_maps = []
    for b in range(NCORES):
        m = dict(base)
        m['x'] = np.ascontiguousarray(x[b])
        m['cvec'] = np.ascontiguousarray(carr[b:b + 1])
        in_maps.append(m)
    res = run_bass_kernel_spmd(nc, in_maps, core_ids=list(range(NCORES)))
    out = np.stack([res.results[b]['out'] for b in range(NCORES)], axis=0)
    return out
